# version 08 (final)
# baseline (speedup 1.0000x reference)
import numpy as np
from contextlib import ExitStack

import concourse.bass as bass
import concourse.mybir as mybir
from concourse.bass_utils import run_bass_kernel_spmd
from concourse.library_config import mlp as _MLP_LIB
from concourse.library_overlay import lower_extended_insts

F32 = mybir.dt.float32
BF16 = mybir.dt.bfloat16
I16 = mybir.dt.int16
ALU = mybir.AluOpType
ACT = mybir.ActivationFunctionType

NCORE = 8
DM = 1024
DMC = 8
NTOK = 16384
TPC = 2048
TV = 16
NG = 8
E = 16
FF = 1408
FFC = 11
RL = 384
RLC = 3
NR = E * RL
NRV = NR // 128
NS = TPC * 2
NSV = NS // 128
CAP = 2560
TR1 = NR
TR2 = NS
EOTR = NR
EPS_LN = 1e-5

DEBUG = ()


def _host_constants():
    c = {}
    c["ones128"] = np.ones((1, 128), np.float32)
    e16 = np.zeros((8, 128), np.float32)
    for g in range(8):
        e16[g, 16 * g:16 * g + 16] = 1.0
    c["E16dup"] = e16
    e8 = np.zeros((8, 128), np.float32)
    for g in range(8):
        e8[g, 8 * g:8 * g + 8] = 1.0
        e8[g, 64 + 8 * g:64 + 8 * g + 8] = 1.0
    c["E8dup2"] = e8
    p8 = np.zeros((128, 128), np.float32)
    for g in range(8):
        for gp in range(g):
            for ee in range(16):
                p8[16 * gp + ee, 16 * g + ee] = 1.0
    c["P8"] = p8
    t16 = np.zeros((128, 128), np.float32)
    for g in range(8):
        for e1 in range(16):
            for e2 in range(e1, 16):
                t16[16 * g + e1, 16 * g + e2] = 1.0
    c["T16"] = t16
    bd = np.zeros((128, 128), np.float32)
    for g in range(8):
        bd[16 * g:16 * g + 16, 16 * g:16 * g + 16] = 1.0
    c["BD16"] = bd
    c["iotaE"] = (np.arange(128) % 16).astype(np.float32)[:, None]
    s1 = np.zeros((128, NSV, 64), np.float32)
    for m in range(NS):
        k, n = m // TPC, m % TPC
        i = (m % (NS // 16)) * 16 + m // (NS // 16)
        s1[i % 128, i // 128, 0] = n + 1
        s1[i % 128, i // 128, 1] = k
    c["slot1c"] = np.ascontiguousarray(s1.reshape(128, NSV * 64))
    rc = np.zeros((128, NRV, 64), np.float32)
    for m in range(NR):
        rc[m % 128, m // 128, 0] = m + 1
    c["rowc"] = np.ascontiguousarray(rc.reshape(128, NRV * 64))
    return c


def _per_core_consts(core):
    b_i, h_i = core // 2, core % 2
    sown = np.zeros((128, 16), np.float32)
    for j in range(16):
        sown[16 * core + j, j] = 1.0
    sst = np.zeros((64, 8), np.float32)
    for ch in range(8):
        sst[32 * h_i + 8 * b_i + ch, ch] = 1.0
    return sown, sst


class Sem:
    def __init__(self, en, nc, name):
        self.h = en(nc.semaphore(name))
        self.n = 0


class Prog:
    ENGS = ("sync", "scalar", "vector", "tensor", "gpsimd")

    def __init__(self):
        self.streams = {e: [] for e in self.ENGS}

    def emit(self, eng, fn):
        self.streams[eng].append(fn)

    def replay(self, block):
        for eng in self.ENGS:
            fns = self.streams[eng]

            def body(h, fns=fns):
                for f in fns:
                    f(h)

            getattr(block, eng)(body)


def build_nc(debug=()):
    nc = bass.Bass("TRN2", target_bir_lowering=False)

    def din(name, shape, dt=F32):
        return nc.dram_tensor(name, shape, dt, kind="ExternalInput").ap()

    xT = din("xT", [DM, NTOK])
    xT_own = din("xT_own", [DM, TPC])
    x_own = din("x_own", [TPC, DM])
    w_small_c = din("w_small_c", [DM, 25])
    w_inT = din("w_inT", [DM + 8, DM])
    w_g_in = din("w_g_in", [DM + 8, 8])
    b_in_col = din("b_in_col", [DM + 8, 1])
    b_g8 = din("b_g8", [8, 1])
    b_si32 = din("b_si32", [32, 1])
    rb128 = din("rb128", [128, 1])
    w_in_m = din("w_in_m", [DM, DM])
    w_out_in = din("w_out_in", [DM + 8, DM])
    b_out_row = din("b_out_row", [1, DM])
    sw_gate = din("sw_gate", [DM, FF])
    sw_up = din("sw_up", [DM, FF])
    sw_down = din("sw_down", [FF, DM])
    w_gate = din("w_gate", [E, DM, FF])
    w_up = din("w_up", [E, DM, FF])
    w_down = din("w_down", [E, FF, DM])
    ones128_in = din("ones128", [1, 128])
    E16_in = din("E16dup", [8, 128])
    E8_in = din("E8dup2", [8, 128])
    P8_in = din("P8", [128, 128])
    T16_in = din("T16", [128, 128])
    BD16_in = din("BD16", [128, 128])
    iota_in = din("iotaE", [128, 1])
    slot1c_in = din("slot1c", [128, NSV * 64])
    rowc_in = din("rowc", [128, NRV * 64])
    selOwn_in = din("selOwn", [128, 16])
    selST_in = din("selST", [64, 8])

    out_hbm = nc.dram_tensor("out_own", [TPC, DM], F32, kind="ExternalOutput").ap()

    xb_hbm = nc.dram_tensor("xb_hbm", [TPC, DM], BF16, kind="Internal").ap()
    s_hbm = nc.dram_tensor("s_hbm", [TPC, DM], BF16, kind="Internal").ap()
    sh_hbm = nc.dram_tensor("sh_hbm", [TPC, DM], BF16, kind="Internal").ap()
    inv1_hbm = nc.dram_tensor("inv1_hbm", [NR + 1, 64], F32, kind="Internal").ap()
    tb2_hbm = nc.dram_tensor("tb2_hbm", [NS + 1, 64], F32, kind="Internal").ap()
    eo_hbm = nc.dram_tensor("eo_hbm", [NR + 1, DM], BF16, kind="Internal").ap()
    NPRE = 3
    wbf_hbm = nc.dram_tensor("wbf_hbm", [NPRE, 128, 11264], BF16, kind="Internal").ap()

    dbg_shapes = {"l": [128, TPC], "w1": [128, TPC], "w2": [128, TPC],
                  "dr1": [128, TPC], "dr2": [128, TPC], "st": [32, 4096],
                  "cumG": [128, TPC], "i1": [128, TPC], "i2": [128, TPC],
                  "stT": [8, TPC], "gsig": [32, 4096], "sib": [32, 4096],
                  "idxf": [128, NRV], "kf": [128, NRV], "rid": [128, NSV],
                  "moe": [128, TV * DM], "gs": [128, TV * DM],
                  "s": [TPC, DM], "sh": [TPC, DM], "g0": [128, TV * DM],
                  "g1": [128, TV * DM], "eo": [NR + 1, DM],
                  "inv1": [NR + 1, 64], "drw": [128, NS // 16],
                  "xe0": [128, DMC * RL], "xb": [TPC, DM], "ae0": [128, FFC * RL],
                  "wg0": [128, 11264], "wu0": [128, 11264]}
    dbg = {nm: nc.dram_tensor("dbg_" + nm, dbg_shapes[nm],
                              BF16 if nm in ("stT", "s", "sh", "g0", "g1", "eo", "xe0", "xb", "ae0", "wg0", "wu0") else
                              (I16 if nm == "drw" else F32),
                              kind="ExternalOutput").ap()
           for nm in debug}

    ctx = ExitStack()
    en = ctx.enter_context

    def SB(nm, shp, dt=F32):
        return en(nc.sbuf_tensor(nm, shp, dt))

    wsmall = SB("wsmall", [128, DMC * 33], BF16)
    ones_bf = SB("ones_bf", [128, 1], BF16)
    ones1f = SB("ones1f", [1, 128], F32)
    ones33 = SB("ones33", [65, 128], F32)
    e16d = SB("e16d", [8, 128], F32)
    e8d = SB("e8d", [8, 128], F32)
    p8m = SB("p8m", [128, 128], F32)
    t16m = SB("t16m", [128, 128], F32)
    bd16m = SB("bd16m", [128, 128], F32)
    iota_pp = SB("iota_pp", [128, 1], F32)
    selown = SB("selown", [128, 16], F32)
    selst = SB("selst", [64, 8], F32)
    rb_pp = SB("rb_pp", [128, 1], F32)
    bsi_pp = SB("bsi_pp", [32, 1], F32)
    bg8s = SB("bg8s", [8, 1], F32)
    bgeff = SB("bgeff", [8, 1], F32)
    bgeff32 = SB("bgeff32", [32, 1], F32)
    scol_pp = SB("scol_pp", [128, 1], F32)
    sg_pp = SB("sg_pp", [128, 1], F32)
    bic = SB("bic", [128, DMC + 1], F32)
    bicb = SB("bicb", [128, DMC + 1], BF16)
    bout_sb = SB("bout_sb", [1, DM], F32)
    zsrc = SB("zsrc", [128, 512], F32)
    zsrc_bf = SB("zsrc_bf", [1, DM], BF16)
    bb_out = SB("bb_out", [128, DM], BF16)
    scol33 = SB("scol33", [33, 1], F32)
    gpre_pp = SB("gpre_pp", [128, 1], F32)
    epsc = SB("epsc", [128, 1], F32)
    mu_pp = SB("mu_pp", [128, TV], F32)
    is_pp = SB("is_pp", [128, TV], F32)
    w0_pp = SB("w0_pp", [128, TV], F32)
    w1_pp = SB("w1_pp", [128, TV], F32)
    stT = SB("stT", [8, TPC], BF16)
    dr_w = SB("dr_w", [128, NS // 16], I16)
    gidx_w = SB("gidx_w", [128, NR // 16], I16)
    s2_w = SB("s2_w", [128, NR // 16], I16)
    r0_w = SB("r0_w", [128, TPC // 16], I16)
    r1_w = SB("r1_w", [128, TPC // 16], I16)
    ownrowA = SB("ownrowA", [65, TPC], F32)
    ownrowB = SB("ownrowB", [65, TPC], F32)
    dr_i16 = SB("dr_i16", [33, TPC], I16)

    ctxA = ExitStack()
    enA = ctxA.enter_context

    def SA(nm, shp, dt=F32):
        return enA(nc.sbuf_tensor(nm, shp, dt))

    q_l = SA("q_l", [128, TPC], F32)
    q_rz = SA("q_rz", [128, TPC], F32)
    q_o1 = SA("q_o1", [128, TPC], F32)
    q_o2 = SA("q_o2", [128, TPC], F32)
    Lst = SA("Lst", [128, TPC], F32)
    q_cg = SA("q_cg", [128, TPC], F32)
    GSst = SA("GSst", [128, TPC], F32)
    q_e = SA("q_e", [128, TPC], F32)
    q_w1 = SA("q_w1", [128, TPC], F32)
    q_w2 = SA("q_w2", [128, TPC], F32)
    q_dr1 = SA("q_dr1", [128, TPC], F32)
    q_dr2 = SA("q_dr2", [128, TPC], F32)
    q_ex = SA("q_ex", [128, TPC], F32)
    q_t1 = SA("q_t1", [128, TPC], F32)
    q_t2 = SA("q_t2", [128, TPC], F32)
    q_bc1 = SA("q_bc1", [128, TPC], F32)
    q_bc2 = SA("q_bc2", [128, TPC], F32)
    mu8 = SA("mu8", [8, TPC], F32)
    is8 = SA("is8", [8, TPC], F32)
    winT_tl_t = SA("winT_tl_t", [8, DM], BF16)
    wgm_st = SA("wgm_st", [128, DMC * 8], BF16)
    wgm_tl_t = SA("wgm_tl_t", [8, 8], BF16)
    q_i1 = q_l
    q_i2 = Lst
    raws = q_bc1
    addr = lambda t: nc.lookup_mloc(t).addr
    ss8 = nc.alloc_sbuf_tensor_at("ss8", [8, TPC], F32, offset=addr(q_ex))
    sc8a = nc.alloc_sbuf_tensor_at("sc8a", [8, TPC], F32, offset=addr(q_e))
    assert addr(q_t2) == addr(q_t1) + 8192
    winT_st = nc.alloc_sbuf_tensor_at("winT_st", [128, DMC * DM], BF16, offset=addr(q_t1))
    xtb = [nc.alloc_sbuf_tensor_at(f"xtb{i}", [128, TPC], BF16,
                                   offset=addr(q_t1) + 4096 * i) for i in range(2)]
    xsq = [nc.alloc_sbuf_tensor_at(f"xsq{i}", [128, TPC], BF16,
                                   offset=addr(q_t1) + 8192 + 4096 * i) for i in range(2)]
    assert addr(q_rz) == addr(q_l) + 8192 and addr(q_o2) == addr(q_o1) + 8192
    assert addr(q_cg) == addr(Lst) + 8192
    sic = nc.alloc_sbuf_tensor_at("sic", [32, 4096], F32, offset=addr(q_l))
    gsc = nc.alloc_sbuf_tensor_at("gsc", [32, 4096], F32, offset=addr(q_o1))
    gsig = nc.alloc_sbuf_tensor_at("gsig", [32, 4096], F32, offset=addr(Lst))
    st64 = nc.alloc_sbuf_tensor_at("st64", [64, TPC], F32, offset=addr(GSst))
    ps_A = en(nc.psum_tensor("ps_A", [128, TPC], F32))
    ps_B = en(nc.psum_tensor("ps_B", [128, TPC], F32))

    sems = []

    def sem(name):
        s = Sem(en, nc, name)
        sems.append(s)
        return s

    LD = sem("LD")
    LDS = sem("LDS")
    TC = sem("TC")
    VC = sem("VC")
    SCF = sem("SCF")
    STX = sem("STX")
    SCD = sem("SCD")
    GPD = sem("GPD")
    XTs = [sem("XT0"), sem("XT1")]
    XTFs = [sem("XTF0"), sem("XTF1")]
    XSQs = [sem("XSQ0"), sem("XSQ1")]
    WLD = [sem(f"WLD{i}") for i in range(5)]
    WFR = [sem(f"WFR{i}") for i in range(5)]
    GX = [sem("GX0"), sem("GX1")]
    GXF = [sem("GXF0"), sem("GXF1")]
    XL = [sem("XL0"), sem("XL1")]
    XLF = [sem("XLF0"), sem("XLF1")]
    XOT = [sem("XOT0"), sem("XOT1")]
    XOF = [sem("XOF0"), sem("XOF1")]
    XNL = [sem("XNL0"), sem("XNL1")]
    XNF = [sem("XNF0"), sem("XNF1")]
    OUT2 = sem("OUT2")
    PCW = sem("PCW")
    XBS = sem("XBS")
    SST = sem("SST")
    SHST = sem("SHST")
    EOS = sem("EOS")
    OUTS = sem("OUTS")
    MOE = sem("MOE")
    CG = sem("CG")
    PRE = sem("PRE")

    P = Prog()

    def emit(eng, fn):
        P.emit(eng, fn)

    def wait(eng, s, val=None):
        v = s.n if val is None else val
        if v > 0:
            emit(eng, lambda h: h.wait_ge(s.h, v))

    def dma(eng, s, out, in_, slow=False):
        s.n += 16
        emit(eng, lambda h: h.dma_start(out, in_, allow_slow_non_contiguous=slow)
             .then_inc(s.h, 16))

    def finc(eng, s, fn, amt=1):
        s.n += amt
        emit(eng, lambda h: fn(h).then_inc(s.h, amt))

    def rawi(eng, fn):
        emit(eng, lambda h: fn(h))

    def v_op(fn):
        wait("vector", VC)
        finc("vector", VC, fn)

    def s_op(fn):
        wait("scalar", SCF)
        finc("scalar", SCF, fn)

    def dbg_dump(nm, src):
        if nm in dbg:
            wait("scalar", VC)
            wait("scalar", SCF)
            dma("scalar", SCD, dbg[nm], src)

    bank_free = {}

    def bwait(eng, key):
        if key in bank_free:
            s, cnt = bank_free[key]
            wait(eng, s, cnt)

    def bset(key, s):
        bank_free[key] = (s, s.n)

    rawi("gpsimd", lambda h: h.load_library(_MLP_LIB))
    wsv = wsmall.ap().rearrange("p (kc c) -> p kc c", c=33)
    dma("gpsimd", LD, wsv[:, :, 0:25], w_small_c.rearrange("(kc p) c -> p kc c", p=128))
    winT_v = winT_st.ap().rearrange("p (kc m) -> p kc m", m=DM)
    dma("gpsimd", LD, winT_v[:], w_inT[0:DM, :].rearrange("(kc p) m -> p kc m", p=128))
    winT_tl = winT_tl_t[:]
    dma("gpsimd", LD, winT_tl, w_inT[DM:DM + 8, :])
    wgm_v = wgm_st.ap().rearrange("p (kc c) -> p kc c", c=8)
    dma("gpsimd", LD, wgm_v[:], w_g_in[0:DM, :].rearrange("(kc p) c -> p kc c", p=128))
    wgm_tl = wgm_tl_t[:]
    dma("gpsimd", LD, wgm_tl, w_g_in[DM:DM + 8, :])
    for dst, src in [(ones1f[:], ones128_in), (e16d[:], E16_in), (e8d[:], E8_in),
                     (p8m[:], P8_in), (t16m[:], T16_in), (bd16m[:], BD16_in),
                     (iota_pp[:], iota_in), (selown[:], selOwn_in), (selst[:], selST_in),
                     (rb_pp[:], rb128), (bsi_pp[:], b_si32), (bg8s[:], b_g8)]:
        dma("sync", LDS, dst, src)
    bicv = bic.ap().rearrange("p (c o) -> p c o", o=1)
    dma("sync", LDS, bicv[:, 0:DMC, :], b_in_col[0:DM, :].rearrange("(c p) o -> p c o", p=128), slow=True)
    dma("sync", LDS, bic[0:8, DMC:DMC + 1], b_in_col[DM:DM + 8, :])
    dma("sync", LDS, bout_sb[:], b_out_row)
    v_op(lambda h: h.memset(zsrc[:], 0.0))
    v_op(lambda h: h.memset(zsrc_bf[:], 0.0))
    v_op(lambda h: h.memset(ones_bf[:], 1.0))
    v_op(lambda h: h.memset(epsc[:], float(EPS_LN)))
    v_op(lambda h: h.memset(ones33[:], 1.0))
    wait("vector", LD)
    wait("vector", LDS)
    v_op(lambda h: h.tensor_copy(bicb[:, 0:DMC], bic[:, 0:DMC]))
    v_op(lambda h: h.tensor_copy(bicb[0:8, DMC:DMC + 1], bic[0:8, DMC:DMC + 1]))

    wait("tensor", LD)
    wait("tensor", LDS)
    wait("tensor", VC)
    for mc in range(DMC):
        bwait("tensor", ("A", 0))
        for kc in range(DMC):
            rawi("tensor", lambda h, mc=mc, kc=kc: h.matmul(
                ps_A[:, 0:8], winT_v[:, kc, 128 * mc:128 * (mc + 1)], wgm_v[:, kc, :],
                start=(kc == 0), stop=False))
        finc("tensor", TC, lambda h, mc=mc: h.matmul(
            ps_A[:, 0:8], winT_tl_t[:, 128 * mc:128 * (mc + 1)], wgm_tl,
            start=False, stop=True))
        wait("scalar", TC)
        s_op(lambda h, mc=mc: h.copy(wsv[:, mc, 25:33], ps_A[:, 0:8]))
        bset(("A", 0), SCF)
    for kc in range(DMC):
        rawi("tensor", lambda h, kc=kc: h.matmul(
            ps_B[0:8, 0:1], wgm_v[:, kc, :],
            bicb.ap().rearrange("p (c o) -> p c o", o=1)[:, kc, :], start=(kc == 0), stop=False))
    finc("tensor", TC, lambda h: h.matmul(
        ps_B[0:8, 0:1], wgm_tl, bicb[0:8, DMC:DMC + 1], start=False, stop=True))
    wait("vector", TC)
    v_op(lambda h: h.tensor_tensor(bgeff[:], ps_B[0:8, 0:1], bg8s[:], ALU.add))
    wait("sync", VC)
    for r in range(4):
        dma("sync", STX, bgeff32[8 * r:8 * r + 8, :], bgeff[:])
    for nn in range(2):
        wait("tensor", VC)
        finc("tensor", TC, lambda h, nn=nn: h.matmul(
            ps_B[:, 512:1024], ones1f[:], bout_sb[:, 512 * nn:512 * (nn + 1)],
            start=True, stop=True))
        wait("vector", TC)
        v_op(lambda h, nn=nn: h.tensor_copy(bb_out[:, 512 * nn:512 * (nn + 1)],
                                            ps_B[:, 512:1024]))
    wait("tensor", SCF)
    wait("tensor", VC)
    for kc in range(DMC):
        fn = lambda h, kc=kc: h.matmul(ps_B[0:33, 1024:1025], wsv[:, kc, :], ones_bf[:],
                                       start=(kc == 0), stop=(kc == DMC - 1))
        finc("tensor", TC, fn) if kc == DMC - 1 else rawi("tensor", fn)
    wait("scalar", TC)
    s_op(lambda h: h.copy(scol33[:], ps_B[0:33, 1024:1025]))
    wait("sync", SCF)
    for g in range(8):
        dma("sync", STX, scol_pp[16 * g:16 * g + 16, :], scol33[0:16, :])
        dma("sync", STX, sg_pp[8 * g:8 * g + 8, :], scol33[16:24, :])
        dma("sync", STX, sg_pp[64 + 8 * g:64 + 8 * g + 8, :], scol33[25:33, :])

    wstream = [("swg", sw_gate, "ff"), ("swu", sw_up, "ff"), ("swd", sw_down, "dm")]
    for e_ in range(E):
        wstream += [(f"wg{e_}", w_gate[e_], "ff"), (f"wu{e_}", w_up[e_], "ff"),
                    (f"wd{e_}", w_down[e_], "dm")]
    pre_state = {"i": 0}

    def precast_one():
        i = pre_state["i"]
        if i >= NPRE:
            return
        nm, src_, kind = wstream[i]
        m2 = DM if kind == "dm" else FF
        dma("gpsimd", PCW, wbf_hbm[i].rearrange("p (c m2) -> p c m2", m2=m2),
            src_.rearrange("(c p) m2 -> p c m2", p=128))
        pre_state["i"] += 1

    sc_after_g = {}
    stx_after_g = {}
    wait("gpsimd", TC)
    for gg in range(NG):
        for kc in range(DMC):
            it = gg * DMC + kc
            par = it % 2
            if it >= 2:
                wait("gpsimd", XTFs[par])
                wait("gpsimd", XSQs[par])
            dma("gpsimd", XTs[par], xtb[par][:],
                xT[128 * kc:128 * (kc + 1), TPC * gg:TPC * (gg + 1)])
            wait("vector", XTs[par])
            if it >= 2:
                wait("vector", XTFs[par])
            finc("vector", XSQs[par],
                 lambda h, par=par: h.tensor_tensor(xsq[par][:], xtb[par][:], xtb[par][:], ALU.mult))
            wait("tensor", XTs[par])
            wait("tensor", XSQs[par])
            if kc == 0:
                bwait("tensor", ("A", "P1"))
                bwait("tensor", ("B", "P1"))
            for n4 in range(4):
                rawi("tensor", lambda h, kc=kc, par=par, n4=n4: h.matmul(
                    ps_A[0:33, 512 * n4:512 * (n4 + 1)], wsv[:, kc, :],
                    xtb[par][:, 512 * n4:512 * (n4 + 1)],
                    start=(kc == 0), stop=(kc == DMC - 1)))
            for n4 in range(4):
                fn = lambda h, kc=kc, par=par, n4=n4: h.matmul(
                    ps_B[64:65, 512 * n4:512 * (n4 + 1)], ones_bf[:],
                    xsq[par][:, 512 * n4:512 * (n4 + 1)],
                    start=(kc == 0), stop=(kc == DMC - 1))
                finc("tensor", XTFs[par], fn) if n4 == 3 else rawi("tensor", fn)
        lastpar = (gg * DMC + DMC - 1) % 2
        wait("scalar", XTFs[lastpar])
        if gg >= 1:
            wait("scalar", STX, stx_after_g[gg - 1])
        s_op(lambda h: h.copy(raws[0:33, :], ps_A[0:33, :]))
        s_op(lambda h: h.copy(raws[64:65, :], ps_B[64:65, 0:TPC]))
        sc_after_g[gg] = SCF.n
        bset(("A", "P1"), SCF)
        bset(("B", "P1"), SCF)
        wait("sync", SCF, sc_after_g[gg])
        dma("sync", STX, Lst[16 * gg:16 * gg + 16, :], raws[0:16, :])
        dma("sync", STX, GSst[8 * gg:8 * gg + 8, :], raws[16:24, :])
        dma("sync", STX, GSst[64 + 8 * gg:64 + 8 * gg + 8, :], raws[25:33, :])
        dma("sync", STX, mu8[gg:gg + 1, :], raws[24:25, :])
        dma("sync", STX, ss8[gg:gg + 1, :], raws[64:65, :])
        stx_after_g[gg] = STX.n

    while pre_state["i"] < NPRE:
        precast_one()
    PRE_ALL = PCW.n

    wait("vector", STX)
    v_op(lambda h: h.tensor_tensor(sc8a[:], mu8[:], mu8[:], ALU.mult))
    v_op(lambda h: h.tensor_scalar(is8[:], ss8[:], 1.0 / DM, None, ALU.mult))
    v_op(lambda h: h.tensor_tensor(is8[:], is8[:], sc8a[:], ALU.subtract))
    wait("scalar", VC)
    s_op(lambda h: h.activation(sc8a[:], is8[:], ACT.Sqrt, bias=epsc[0:8, :], scale=1.0))
    wait("vector", SCF)
    v_op(lambda h: h.reciprocal(is8[:], sc8a[:]))

    def mm128(dst_sb, lhs, src, psl=ps_A, rows=128):
        wait("tensor", VC)
        wait("tensor", SCF)
        for n4 in range(4):
            fn = lambda h, n4=n4: h.matmul(psl[0:rows, 512 * n4:512 * (n4 + 1)],
                                           lhs if isinstance(lhs, bass.AP) else lhs[:],
                                           src[:, 512 * n4:512 * (n4 + 1)],
                                           start=True, stop=True)
            finc("tensor", TC, fn) if n4 == 3 else rawi("tensor", fn)
        wait("scalar", TC)
        s_op(lambda h: h.copy(dst_sb[0:rows, 0:TPC] if rows != 128 else dst_sb[:],
                              psl[0:rows, 0:TPC]))

    mm128(q_bc1, e16d, mu8)
    mm128(q_bc2, e16d, is8)
    wait("vector", SCF)
    v_op(lambda h: h.tensor_scalar(q_l[:], q_bc1[:], scol_pp[:], None, ALU.mult))
    v_op(lambda h: h.tensor_tensor(q_l[:], Lst[:], q_l[:], ALU.subtract))
    v_op(lambda h: h.tensor_tensor(q_l[:], q_l[:], q_bc2[:], ALU.mult))
    v_op(lambda h: h.tensor_scalar(q_l[:], q_l[:], rb_pp[:], None, ALU.add))
    dbg_dump("l", q_l[:])

    shmask = lambda d: [(p // 16) * 16 + ((p % 16) + d) % 16 for p in range(32)]

    def tree_max(buf, tmp):
        for d in (8, 4, 2, 1):
            v_op(lambda h, d=d, buf=buf, tmp=tmp: h.stream_shuffle(tmp[:], buf[:], shmask(d)))
            v_op(lambda h, buf=buf, tmp=tmp: h.tensor_tensor(buf[:], buf[:], tmp[:], ALU.max))

    v_op(lambda h: h.tensor_copy(q_t1[:], q_l[:]))
    tree_max(q_t1, q_t2)
    v_op(lambda h: h.tensor_tensor(q_t2[:], q_l[:], q_t1[:], ALU.subtract))
    wait("scalar", VC)
    s_op(lambda h: h.activation(q_e[:], q_t2[:], ACT.Exp))
    wait("vector", SCF)
    mm128(q_rz, bd16m, q_e)
    wait("vector", SCF)
    v_op(lambda h: h.reciprocal(q_rz[:], q_rz[:]))
    v_op(lambda h: h.tensor_scalar(q_o1[:], q_e[:], 1.0, None, ALU.is_equal))
    mm128(q_t2, t16m, q_o1)
    wait("vector", SCF)
    v_op(lambda h: h.tensor_scalar(q_t2[:], q_t2[:], 1.0, None, ALU.is_equal))
    v_op(lambda h: h.tensor_tensor(q_o1[:], q_o1[:], q_t2[:], ALU.mult))
    v_op(lambda h: h.tensor_scalar(q_t2[:], q_o1[:], iota_pp[:], None, ALU.mult))
    mm128(q_i1, bd16m, q_t2)
    wait("vector", SCF)
    v_op(lambda h: h.tensor_tensor(q_t2[:], q_e[:], q_o1[:], ALU.mult))
    v_op(lambda h: h.tensor_tensor(q_t2[:], q_e[:], q_t2[:], ALU.subtract))
    v_op(lambda h: h.tensor_copy(q_w2[:], q_t2[:]))
    tree_max(q_w2, q_t1)
    v_op(lambda h: h.tensor_tensor(q_o2[:], q_t2[:], q_w2[:], ALU.is_equal))
    mm128(q_t1, t16m, q_o2)
    wait("vector", SCF)
    v_op(lambda h: h.tensor_scalar(q_t1[:], q_t1[:], 1.0, None, ALU.is_equal))
    v_op(lambda h: h.tensor_tensor(q_o2[:], q_o2[:], q_t1[:], ALU.mult))
    v_op(lambda h: h.tensor_scalar(q_t1[:], q_o2[:], iota_pp[:], None, ALU.mult))
    mm128(q_i2, bd16m, q_t1)
    dbg_dump("i1", q_i1[:])
    dbg_dump("i2", q_i2[:])
    wait("vector", SCF)
    v_op(lambda h: h.tensor_tensor(q_w2[:], q_w2[:], q_rz[:], ALU.mult))
    v_op(lambda h: h.tensor_tensor(q_t1[:], q_rz[:], q_w2[:], ALU.add))
    v_op(lambda h: h.tensor_scalar(q_t1[:], q_t1[:], 1e-9, None, ALU.add))
    v_op(lambda h: h.reciprocal(q_t1[:], q_t1[:]))
    v_op(lambda h: h.tensor_tensor(q_w1[:], q_rz[:], q_t1[:], ALU.mult))
    v_op(lambda h: h.tensor_tensor(q_w2[:], q_w2[:], q_t1[:], ALU.mult))
    v_op(lambda h: h.tensor_tensor(q_t1[:], q_o1[:], q_o2[:], ALU.add))
    v_op(lambda h: h.tensor_tensor_scan(q_ex[:], q_t1[:], q_t1[:], 0.0, ALU.add, ALU.bypass))
    v_op(lambda h: h.tensor_copy(q_t2[:, 0:1], q_ex[:, TPC - 1:TPC]))
    v_op(lambda h: h.tensor_tensor(q_ex[:], q_ex[:], q_t1[:], ALU.subtract))
    wait("tensor", VC)
    wait("tensor", SCF)
    finc("tensor", TC, lambda h: h.matmul(ps_B[:, 0:1], p8m[:], q_t2[:, 0:1],
                                          start=True, stop=True))
    wait("scalar", TC)
    s_op(lambda h: h.copy(gpre_pp[:], ps_B[:, 0:1]))
    wait("vector", SCF)
    v_op(lambda h: h.tensor_scalar(q_cg[:], q_ex[:], gpre_pp[:], None, ALU.add))
    dbg_dump("cumG", q_cg[:])

    def pos_k(oh, ii, wv, drv):
        v_op(lambda h: h.tensor_tensor(q_t1[:], oh[:], q_cg[:], ALU.mult))
        mm128(q_t2, bd16m, q_t1)
        wait("vector", SCF)
        v_op(lambda h: h.tensor_scalar(q_t2[:], q_t2[:], float(CAP), None, ALU.is_lt))
        v_op(lambda h: h.tensor_tensor(wv[:], wv[:], q_t2[:], ALU.mult))
        v_op(lambda h: h.tensor_tensor(q_t1[:], oh[:], q_ex[:], ALU.mult))
        mm128(drv, bd16m, q_t1)
        wait("vector", SCF)
        v_op(lambda h: h.tensor_scalar(q_t1[:], drv[:], float(RL), None, ALU.is_lt))
        v_op(lambda h: h.tensor_tensor(q_t1[:], q_t1[:], q_t2[:], ALU.mult))
        v_op(lambda h: h.tensor_scalar(q_t2[:], ii[:], float(RL), None, ALU.mult))
        v_op(lambda h: h.tensor_tensor(drv[:], q_t2[:], drv[:], ALU.add))
        v_op(lambda h: h.tensor_tensor(drv[:], drv[:], q_t1[:], ALU.mult))
        v_op(lambda h: h.tensor_scalar(q_t1[:], q_t1[:], -float(TR1), float(TR1), ALU.mult, ALU.add))
        v_op(lambda h: h.tensor_tensor(drv[:], drv[:], q_t1[:], ALU.add))

    pos_k(q_o1, q_i1, q_w1, q_dr1)
    pos_k(q_o2, q_i2, q_w2, q_dr2)
    dbg_dump("w1", q_w1[:])
    dbg_dump("w2", q_w2[:])
    dbg_dump("dr1", q_dr1[:])
    dbg_dump("dr2", q_dr2[:])

    mm128(q_t1, e8d, mu8)
    mm128(q_t2, e8d, is8)
    wait("vector", SCF)
    v_op(lambda h: h.tensor_scalar(q_e[:], q_t1[:], sg_pp[:], None, ALU.mult))
    v_op(lambda h: h.tensor_tensor(q_e[:], GSst[:], q_e[:], ALU.subtract))
    v_op(lambda h: h.tensor_tensor(q_e[:], q_e[:], q_t2[:], ALU.mult))
    wait("sync", VC)
    wait("sync", SCF)
    for bb_ in range(4):
        for hh in range(2):
            g_ = 2 * bb_ + hh
            dma("sync", STX, sic.ap()[8 * bb_:8 * bb_ + 8, 2048 * hh:2048 * (hh + 1)],
                q_e[8 * g_:8 * g_ + 8, :])
            dma("sync", STX, gsc.ap()[8 * bb_:8 * bb_ + 8, 2048 * hh:2048 * (hh + 1)],
                q_e[64 + 8 * g_:64 + 8 * g_ + 8, :])
    wait("scalar", STX)
    s_op(lambda h: h.activation(gsig.ap()[0:32, :], gsc.ap()[0:32, :], ACT.Sigmoid,
                                bias=bgeff32[:], scale=1.0))
    dbg_dump("gsig", gsig.ap()[0:32, :])
    wait("vector", STX)
    wait("vector", SCF)
    v_op(lambda h: h.tensor_scalar(sic.ap()[0:32, :], sic.ap()[0:32, :], bsi_pp[:], None, ALU.add))
    dbg_dump("sib", sic.ap()[0:32, :])
    v_op(lambda h: h.tensor_scalar(gsc.ap()[0:32, :], gsig.ap()[0:32, :], -1.0, 1.0, ALU.mult, ALU.add))
    v_op(lambda h: h.tensor_tensor(gsc.ap()[0:32, :], gsc.ap()[0:32, :], sic.ap()[0:32, :], ALU.mult))
    v_op(lambda h: h.tensor_tensor_scan(sic.ap()[0:32, :], gsig.ap()[0:32, :], gsc.ap()[0:32, :],
                                        0.0, ALU.mult, ALU.add))
    dbg_dump("st", sic.ap()[0:32, :])
    wait("sync", VC)
    for hh in range(2):
        dma("sync", STX, st64.ap()[32 * hh:32 * hh + 32, :],
            sic.ap()[0:32, 2048 * hh:2048 * (hh + 1)])
    wait("tensor", STX)
    wait("tensor", SCF)
    wait("tensor", VC)
    for n4 in range(4):
        fn = lambda h, n4=n4: h.matmul(ps_B[0:8, 512 * n4:512 * (n4 + 1)], selst[:],
                                       st64.ap()[:, 512 * n4:512 * (n4 + 1)], start=True, stop=True)
        finc("tensor", TC, fn) if n4 == 3 else rawi("tensor", fn)
    wait("scalar", TC)
    s_op(lambda h: h.copy(stT[:], ps_B[0:8, 0:TPC]))
    dbg_dump("stT", stT[:])

    mm128(q_bc1, e16d, mu8)
    mm128(q_bc2, e16d, is8)
    extr = [(ps_A, 0, ownrowA, 0, q_bc1), (ps_A, 32, ownrowA, 32, q_w1),
            (ps_A, 64, ownrowA, 64, q_w2), (ps_B, 0, ownrowB, 0, q_dr1),
            (ps_B, 32, ownrowB, 32, q_dr2), (ps_B, 64, ownrowB, 64, q_bc2)]
    for psx, pb, dstT, drow, srcq in extr:
        wait("tensor", VC)
        wait("tensor", SCF)
        for n4 in range(4):
            fn = lambda h, n4=n4, srcq=srcq, psx=psx, pb=pb: h.matmul(
                psx[pb:pb + 16, 512 * n4:512 * (n4 + 1)], selown[:],
                srcq[:, 512 * n4:512 * (n4 + 1)], start=True, stop=True)
            finc("tensor", TC, fn) if n4 == 3 else rawi("tensor", fn)
        wait("scalar", TC)
        s_op(lambda h, psx=psx, pb=pb, dstT=dstT, drow=drow: h.copy(
            dstT[drow:drow + 1, :], psx[pb:pb + 1, 0:TPC]))
    wait("vector", SCF)
    v_op(lambda h: h.tensor_copy(dr_i16[0:1, :], ownrowB[0:1, :]))
    v_op(lambda h: h.tensor_copy(dr_i16[32:33, :], ownrowB[32:33, :]))
    wait("gpsimd", VC)
    dma("gpsimd", GPD, dr_w[0:8, :], dr_i16[0:1, :].rearrange("o (q j) -> o q j", q=8))
    dma("gpsimd", GPD, dr_w[8:16, :], dr_i16[32:33, :].rearrange("o (q j) -> o q j", q=8))
    wait("gpsimd", GPD)
    for rep in range(1, 8):
        dma("gpsimd", GPD, dr_w[16 * rep:16 * (rep + 1), :], dr_w[0:16, :])
    wait("sync", SCF)
    for dst, srcT, row in [(w0_pp, ownrowA, 32), (w1_pp, ownrowA, 64),
                           (mu_pp, ownrowA, 0), (is_pp, ownrowB, 64)]:
        for vv in range(TV):
            dma("sync", STX, dst[:, vv:vv + 1],
                srcT[row:row + 1, 128 * vv:128 * (vv + 1)].rearrange("o (q c) -> o q c", c=1),
                slow=True)

    ctxA.close()
    ctxT = ExitStack()
    xtile = [ctxT.enter_context(nc.sbuf_tensor(f"xtile{i}", [128, DM], F32)) for i in range(2)]
    ttile = ctxT.enter_context(nc.sbuf_tensor("ttile", [128, DM], F32))
    dtile = ctxT.enter_context(nc.sbuf_tensor("dtile", [128, DM], F32))
    otile = ctxT.enter_context(nc.sbuf_tensor("otile", [128, DM], F32))
    ctxB0 = ExitStack()
    enB0 = ctxB0.enter_context
    btile = enB0(nc.sbuf_tensor("btile", [128, DM], BF16))
    gtl = enB0(nc.sbuf_tensor("gtl", [128, 512], F32))
    ctxB1 = ExitStack()
    xnT = ctxB1.enter_context(nc.sbuf_tensor("xnT", [128, DMC * TPC], BF16))
    ctxB2a = ExitStack()
    embed = ctxB2a.enter_context(nc.sbuf_tensor("embed", [128, DMC * TPC], BF16))
    ctxB5t = ExitStack()
    enB5 = ctxB5t.enter_context
    tbl1 = enB5(nc.sbuf_tensor("tbl1", [128, NRV * 64], F32))
    tbl2 = enB5(nc.sbuf_tensor("tbl2", [128, NSV * 64], F32))
    slot1s = nc.alloc_sbuf_tensor_at("slot1s", [128, NSV * 64], F32,
                                     offset=nc.lookup_mloc(tbl1).addr)
    idx_f = enB5(nc.sbuf_tensor("idx_f", [128, NRV], F32))
    k_f = enB5(nc.sbuf_tensor("k_f", [128, NRV], F32))
    mskE = enB5(nc.sbuf_tensor("mskE", [128, NRV], F32))
    gidx_f = enB5(nc.sbuf_tensor("gidx_f", [128, NRV], F32))
    s2_f = enB5(nc.sbuf_tensor("s2_f", [128, NRV], F32))
    scrE = enB5(nc.sbuf_tensor("scrE", [128, NRV], F32))
    gidx_i = enB5(nc.sbuf_tensor("gidx_i", [128, NRV], I16))
    s2_i = enB5(nc.sbuf_tensor("s2_i", [128, NRV], I16))
    rid_f = enB5(nc.sbuf_tensor("rid_f", [128, NSV], F32))
    rid_i = enB5(nc.sbuf_tensor("rid_i", [128, NSV], I16))
    rowcs = nc.alloc_sbuf_tensor_at("rowcs", [128, NRV * 64], F32,
                                    offset=nc.lookup_mloc(tbl1).addr)
    wait("scalar", VC, 3)
    zv8 = zsrc.ap().rearrange("p (v c) -> p v c", c=64)
    for c6 in range(6):
        dma("scalar", SCD, inv1_hbm[1024 * c6:1024 * (c6 + 1), :].rearrange("(v p) c -> p v c", p=128),
            zv8[:, 0:8, :])
    dma("scalar", SCD, inv1_hbm[NR:NR + 1, :], zsrc[0:1, 0:64])
    for c4 in range(4):
        dma("scalar", SCD, tb2_hbm[1024 * c4:1024 * (c4 + 1), :].rearrange("(v p) c -> p v c", p=128),
            zv8[:, 0:8, :])
    dma("scalar", SCD, tb2_hbm[NS:NS + 1, :], zsrc[0:1, 0:64])
    dma("scalar", SCD, eo_hbm[EOTR:EOTR + 1, :], zsrc_bf[0:1, :])
    INIT_ALL = SCD.n
    wait("sync", VC)
    wait("sync", TC)
    wait("sync", SCF)
    dma("sync", STX, slot1s[:], slot1c_in)
    PAY_ALL = STX.n
    ctxB2b = ExitStack()
    winm_sb = ctxB2b.enter_context(nc.sbuf_tensor("winm_sb", [128, DMC * DM], BF16))
    wout_sb = ctxB2b.enter_context(nc.sbuf_tensor("wout_sb", [128, DMC * DM], BF16))
    wout_tl = ctxB2b.enter_context(nc.sbuf_tensor("wout_tl", [8, DM], BF16))
    mubc = ctxB2b.enter_context(nc.sbuf_tensor("mubc", [128, TPC], BF16))
    isbc = ctxB2b.enter_context(nc.sbuf_tensor("isbc", [128, TPC], BF16))
    xob = [ctxB2b.enter_context(nc.sbuf_tensor(f"xob{i}", [128, TPC], BF16)) for i in range(2)]
    for dst, srcT, row in [(mubc, ownrowA, 0), (isbc, ownrowB, 64)]:
        wait("tensor", SCF)
        wait("tensor", VC)
        for n4 in range(4):
            fn = lambda h, n4=n4, srcT=srcT, row=row: h.matmul(
                ps_A[:, 512 * n4:512 * (n4 + 1)], ones33[row:row + 1, :],
                srcT[row:row + 1, 512 * n4:512 * (n4 + 1)], start=True, stop=True)
            finc("tensor", TC, fn) if n4 == 3 else rawi("tensor", fn)
        wait("scalar", TC)
        s_op(lambda h, dst=dst: h.copy(dst[:], ps_A[:, 0:TPC]))
    wait("gpsimd", VC)
    wait("gpsimd", TC)
    dma("gpsimd", LD, winm_sb.ap().rearrange("p (kc m) -> p kc m", m=DM),
        w_in_m.rearrange("(kc p) m -> p kc m", p=128))
    dma("gpsimd", LD, wout_sb.ap().rearrange("p (kc m) -> p kc m", m=DM),
        w_out_in[0:DM, :].rearrange("(kc p) m -> p kc m", p=128))
    dma("gpsimd", LD, wout_tl[:], w_out_in[DM:DM + 8, :])
    wait("gpsimd", SCD, INIT_ALL)
    wait("gpsimd", STX, PAY_ALL)
    wait("gpsimd", GPD)
    finc("gpsimd", MOE, lambda h: h.dma_scatter_add(
        inv1_hbm, slot1s.ap().rearrange("p (v c) -> p v c", c=64), dr_w[:],
        NS, NS, 64), amt=16)
    xnv = xnT.ap().rearrange("p (kc n) -> p kc n", n=TPC)
    for kc in range(DMC):
        par = kc % 2
        if kc >= 2:
            wait("gpsimd", XOF[par])
        dma("gpsimd", XOT[par], xob[par][:], xT_own[128 * kc:128 * (kc + 1), :])
        wait("vector", XOT[par])
        wait("vector", SCF)
        wait("vector", VC)
        if kc >= 1:
            wait("vector", XOF[1 - par])
        finc("vector", VC, lambda h, par=par: h.tensor_tensor(
            ttile[:, 0:DM] if False else dtile[:], xob[par][:, 0:DM], mubc[:, 0:DM], ALU.subtract))
        finc("vector", VC, lambda h, par=par: h.tensor_tensor(
            otile[:], xob[par][:, DM:TPC], mubc[:, DM:TPC], ALU.subtract))
        wait("vector", VC)
        finc("vector", VC, lambda h, kc=kc: h.tensor_tensor(
            xnv[:, kc, 0:DM], dtile[:], isbc[:, 0:DM], ALU.mult))
        finc("vector", XOF[par], lambda h, kc=kc: h.tensor_tensor(
            xnv[:, kc, DM:TPC], otile[:], isbc[:, DM:TPC], ALU.mult))
    XNDONE = (XOF[0].n, XOF[1].n)
    wait("vector", STX)
    for vv in range(TV):
        par = vv % 2
        if vv >= 2:
            wait("sync", XLF[par])
        dma("sync", XL[par], xtile[par][:], x_own[128 * vv:128 * (vv + 1), :])
        wait("vector", XL[par])
        if vv >= 1:
            wait("vector", XBS)
            wait("vector", XLF[1 - par])
        wait("vector", VC)
        finc("vector", VC, lambda h, par=par, vv=vv: h.tensor_scalar(
            ttile[:], xtile[par][:], mu_pp[:, vv:vv + 1], None, ALU.subtract))
        wait("vector", VC)
        finc("vector", XLF[par], lambda h, vv=vv: h.tensor_scalar(
            btile[:], ttile[:], is_pp[:, vv:vv + 1], None, ALU.mult))
        wait("scalar", XLF[par])
        dma("scalar", XBS, xb_hbm[128 * vv:128 * (vv + 1), :], btile[:])
    XBS_ALL = XBS.n

    wait("gpsimd", MOE)
    dma("gpsimd", MOE, tbl1.ap().rearrange("p (v c) -> p v c", c=64),
        inv1_hbm[0:NR, :].rearrange("(v p) c -> p v c", p=128))
    TBL1_ALL = MOE.n
    if "inv1" in dbg:
        wait("scalar", MOE, TBL1_ALL)
        dma("scalar", SCD, dbg["inv1"], inv1_hbm)
    if "drw" in dbg:
        wait("scalar", GPD)
        dma("scalar", SCD, dbg["drw"], dr_w[:])
    wait("vector", MOE, TBL1_ALL)
    t1v = tbl1.ap().rearrange("p (v c) -> p v c", c=64)
    v_op(lambda h: h.tensor_copy(idx_f.ap().rearrange("p (v o) -> p v o", o=1), t1v[:, :, 0:1]))
    v_op(lambda h: h.tensor_scalar(idx_f[:], idx_f[:], 1.0, None, ALU.subtract))
    v_op(lambda h: h.tensor_copy(k_f.ap().rearrange("p (v o) -> p v o", o=1), t1v[:, :, 1:2]))
    v_op(lambda h: h.tensor_scalar(mskE[:], idx_f[:], 0.0, None, ALU.is_lt))
    v_op(lambda h: h.tensor_tensor(gidx_f[:], idx_f[:], mskE[:], ALU.add))
    v_op(lambda h: h.tensor_scalar(scrE[:], mskE[:], -1.0, 1.0, ALU.mult, ALU.add))
    v_op(lambda h: h.tensor_tensor(s2_f[:], k_f[:], scrE[:], ALU.mult))
    v_op(lambda h: h.tensor_scalar(s2_f[:], s2_f[:], float(TPC), None, ALU.mult))
    v_op(lambda h: h.tensor_tensor(s2_f[:], s2_f[:], gidx_f[:], ALU.add))
    v_op(lambda h: h.tensor_scalar(scrE[:], mskE[:], float(TR2), None, ALU.mult))
    v_op(lambda h: h.tensor_tensor(s2_f[:], s2_f[:], scrE[:], ALU.add))
    v_op(lambda h: h.tensor_copy(gidx_i[:], gidx_f[:]))
    v_op(lambda h: h.tensor_copy(s2_i[:], s2_f[:]))
    dbg_dump("idxf", idx_f[:])
    dbg_dump("kf", k_f[:])

    def wrap8(t_i, t_w, nv, col0=0):
        wv = t_w.ap().rearrange("p (v hh) -> p v hh", hh=8)
        for hh in range(8):
            dma("sync", STX, wv[0:16, 0:nv, hh], t_i[16 * hh:16 * hh + 16, col0:col0 + nv],
                slow=True)
        wait("sync", STX)
        for rep in range(1, 8):
            dma("sync", STX, t_w[16 * rep:16 * (rep + 1), 0:nv * 8], t_w[0:16, 0:nv * 8])
        wait("sync", STX)

    wait("sync", VC)
    dma("sync", STX, rowcs.ap()[:, :], rowc_in)
    wrap8(gidx_i, gidx_w, NRV)
    wrap8(s2_i, s2_w, NRV)
    WRAP_ALL = STX.n
    wait("gpsimd", STX, WRAP_ALL)
    finc("gpsimd", MOE, lambda h: h.dma_scatter_add(
        tb2_hbm, rowcs.ap().rearrange("p (v c) -> p v c", c=64), s2_w[:],
        NR, NR, 64), amt=16)
    wait("gpsimd", MOE)
    dma("gpsimd", MOE, tbl2.ap().rearrange("p (v c) -> p v c", c=64),
        tb2_hbm[0:NS, :].rearrange("(v p) c -> p v c", p=128))
    TBL2_ALL = MOE.n
    winv = winm_sb.ap().rearrange("p (kc m) -> p kc m", m=DM)
    emv = embed.ap().rearrange("p (mc n) -> p mc n", n=TPC)
    wait("tensor", LD)
    wait("tensor", XOF[0], XNDONE[0])
    wait("tensor", XOF[1], XNDONE[1])
    for mc in range(DMC):
        for n4 in range(4):
            bank = ("A", (mc * 4 + n4) % 4)
            bwait("tensor", bank)
            bs = 512 * bank[1]
            for kc in range(DMC):
                fn = lambda h, mc=mc, n4=n4, kc=kc, bs=bs: h.matmul(
                    ps_A[:, bs:bs + 512], winv[:, kc, 128 * mc:128 * (mc + 1)],
                    xnv[:, kc, 512 * n4:512 * (n4 + 1)],
                    start=(kc == 0), stop=(kc == DMC - 1))
                finc("tensor", TC, fn) if kc == DMC - 1 else rawi("tensor", fn)
            wait("vector", TC)
            wait("vector", VC)
            finc("vector", VC, lambda h, mc=mc, n4=n4, bs=bs: h.tensor_scalar(
                emv[:, mc, 512 * n4:512 * (n4 + 1)], ps_A[:, bs:bs + 512],
                bic[:, mc:mc + 1], None, ALU.add))
            bset(bank, VC)
    wouv = wout_sb.ap().rearrange("p (kc m) -> p kc m", m=DM)
    wait("tensor", LD)
    wait("tensor", VC)
    wait("vector", XBS, XBS_ALL)
    for vv in range(TV):
        if vv >= 1:
            wait("vector", SST)
        for n2 in range(2):
            bank = ("B", (vv * 2 + n2) % 2)
            bwait("tensor", bank)
            bs = 512 * bank[1]
            for kc in range(DMC):
                rawi("tensor", lambda h, vv=vv, n2=n2, kc=kc, bs=bs: h.matmul(
                    ps_B[:, bs:bs + 512], emv[:, kc, 128 * vv:128 * (vv + 1)],
                    wouv[:, kc, 512 * n2:512 * (n2 + 1)], start=(kc == 0), stop=False))
            finc("tensor", TC, lambda h, vv=vv, n2=n2, bs=bs: h.matmul(
                ps_B[:, bs:bs + 512], stT[:, 128 * vv:128 * (vv + 1)],
                wout_tl[:, 512 * n2:512 * (n2 + 1)], start=False, stop=True))
            wait("vector", TC)
            wait("vector", VC)
            finc("vector", VC, lambda h, n2=n2, bs=bs: h.tensor_tensor(
                btile[:, 512 * n2:512 * (n2 + 1)], ps_B[:, bs:bs + 512],
                bb_out[:, 512 * n2:512 * (n2 + 1)], ALU.add))
            bset(bank, VC)
        wait("scalar", VC)
        dma("scalar", SST, s_hbm[128 * vv:128 * (vv + 1), :], btile[:])
    SST_ALL = SST.n
    wait("vector", MOE, TBL2_ALL)
    t2v = tbl2.ap().rearrange("p (v c) -> p v c", c=64)
    v_op(lambda h: h.tensor_copy(rid_f.ap().rearrange("p (v o) -> p v o", o=1), t2v[:, :, 0:1]))
    v_op(lambda h: h.tensor_scalar(rid_f[:], rid_f[:], 1.0, None, ALU.subtract))
    v_op(lambda h: h.tensor_scalar(scrE[:, 0:NSV], rid_f[:], 0.0, None, ALU.is_lt))
    v_op(lambda h: h.tensor_scalar(scrE[:, 0:NSV], scrE[:, 0:NSV], float(EOTR + 1), None, ALU.mult))
    v_op(lambda h: h.tensor_tensor(rid_f[:], rid_f[:], scrE[:, 0:NSV], ALU.add))
    v_op(lambda h: h.tensor_copy(rid_i[:], rid_f[:]))
    dbg_dump("rid", rid_f[:])
    wait("sync", VC)
    wrap8(rid_i, r0_w, TV, col0=0)
    wrap8(rid_i, r1_w, TV, col0=TV)
    RWRAP_ALL = STX.n

    ctxB2b.close()
    ctxB5t.close()
    ctxB2a.close()
    ctxB3a = ExitStack()
    NWB = 5
    wbuf = [ctxB3a.enter_context(nc.sbuf_tensor(f"wbuf{i}", [128, 11264], BF16))
            for i in range(NWB - 1)]
    _own_base = nc.lookup_mloc(ownrowA).addr
    assert nc.lookup_mloc(ownrowB).addr == _own_base + 8192
    assert nc.lookup_mloc(dr_i16).addr == _own_base + 16384
    assert nc.lookup_mloc(xtile[0]).addr == _own_base + 20480
    wbuf.append(nc.alloc_sbuf_tensor_at("wbuf4", [128, 11264], BF16,
                                        offset=_own_base))
    ctxB3b = ExitStack()
    a_sht = ctxB3b.enter_context(nc.sbuf_tensor("a_sht", [128, FFC * 512], BF16))
    wload_state = {"i": 0}
    wfr_expect = [0] * NWB
    wld_mark = {}

    def wload():
        i = wload_state["i"]
        nm, src, kind = wstream[i]
        par = i % NWB
        if i == 0:
            wait("gpsimd", PCW, PRE_ALL)
        if i >= NWB:
            wait("gpsimd", WFR[par], wfr_expect[par])
        if i < NPRE:
            dma("gpsimd", WLD[par], wbuf[par].ap(), wbf_hbm[i])
        elif kind == "dm":
            view = wbuf[par].ap().rearrange("p (c m) -> p c m", m=DM)[:, 0:FFC, :]
            dma("gpsimd", WLD[par], view, src.rearrange("(c p) m -> p c m", p=128))
        else:
            view = wbuf[par].ap().rearrange("p (c m) -> p c m", m=FF)[:, 0:DMC, :]
            dma("gpsimd", WLD[par], view, src.rearrange("(c p) m -> p c m", p=128))
        wld_mark[i] = (par, WLD[par].n)
        wload_state["i"] += 1
        return i

    def wrelease(i):
        par = i % NWB
        wfr_expect[par] = WFR[par].n

    wait("gpsimd", TC)
    wait("gpsimd", STX, RWRAP_ALL)
    wait("gpsimd", VC)
    for _ in range(NWB):
        wload()
    ashv = a_sht.ap().rearrange("p (fc n) -> p fc n", n=512)

    def wview(i, kind):
        par = i % NWB
        if kind == "dm":
            return wbuf[par].ap().rearrange("p (c m) -> p c m", m=DM)
        return wbuf[par].ap().rearrange("p (c m) -> p c m", m=FF)

    ig, iu, idn = 0, 1, 2
    gv = wview(ig, "ff")
    uv = wview(iu, "ff")
    dv = wview(idn, "dm")
    wait("tensor", WLD[wld_mark[ig][0]], wld_mark[ig][1])
    wait("tensor", WLD[wld_mark[iu][0]], wld_mark[iu][1])
    for t4 in range(4):
        if t4 >= 1:
            wait("vector", TC)
        for fc in range(FFC):
            gb = ("A", fc % 2)
            ub = ("A", 2 + fc % 2)
            bwait("tensor", gb)
            bwait("tensor", ub)
            gbs, ubs = 512 * gb[1], 512 * ub[1]
            for kc in range(DMC):
                fn = lambda h, fc=fc, t4=t4, kc=kc, gbs=gbs: h.matmul(
                    ps_A[:, gbs:gbs + 512], gv[:, kc, 128 * fc:128 * (fc + 1)],
                    xnv[:, kc, 512 * t4:512 * (t4 + 1)],
                    start=(kc == 0), stop=(kc == DMC - 1))
                if t4 == 3 and fc == FFC - 1 and kc == DMC - 1:
                    finc("tensor", WFR[ig], fn)
                else:
                    rawi("tensor", fn)
            for kc in range(DMC):
                fn = lambda h, fc=fc, t4=t4, kc=kc, ubs=ubs: h.matmul(
                    ps_A[:, ubs:ubs + 512], uv[:, kc, 128 * fc:128 * (fc + 1)],
                    xnv[:, kc, 512 * t4:512 * (t4 + 1)],
                    start=(kc == 0), stop=(kc == DMC - 1))
                if kc == DMC - 1:
                    if t4 == 3 and fc == FFC - 1:
                        finc("tensor", WFR[iu], fn)
                    else:
                        finc("tensor", TC, fn)
                else:
                    rawi("tensor", fn)
            if t4 == 3 and fc == FFC - 1:
                wait("scalar", WFR[iu])
            else:
                wait("scalar", TC)
            wait("scalar", VC)
            s_op(lambda h, gbs=gbs: h.activation(gtl[:], ps_A[:, gbs:gbs + 512], ACT.Silu))
            bset(gb, SCF)
            wait("vector", SCF)
            wait("vector", VC)
            finc("vector", VC, lambda h, fc=fc, ubs=ubs: h.tensor_tensor(
                ashv[:, fc, :], gtl[:], ps_A[:, ubs:ubs + 512], ALU.mult))
            bset(ub, VC)
        if t4 == 0:
            wait("tensor", WLD[wld_mark[idn][0]], wld_mark[idn][1])
            wait("vector", SST, SST_ALL)
        wait("tensor", VC)
        for vsub in range(4):
            vv = 4 * t4 + vsub
            if vv >= 1:
                wait("vector", SHST)
            for n2 in range(2):
                bank = ("B", n2)
                bwait("tensor", bank)
                bs = 512 * bank[1]
                for fc in range(FFC):
                    fn = lambda h, vsub=vsub, n2=n2, fc=fc, bs=bs: h.matmul(
                        ps_B[:, bs:bs + 512], ashv[:, fc, 128 * vsub:128 * (vsub + 1)],
                        dv[:, fc, 512 * n2:512 * (n2 + 1)],
                        start=(fc == 0), stop=(fc == FFC - 1))
                    if fc == FFC - 1:
                        if vv == TV - 1 and n2 == 1:
                            finc("tensor", WFR[idn], fn)
                        else:
                            finc("tensor", TC, fn)
                    else:
                        rawi("tensor", fn)
                if vv == TV - 1 and n2 == 1:
                    wait("vector", WFR[idn])
                else:
                    wait("vector", TC)
                wait("vector", VC)
                finc("vector", VC, lambda h, n2=n2, bs=bs: h.tensor_copy(
                    btile[:, 512 * n2:512 * (n2 + 1)], ps_B[:, bs:bs + 512]))
                bset(bank, VC)
            wait("scalar", VC)
            dma("scalar", SHST, sh_hbm[128 * vv:128 * (vv + 1), :], btile[:])
    SHST_ALL = SHST.n
    wrelease(ig)
    wrelease(iu)
    wrelease(idn)
    SHG_DONE = WFR[ig].n
    SHU_DONE = WFR[iu].n
    for _ in range(6 - NWB):
        wload()

    ctxB3b.close()
    xnT_base = nc.lookup_mloc(xnT).addr
    eo_t = nc.alloc_sbuf_tensor_at("eo_t", [128, RLC * DM], BF16, offset=xnT_base)
    xeT = [nc.alloc_sbuf_tensor_at(f"xeT{i}", [128, DMC * RL], BF16,
                                   offset=xnT_base + RLC * DM * 2 + i * DMC * RL * 2)
           for i in range(2)]
    a_e = nc.alloc_sbuf_tensor_at("a_e", [128, FFC * RL], BF16,
                                  offset=xnT_base + RLC * DM * 2 + 2 * DMC * RL * 2)
    assert RLC * DM * 2 + 2 * DMC * RL * 2 + FFC * RL * 2 <= DMC * TPC * 2
    aev = a_e.ap().rearrange("p (fc n) -> p fc n", n=RL)

    gxf_mark = [0, 0]
    for e_ in range(E):
        par = e_ % 2
        if e_ == 0:
            wait("gpsimd", WFR[0], SHG_DONE)
            wait("gpsimd", WFR[1], SHU_DONE)
        if e_ >= 2:
            wait("gpsimd", GXF[par], gxf_mark[par])
        wait("gpsimd", XBS, XBS_ALL)
        wait("gpsimd", STX, WRAP_ALL)
        finc("gpsimd", GX[par], lambda h, par=par, e_=e_: h.dma_gather(
            xeT[par].ap().rearrange("p (c n) -> p c n", n=RL), xb_hbm,
            gidx_w[:, (RL // 16) * e_:(RL // 16) * (e_ + 1)], RL, RL, DM,
            transpose=True), amt=16)
        if e_ == 0 and "xe0" in dbg:
            wait("scalar", GX[par])
            dma("scalar", SCD, dbg["xe0"], xeT[par][:])
        if e_ == 0 and "xb" in dbg:
            wait("scalar", XBS, XBS_ALL)
            dma("scalar", SCD, dbg["xb"], xb_hbm)
        iwg, iwu, iwd = 3 * e_ + 3, 3 * e_ + 4, 3 * e_ + 5
        gv_ = wview(iwg, "ff")
        uv_ = wview(iwu, "ff")
        xev = xeT[par].ap().rearrange("p (c n) -> p c n", n=RL)
        wait("tensor", GX[par])
        wait("tensor", WLD[wld_mark[iwg][0]], wld_mark[iwg][1])
        wait("tensor", WLD[wld_mark[iwu][0]], wld_mark[iwu][1])
        for fc in range(FFC):
            gb = ("A", fc % 2)
            ub = ("A", 2 + fc % 2)
            bwait("tensor", gb)
            bwait("tensor", ub)
            gbs, ubs = 512 * gb[1], 512 * ub[1]
            for kc in range(DMC):
                fn = lambda h, fc=fc, kc=kc, gbs=gbs, gv_=gv_, xev=xev: h.matmul(
                    ps_A[:, gbs:gbs + RL], gv_[:, kc, 128 * fc:128 * (fc + 1)],
                    xev[:, kc, :], start=(kc == 0), stop=(kc == DMC - 1))
                if fc == FFC - 1 and kc == DMC - 1:
                    finc("tensor", WFR[iwg % NWB], fn)
                else:
                    rawi("tensor", fn)
            for kc in range(DMC):
                fn = lambda h, fc=fc, kc=kc, ubs=ubs, uv_=uv_, xev=xev: h.matmul(
                    ps_A[:, ubs:ubs + RL], uv_[:, kc, 128 * fc:128 * (fc + 1)],
                    xev[:, kc, :], start=(kc == 0), stop=(kc == DMC - 1))
                if kc == DMC - 1:
                    if fc == FFC - 1:
                        finc("tensor", WFR[iwu % NWB], fn)
                    else:
                        finc("tensor", TC, fn)
                else:
                    rawi("tensor", fn)
            if fc == FFC - 1:
                wait("scalar", WFR[iwu % NWB])
            else:
                wait("scalar", TC)
            wait("scalar", VC)
            s_op(lambda h, gbs=gbs: h.activation(gtl[:, 0:RL], ps_A[:, gbs:gbs + RL], ACT.Silu))
            bset(gb, SCF)
            wait("vector", SCF)
            wait("vector", VC)
            finc("vector", VC, lambda h, fc=fc, ubs=ubs: h.tensor_tensor(
                aev[:, fc, :], gtl[:, 0:RL], ps_A[:, ubs:ubs + RL], ALU.mult))
            bset(ub, VC)
        if e_ == 0 and "ae0" in dbg:
            wait("scalar", VC)
            dma("scalar", SCD, dbg["ae0"], a_e[:])
        wrelease(iwg)
        wrelease(iwu)
        if e_ + 1 < E:
            wload()
            wload()
            wload()
        dv_ = wview(iwd, "dm")
        wait("tensor", WLD[wld_mark[iwd][0]], wld_mark[iwd][1])
        wait("tensor", VC)
        if e_ >= 1:
            wait("scalar", EOS)
        for rc in range(RLC):
            for n2 in range(2):
                bank = ("B", (rc * 2 + n2) % 2)
                bwait("tensor", bank)
                bs = 512 * bank[1]
                for fc in range(FFC):
                    fn = lambda h, rc=rc, n2=n2, fc=fc, bs=bs, dv_=dv_: h.matmul(
                        ps_B[:, bs:bs + 512], aev[:, fc, 128 * rc:128 * (rc + 1)],
                        dv_[:, fc, 512 * n2:512 * (n2 + 1)],
                        start=(fc == 0), stop=(fc == FFC - 1))
                    if fc == 0 and rc == 0 and n2 == 0:
                        finc("tensor", GXF[par], fn)
                    elif fc == FFC - 1:
                        if rc == RLC - 1 and n2 == 1:
                            finc("tensor", WFR[iwd % NWB], fn)
                        else:
                            finc("tensor", TC, fn)
                    else:
                        rawi("tensor", fn)
                if rc == RLC - 1 and n2 == 1:
                    wait("scalar", WFR[iwd % NWB])
                else:
                    wait("scalar", TC)
                s_op(lambda h, rc=rc, n2=n2, bs=bs: h.copy(
                    eo_t.ap().rearrange("p (r m) -> p r m", m=DM)[:, rc, 512 * n2:512 * (n2 + 1)],
                    ps_B[:, bs:bs + 512]))
                bset(bank, SCF)
        wait("scalar", SCF)
        dma("scalar", EOS, eo_hbm[RL * e_:RL * (e_ + 1), :].rearrange("(r p) m -> p r m", p=128),
            eo_t.ap().rearrange("p (r m) -> p r m", m=DM))
        wrelease(iwd)
        gxf_mark[par] = GXF[par].n
    EOS_ALL = EOS.n

    ctxB3a.close()
    ctxB1.close()
    ctxB0.close()
    ctxB4 = ExitStack()
    enB4 = ctxB4.enter_context
    g0b = enB4(nc.sbuf_tensor("g0b", [128, TV * DM], BF16))
    g1b = enB4(nc.sbuf_tensor("g1b", [128, TV * DM], BF16))
    sfull = enB4(nc.sbuf_tensor("sfull", [128, TV * DM], BF16))
    shfull = enB4(nc.sbuf_tensor("shfull", [128, TV * DM], BF16))
    ot2 = enB4(nc.sbuf_tensor("ot2", [128, DM], F32))
    xnb = [enB4(nc.sbuf_tensor(f"xnb{i}", [128, DM], BF16)) for i in range(2)]
    wait("gpsimd", EOS, EOS_ALL)
    wait("gpsimd", SCD, INIT_ALL)
    wait("gpsimd", STX, RWRAP_ALL)
    wait("gpsimd", SST, SST_ALL)
    wait("gpsimd", SHST, SHST_ALL)
    for (buf, wtab) in ((g0b, r0_w), (g1b, r1_w)):
        off = 0
        for n_ in (768, 768, 512):
            finc("gpsimd", CG, lambda h, buf=buf, wtab=wtab, off=off, n_=n_: h.dma_gather(
                buf.ap().rearrange("p (v m) -> p v m", m=DM)[:, off // 128:(off + n_) // 128, :],
                eo_hbm, wtab[:, off // 16:(off + n_) // 16], n_, n_, DM,
                transpose=False), amt=16)
            off += n_
    CG_ALL = CG.n

    wait("sync", SST, SST_ALL)
    wait("sync", SHST, SHST_ALL)
    wait("sync", EOS, EOS_ALL)
    for i_ in range(NWB):
        wait("sync", WFR[i_])
    wait("sync", TC)
    wait("sync", VC)
    wait("sync", SCF)
    dma("sync", PRE, sfull.ap().rearrange("p (v m) -> p v m", m=DM),
        s_hbm.rearrange("(v p) m -> p v m", p=128))
    dma("sync", PRE, shfull.ap().rearrange("p (v m) -> p v m", m=DM),
        sh_hbm.rearrange("(v p) m -> p v m", p=128))
    if "s" in dbg:
        wait("scalar", SST, SST_ALL)
        dma("scalar", SCD, dbg["s"], s_hbm)
    if "sh" in dbg:
        wait("scalar", SHST, SHST_ALL)
        dma("scalar", SCD, dbg["sh"], sh_hbm)
    if "eo" in dbg:
        wait("scalar", EOS, EOS_ALL)
        dma("scalar", SCD, dbg["eo"], eo_hbm)
    if "g0" in dbg:
        wait("scalar", CG, CG_ALL)
        dma("scalar", SCD, dbg["g0"], g0b[:])
        dma("scalar", SCD, dbg["g1"], g1b[:])
    sfv = sfull.ap().rearrange("p (v m) -> p v m", m=DM)
    shv = shfull.ap().rearrange("p (v m) -> p v m", m=DM)
    g0v = g0b.ap().rearrange("p (v m) -> p v m", m=DM)
    g1v = g1b.ap().rearrange("p (v m) -> p v m", m=DM)
    otl = [otile, ot2]
    OUTP = [OUTS, OUT2]
    outs_mark = [0, 0]
    wait("sync", XBS, XBS_ALL)
    for vv in range(TV):
        par = vv % 2
        wait("sync", XLF[par])
        dma("sync", XL[par], xtile[par][:], x_own[128 * vv:128 * (vv + 1), :])
        wait("sync", XNF[par])
        dma("sync", XNL[par], xnb[par][:], xb_hbm[128 * vv:128 * (vv + 1), :])
        wait("vector", XL[par])
        wait("vector", XNL[par])
        wait("vector", CG, CG_ALL)
        wait("vector", PRE)
        wait("vector", VC)
        finc("vector", VC, lambda h, vv=vv: h.tensor_scalar(
            dtile[:], g0v[:, vv, :], w0_pp[:, vv:vv + 1], None, ALU.mult))
        v_op(lambda h, vv=vv: h.tensor_scalar(ttile[:], g1v[:, vv, :], w1_pp[:, vv:vv + 1], None, ALU.mult))
        v_op(lambda h: h.tensor_tensor(dtile[:], dtile[:], ttile[:], ALU.add))
        wait("vector", VC)
        finc("vector", XNF[par], lambda h, par=par: h.tensor_tensor(
            dtile[:], dtile[:], xnb[par][:], ALU.add))
        wait("vector", XNF[par])
        v_op(lambda h, vv=vv: h.tensor_tensor(dtile[:], dtile[:], shv[:, vv, :], ALU.add))
        if "gs" in dbg:
            wait("scalar", VC)
            dma("scalar", SCD, dbg["gs"].rearrange("p (v m) -> p v m", m=DM)[:, vv, :], dtile[:])
        wait("scalar", VC)
        wait("scalar", OUTP[par], outs_mark[par])
        s_op(lambda h, par=par: h.activation(otl[par][:], dtile[:], ACT.Silu))
        wait("vector", SCF)
        v_op(lambda h, par=par, vv=vv: h.tensor_tensor(otl[par][:], otl[par][:], sfv[:, vv, :], ALU.mult))
        wait("vector", VC)
        finc("vector", XLF[par], lambda h, par=par: h.tensor_tensor(
            otl[par][:], otl[par][:], xtile[par][:], ALU.add))
        wait("scalar", XLF[par])
        dma("scalar", OUTP[par], out_hbm[128 * vv:128 * (vv + 1), :], otl[par][:])
        outs_mark[par] = OUTP[par].n
    wait("scalar", OUTS)
    wait("scalar", OUT2)

    block = en(nc.Block())
    P.replay(block)
    ctxB4.close()
    ctxT.close()
    ctx.close()
    return nc


_NC_CACHE = {}


def _get_nc(debug=()):
    key = tuple(debug)
    if key not in _NC_CACHE:
        nc = build_nc(debug=key)
        lower_extended_insts(nc)
        _NC_CACHE[key] = nc
    return _NC_CACHE[key]


def kernel(L0_attn_x, L0_attn_W_in, L0_attn_b_in, L0_attn_W_g, L0_attn_b_g,
           L0_attn_W_out, L0_attn_b_out, L0_ffn_router_w, L0_ffn_router_b,
           L0_ffn_w_gate, L0_ffn_w_up, L0_ffn_w_down, L0_ffn_sw_gate,
           L0_ffn_sw_up, L0_ffn_sw_down):
    x = np.ascontiguousarray(np.asarray(L0_attn_x, dtype=np.float32).reshape(NTOK, DM))
    xTf = np.ascontiguousarray(x.T)
    W_in = np.asarray(L0_attn_W_in, np.float32)
    consts = _host_constants()
    wsc = np.ascontiguousarray(np.concatenate(
        [np.asarray(L0_ffn_router_w, np.float32),
         W_in[:, DM:DM + 8],
         np.full((DM, 1), 1.0 / DM, np.float32)], axis=1))
    shared_map = {
        "xT": xTf,
        "w_small_c": wsc,
        "w_inT": np.ascontiguousarray(W_in.T),
        "w_g_in": np.ascontiguousarray(np.asarray(L0_attn_W_g, np.float32)),
        "b_in_col": np.ascontiguousarray(np.asarray(L0_attn_b_in, np.float32)[:, None]),
        "b_g8": np.ascontiguousarray(np.asarray(L0_attn_b_g, np.float32)[:, None]),
        "b_si32": np.ascontiguousarray(np.tile(np.asarray(L0_attn_b_in, np.float32)[DM:DM + 8], 4)[:, None]),
        "rb128": np.ascontiguousarray(np.tile(np.asarray(L0_ffn_router_b, np.float32), 8)[:, None]),
        "w_in_m": np.ascontiguousarray(W_in[:, 0:DM]),
        "w_out_in": np.ascontiguousarray(np.asarray(L0_attn_W_out, np.float32)),
        "b_out_row": np.ascontiguousarray(np.asarray(L0_attn_b_out, np.float32)[None, :]),
        "sw_gate": np.ascontiguousarray(np.asarray(L0_ffn_sw_gate, np.float32)),
        "sw_up": np.ascontiguousarray(np.asarray(L0_ffn_sw_up, np.float32)),
        "sw_down": np.ascontiguousarray(np.asarray(L0_ffn_sw_down, np.float32)),
        "w_gate": np.ascontiguousarray(np.asarray(L0_ffn_w_gate, np.float32)),
        "w_up": np.ascontiguousarray(np.asarray(L0_ffn_w_up, np.float32)),
        "w_down": np.ascontiguousarray(np.asarray(L0_ffn_w_down, np.float32)),
        "ones128": consts["ones128"],
        "E16dup": consts["E16dup"],
        "E8dup2": consts["E8dup2"],
        "P8": consts["P8"],
        "T16": consts["T16"],
        "BD16": consts["BD16"],
        "iotaE": consts["iotaE"],
        "slot1c": consts["slot1c"],
        "rowc": consts["rowc"],
    }
    in_maps = []
    for c in range(NCORE):
        sown, sst = _per_core_consts(c)
        m = dict(shared_map)
        m["xT_own"] = np.ascontiguousarray(xTf[:, TPC * c:TPC * (c + 1)])
        m["x_own"] = np.ascontiguousarray(x[TPC * c:TPC * (c + 1), :])
        m["selOwn"] = sown
        m["selST"] = sst
        in_maps.append(m)
    nc = _get_nc(DEBUG)
    res = run_bass_kernel_spmd(nc, in_maps, core_ids=list(range(NCORE)))
    out = np.concatenate([res.results[c]["out_own"] for c in range(NCORE)], axis=0)
    if DEBUG:
        kernel._dbg = [{k: v for k, v in r.items() if k.startswith("dbg_")}
                       for r in res.results]
    return out.reshape(4, 4096, DM)


# version 09
# speedup vs baseline: 1.0311x; 1.0311x over previous
import numpy as np
from contextlib import ExitStack

import concourse.bass as bass
import concourse.mybir as mybir
from concourse.bass_utils import run_bass_kernel_spmd
from concourse.library_config import mlp as _MLP_LIB
from concourse.library_overlay import lower_extended_insts

F32 = mybir.dt.float32
BF16 = mybir.dt.bfloat16
I16 = mybir.dt.int16
ALU = mybir.AluOpType
ACT = mybir.ActivationFunctionType

NCORE = 8
DM = 1024
DMC = 8
NTOK = 16384
TPC = 2048
TV = 16
NG = 8
E = 16
FF = 1408
FFC = 11
RL = 384
RLC = 3
NR = E * RL
NRV = NR // 128
NS = TPC * 2
NSV = NS // 128
CAP = 2560
TR1 = NR
TR2 = NS
EOTR = NR
EPS_LN = 1e-5

DEBUG = ()


def _host_constants():
    c = {}
    c["ones128"] = np.ones((1, 128), np.float32)
    e16 = np.zeros((8, 128), np.float32)
    for g in range(8):
        e16[g, 16 * g:16 * g + 16] = 1.0
    c["E16dup"] = e16
    e8 = np.zeros((8, 128), np.float32)
    for g in range(8):
        e8[g, 8 * g:8 * g + 8] = 1.0
        e8[g, 64 + 8 * g:64 + 8 * g + 8] = 1.0
    c["E8dup2"] = e8
    p8 = np.zeros((128, 128), np.float32)
    for g in range(8):
        for gp in range(g):
            for ee in range(16):
                p8[16 * gp + ee, 16 * g + ee] = 1.0
    c["P8"] = p8
    t16 = np.zeros((128, 128), np.float32)
    for g in range(8):
        for e1 in range(16):
            for e2 in range(e1, 16):
                t16[16 * g + e1, 16 * g + e2] = 1.0
    c["T16"] = t16
    bd = np.zeros((128, 128), np.float32)
    for g in range(8):
        bd[16 * g:16 * g + 16, 16 * g:16 * g + 16] = 1.0
    c["BD16"] = bd
    c["iotaE"] = (np.arange(128) % 16).astype(np.float32)[:, None]
    s1 = np.zeros((128, NSV, 64), np.float32)
    for m in range(NS):
        k, n = m // TPC, m % TPC
        i = (m % (NS // 16)) * 16 + m // (NS // 16)
        s1[i % 128, i // 128, 0] = n + 1
        s1[i % 128, i // 128, 1] = k
    c["slot1c"] = np.ascontiguousarray(s1.reshape(128, NSV * 64))
    rc = np.zeros((128, NRV, 64), np.float32)
    for m in range(NR):
        rc[m % 128, m // 128, 0] = m + 1
    c["rowc"] = np.ascontiguousarray(rc.reshape(128, NRV * 64))
    return c


def _per_core_consts(core):
    b_i, h_i = core // 2, core % 2
    sown = np.zeros((128, 16), np.float32)
    for j in range(16):
        sown[16 * core + j, j] = 1.0
    sst = np.zeros((64, 8), np.float32)
    for ch in range(8):
        sst[32 * h_i + 8 * b_i + ch, ch] = 1.0
    return sown, sst


class Sem:
    def __init__(self, en, nc, name):
        self.h = en(nc.semaphore(name))
        self.n = 0


class Prog:
    ENGS = ("sync", "scalar", "vector", "tensor", "gpsimd")

    def __init__(self):
        self.streams = {e: [] for e in self.ENGS}

    def emit(self, eng, fn):
        self.streams[eng].append(fn)

    def replay(self, block):
        for eng in self.ENGS:
            fns = self.streams[eng]

            def body(h, fns=fns):
                for f in fns:
                    f(h)

            getattr(block, eng)(body)


def build_nc(debug=()):
    nc = bass.Bass("TRN2", target_bir_lowering=False)

    def din(name, shape, dt=F32):
        return nc.dram_tensor(name, shape, dt, kind="ExternalInput").ap()

    xT = din("xT", [DM, NTOK])
    xT_own = din("xT_own", [DM, TPC])
    x_own = din("x_own", [TPC, DM])
    w_small_c = din("w_small_c", [DM, 25])
    w_inT = din("w_inT", [DM + 8, DM])
    w_g_in = din("w_g_in", [DM + 8, 8])
    b_in_col = din("b_in_col", [DM + 8, 1])
    b_g8 = din("b_g8", [8, 1])
    b_si32 = din("b_si32", [32, 1])
    rb128 = din("rb128", [128, 1])
    w_in_m = din("w_in_m", [DM, DM])
    w_out_in = din("w_out_in", [DM + 8, DM])
    b_out_row = din("b_out_row", [1, DM])
    sw_gate = din("sw_gate", [DM, FF])
    sw_up = din("sw_up", [DM, FF])
    sw_down = din("sw_down", [FF, DM])
    w_gate = din("w_gate", [E, DM, FF])
    w_up = din("w_up", [E, DM, FF])
    w_down = din("w_down", [E, FF, DM])
    ones128_in = din("ones128", [1, 128])
    E16_in = din("E16dup", [8, 128])
    E8_in = din("E8dup2", [8, 128])
    P8_in = din("P8", [128, 128])
    T16_in = din("T16", [128, 128])
    BD16_in = din("BD16", [128, 128])
    iota_in = din("iotaE", [128, 1])
    slot1c_in = din("slot1c", [128, NSV * 64])
    rowc_in = din("rowc", [128, NRV * 64])
    selOwn_in = din("selOwn", [128, 16])
    selST_in = din("selST", [64, 8])

    out_hbm = nc.dram_tensor("out_own", [TPC, DM], F32, kind="ExternalOutput").ap()

    xb_hbm = nc.dram_tensor("xb_hbm", [TPC, DM], BF16, kind="Internal").ap()
    s_hbm = nc.dram_tensor("s_hbm", [TPC, DM], BF16, kind="Internal").ap()
    sh_hbm = nc.dram_tensor("sh_hbm", [TPC, DM], BF16, kind="Internal").ap()
    inv1_hbm = nc.dram_tensor("inv1_hbm", [NR + 1, 64], F32, kind="Internal").ap()
    tb2_hbm = nc.dram_tensor("tb2_hbm", [NS + 1, 64], F32, kind="Internal").ap()
    eo_hbm = nc.dram_tensor("eo_hbm", [NR + 1, DM], BF16, kind="Internal").ap()
    NPRE = 9
    wbf_hbm = nc.dram_tensor("wbf_hbm", [NPRE, 128, 11264], BF16, kind="Internal").ap()

    dbg_shapes = {"l": [128, TPC], "w1": [128, TPC], "w2": [128, TPC],
                  "dr1": [128, TPC], "dr2": [128, TPC], "st": [32, 4096],
                  "cumG": [128, TPC], "i1": [128, TPC], "i2": [128, TPC],
                  "stT": [8, TPC], "gsig": [32, 4096], "sib": [32, 4096],
                  "idxf": [128, NRV], "kf": [128, NRV], "rid": [128, NSV],
                  "moe": [128, TV * DM], "gs": [128, TV * DM],
                  "s": [TPC, DM], "sh": [TPC, DM], "g0": [128, TV * DM],
                  "g1": [128, TV * DM], "eo": [NR + 1, DM],
                  "inv1": [NR + 1, 64], "drw": [128, NS // 16],
                  "xe0": [128, DMC * RL], "xb": [TPC, DM], "ae0": [128, FFC * RL],
                  "wg0": [128, 11264], "wu0": [128, 11264]}
    dbg = {nm: nc.dram_tensor("dbg_" + nm, dbg_shapes[nm],
                              BF16 if nm in ("stT", "s", "sh", "g0", "g1", "eo", "xe0", "xb", "ae0", "wg0", "wu0") else
                              (I16 if nm == "drw" else F32),
                              kind="ExternalOutput").ap()
           for nm in debug}

    ctx = ExitStack()
    en = ctx.enter_context

    def SB(nm, shp, dt=F32):
        return en(nc.sbuf_tensor(nm, shp, dt))

    wsmall = SB("wsmall", [128, DMC * 33], BF16)
    ones_bf = SB("ones_bf", [128, 1], BF16)
    ones1f = SB("ones1f", [1, 128], F32)
    ones33 = SB("ones33", [65, 128], F32)
    e16d = SB("e16d", [8, 128], F32)
    e8d = SB("e8d", [8, 128], F32)
    p8m = SB("p8m", [128, 128], F32)
    t16m = SB("t16m", [128, 128], F32)
    bd16m = SB("bd16m", [128, 128], F32)
    iota_pp = SB("iota_pp", [128, 1], F32)
    selown = SB("selown", [128, 16], F32)
    selst = SB("selst", [64, 8], F32)
    rb_pp = SB("rb_pp", [128, 1], F32)
    bsi_pp = SB("bsi_pp", [32, 1], F32)
    bg8s = SB("bg8s", [8, 1], F32)
    bgeff = SB("bgeff", [8, 1], F32)
    bgeff32 = SB("bgeff32", [32, 1], F32)
    scol_pp = SB("scol_pp", [128, 1], F32)
    sg_pp = SB("sg_pp", [128, 1], F32)
    bic = SB("bic", [128, DMC + 1], F32)
    bicb = SB("bicb", [128, DMC + 1], BF16)
    bout_sb = SB("bout_sb", [1, DM], F32)
    zsrc = SB("zsrc", [128, 512], F32)
    zsrc_bf = SB("zsrc_bf", [1, DM], BF16)
    bb_out = SB("bb_out", [128, DM], BF16)
    scol33 = SB("scol33", [33, 1], F32)
    gpre_pp = SB("gpre_pp", [128, 1], F32)
    epsc = SB("epsc", [128, 1], F32)
    mu_pp = SB("mu_pp", [128, TV], F32)
    is_pp = SB("is_pp", [128, TV], F32)
    w0_pp = SB("w0_pp", [128, TV], F32)
    w1_pp = SB("w1_pp", [128, TV], F32)
    stT = SB("stT", [8, TPC], BF16)
    dr_w = SB("dr_w", [128, NS // 16], I16)
    gidx_w = SB("gidx_w", [128, NR // 16], I16)
    s2_w = SB("s2_w", [128, NR // 16], I16)
    r0_w = SB("r0_w", [128, TPC // 16], I16)
    r1_w = SB("r1_w", [128, TPC // 16], I16)
    ownrowA = SB("ownrowA", [65, TPC], F32)
    ownrowB = SB("ownrowB", [65, TPC], F32)
    dr_i16 = SB("dr_i16", [33, TPC], I16)

    ctxA = ExitStack()
    enA = ctxA.enter_context

    def SA(nm, shp, dt=F32):
        return enA(nc.sbuf_tensor(nm, shp, dt))

    q_l = SA("q_l", [128, TPC], F32)
    q_rz = SA("q_rz", [128, TPC], F32)
    q_o1 = SA("q_o1", [128, TPC], F32)
    q_o2 = SA("q_o2", [128, TPC], F32)
    Lst = SA("Lst", [128, TPC], F32)
    q_cg = SA("q_cg", [128, TPC], F32)
    GSst = SA("GSst", [128, TPC], F32)
    q_e = SA("q_e", [128, TPC], F32)
    q_w1 = SA("q_w1", [128, TPC], F32)
    q_w2 = SA("q_w2", [128, TPC], F32)
    q_dr1 = SA("q_dr1", [128, TPC], F32)
    q_dr2 = SA("q_dr2", [128, TPC], F32)
    q_ex = SA("q_ex", [128, TPC], F32)
    q_t1 = SA("q_t1", [128, TPC], F32)
    q_t2 = SA("q_t2", [128, TPC], F32)
    q_bc1 = SA("q_bc1", [128, TPC], F32)
    q_bc2 = SA("q_bc2", [128, TPC], F32)
    mu8 = SA("mu8", [8, TPC], F32)
    is8 = SA("is8", [8, TPC], F32)
    winT_tl_t = SA("winT_tl_t", [8, DM], BF16)
    wgm_st = SA("wgm_st", [128, DMC * 8], BF16)
    wgm_tl_t = SA("wgm_tl_t", [8, 8], BF16)
    q_i1 = q_l
    q_i2 = Lst
    raws = q_bc1
    addr = lambda t: nc.lookup_mloc(t).addr
    ss8 = nc.alloc_sbuf_tensor_at("ss8", [8, TPC], F32, offset=addr(q_ex))
    sc8a = nc.alloc_sbuf_tensor_at("sc8a", [8, TPC], F32, offset=addr(q_e))
    assert addr(q_t2) == addr(q_t1) + 8192
    winT_st = nc.alloc_sbuf_tensor_at("winT_st", [128, DMC * DM], BF16, offset=addr(q_t1))
    xtb = [nc.alloc_sbuf_tensor_at(f"xtb{i}", [128, TPC], BF16,
                                   offset=addr(q_t1) + 4096 * i) for i in range(2)]
    xsq = [nc.alloc_sbuf_tensor_at(f"xsq{i}", [128, TPC], BF16,
                                   offset=addr(q_t1) + 8192 + 4096 * i) for i in range(2)]
    assert addr(q_rz) == addr(q_l) + 8192 and addr(q_o2) == addr(q_o1) + 8192
    assert addr(q_cg) == addr(Lst) + 8192
    sic = nc.alloc_sbuf_tensor_at("sic", [32, 4096], F32, offset=addr(q_l))
    gsc = nc.alloc_sbuf_tensor_at("gsc", [32, 4096], F32, offset=addr(q_o1))
    gsig = nc.alloc_sbuf_tensor_at("gsig", [32, 4096], F32, offset=addr(Lst))
    st64 = nc.alloc_sbuf_tensor_at("st64", [64, TPC], F32, offset=addr(GSst))
    ps_A = en(nc.psum_tensor("ps_A", [128, TPC], F32))
    ps_B = en(nc.psum_tensor("ps_B", [128, TPC], F32))

    sems = []

    def sem(name):
        s = Sem(en, nc, name)
        sems.append(s)
        return s

    LD = sem("LD")
    LDS = sem("LDS")
    TC = sem("TC")
    VC = sem("VC")
    SCF = sem("SCF")
    STX = sem("STX")
    SCD = sem("SCD")
    GPD = sem("GPD")
    XTs = [sem("XT0"), sem("XT1")]
    XTFs = [sem("XTF0"), sem("XTF1")]
    XSQs = [sem("XSQ0"), sem("XSQ1")]
    WLD = [sem(f"WLD{i}") for i in range(5)]
    WFR = [sem(f"WFR{i}") for i in range(5)]
    GX = [sem("GX0"), sem("GX1")]
    GXF = [sem("GXF0"), sem("GXF1")]
    XL = [sem("XL0"), sem("XL1")]
    XLF = [sem("XLF0"), sem("XLF1")]
    XOT = [sem("XOT0"), sem("XOT1")]
    XOF = [sem("XOF0"), sem("XOF1")]
    XNL = [sem("XNL0"), sem("XNL1")]
    XNF = [sem("XNF0"), sem("XNF1")]
    OUT2 = sem("OUT2")
    PCW = sem("PCW")
    XBS = sem("XBS")
    SST = sem("SST")
    SHST = sem("SHST")
    EOS = sem("EOS")
    OUTS = sem("OUTS")
    MOE = sem("MOE")
    CG = sem("CG")
    PRE = sem("PRE")

    P = Prog()

    def emit(eng, fn):
        P.emit(eng, fn)

    def wait(eng, s, val=None):
        v = s.n if val is None else val
        if v > 0:
            emit(eng, lambda h: h.wait_ge(s.h, v))

    def dma(eng, s, out, in_, slow=False):
        s.n += 16
        emit(eng, lambda h: h.dma_start(out, in_, allow_slow_non_contiguous=slow)
             .then_inc(s.h, 16))

    def finc(eng, s, fn, amt=1):
        s.n += amt
        emit(eng, lambda h: fn(h).then_inc(s.h, amt))

    def rawi(eng, fn):
        emit(eng, lambda h: fn(h))

    def v_op(fn):
        wait("vector", VC)
        finc("vector", VC, fn)

    def s_op(fn):
        wait("scalar", SCF)
        finc("scalar", SCF, fn)

    def dbg_dump(nm, src):
        if nm in dbg:
            wait("scalar", VC)
            wait("scalar", SCF)
            dma("scalar", SCD, dbg[nm], src)

    bank_free = {}

    def bwait(eng, key):
        if key in bank_free:
            s, cnt = bank_free[key]
            wait(eng, s, cnt)

    def bset(key, s):
        bank_free[key] = (s, s.n)

    rawi("gpsimd", lambda h: h.load_library(_MLP_LIB))
    wsv = wsmall.ap().rearrange("p (kc c) -> p kc c", c=33)
    dma("gpsimd", LD, wsv[:, :, 0:25], w_small_c.rearrange("(kc p) c -> p kc c", p=128))
    winT_v = winT_st.ap().rearrange("p (kc m) -> p kc m", m=DM)
    dma("gpsimd", LD, winT_v[:], w_inT[0:DM, :].rearrange("(kc p) m -> p kc m", p=128))
    winT_tl = winT_tl_t[:]
    dma("gpsimd", LD, winT_tl, w_inT[DM:DM + 8, :])
    wgm_v = wgm_st.ap().rearrange("p (kc c) -> p kc c", c=8)
    dma("gpsimd", LD, wgm_v[:], w_g_in[0:DM, :].rearrange("(kc p) c -> p kc c", p=128))
    wgm_tl = wgm_tl_t[:]
    dma("gpsimd", LD, wgm_tl, w_g_in[DM:DM + 8, :])
    for dst, src in [(ones1f[:], ones128_in), (e16d[:], E16_in), (e8d[:], E8_in),
                     (p8m[:], P8_in), (t16m[:], T16_in), (bd16m[:], BD16_in),
                     (iota_pp[:], iota_in), (selown[:], selOwn_in), (selst[:], selST_in),
                     (rb_pp[:], rb128), (bsi_pp[:], b_si32), (bg8s[:], b_g8)]:
        dma("sync", LDS, dst, src)
    bicv = bic.ap().rearrange("p (c o) -> p c o", o=1)
    dma("sync", LDS, bicv[:, 0:DMC, :], b_in_col[0:DM, :].rearrange("(c p) o -> p c o", p=128), slow=True)
    dma("sync", LDS, bic[0:8, DMC:DMC + 1], b_in_col[DM:DM + 8, :])
    dma("sync", LDS, bout_sb[:], b_out_row)
    v_op(lambda h: h.memset(zsrc[:], 0.0))
    v_op(lambda h: h.memset(zsrc_bf[:], 0.0))
    v_op(lambda h: h.memset(ones_bf[:], 1.0))
    v_op(lambda h: h.memset(epsc[:], float(EPS_LN)))
    v_op(lambda h: h.memset(ones33[:], 1.0))
    wait("vector", LD)
    wait("vector", LDS)
    v_op(lambda h: h.tensor_copy(bicb[:, 0:DMC], bic[:, 0:DMC]))
    v_op(lambda h: h.tensor_copy(bicb[0:8, DMC:DMC + 1], bic[0:8, DMC:DMC + 1]))

    wait("tensor", LD)
    wait("tensor", LDS)
    wait("tensor", VC)
    for mc in range(DMC):
        bwait("tensor", ("A", 0))
        for kc in range(DMC):
            rawi("tensor", lambda h, mc=mc, kc=kc: h.matmul(
                ps_A[:, 0:8], winT_v[:, kc, 128 * mc:128 * (mc + 1)], wgm_v[:, kc, :],
                start=(kc == 0), stop=False))
        finc("tensor", TC, lambda h, mc=mc: h.matmul(
            ps_A[:, 0:8], winT_tl_t[:, 128 * mc:128 * (mc + 1)], wgm_tl,
            start=False, stop=True))
        wait("scalar", TC)
        s_op(lambda h, mc=mc: h.copy(wsv[:, mc, 25:33], ps_A[:, 0:8]))
        bset(("A", 0), SCF)
    for kc in range(DMC):
        rawi("tensor", lambda h, kc=kc: h.matmul(
            ps_B[0:8, 0:1], wgm_v[:, kc, :],
            bicb.ap().rearrange("p (c o) -> p c o", o=1)[:, kc, :], start=(kc == 0), stop=False))
    finc("tensor", TC, lambda h: h.matmul(
        ps_B[0:8, 0:1], wgm_tl, bicb[0:8, DMC:DMC + 1], start=False, stop=True))
    wait("vector", TC)
    v_op(lambda h: h.tensor_tensor(bgeff[:], ps_B[0:8, 0:1], bg8s[:], ALU.add))
    wait("sync", VC)
    for r in range(4):
        dma("sync", STX, bgeff32[8 * r:8 * r + 8, :], bgeff[:])
    for nn in range(2):
        wait("tensor", VC)
        finc("tensor", TC, lambda h, nn=nn: h.matmul(
            ps_B[:, 512:1024], ones1f[:], bout_sb[:, 512 * nn:512 * (nn + 1)],
            start=True, stop=True))
        wait("vector", TC)
        v_op(lambda h, nn=nn: h.tensor_copy(bb_out[:, 512 * nn:512 * (nn + 1)],
                                            ps_B[:, 512:1024]))
    wait("tensor", SCF)
    wait("tensor", VC)
    for kc in range(DMC):
        fn = lambda h, kc=kc: h.matmul(ps_B[0:33, 1024:1025], wsv[:, kc, :], ones_bf[:],
                                       start=(kc == 0), stop=(kc == DMC - 1))
        finc("tensor", TC, fn) if kc == DMC - 1 else rawi("tensor", fn)
    wait("scalar", TC)
    s_op(lambda h: h.copy(scol33[:], ps_B[0:33, 1024:1025]))
    wait("sync", SCF)
    for g in range(8):
        dma("sync", STX, scol_pp[16 * g:16 * g + 16, :], scol33[0:16, :])
        dma("sync", STX, sg_pp[8 * g:8 * g + 8, :], scol33[16:24, :])
        dma("sync", STX, sg_pp[64 + 8 * g:64 + 8 * g + 8, :], scol33[25:33, :])

    wstream = [("swg", sw_gate, "ff"), ("swu", sw_up, "ff"), ("swd", sw_down, "dm")]
    for e_ in range(E):
        wstream += [(f"wg{e_}", w_gate[e_], "ff"), (f"wu{e_}", w_up[e_], "ff"),
                    (f"wd{e_}", w_down[e_], "dm")]
    pre_state = {"i": 0}

    def precast_one():
        i = pre_state["i"]
        if i >= NPRE:
            return
        nm, src_, kind = wstream[i]
        m2 = DM if kind == "dm" else FF
        dma("gpsimd", PCW, wbf_hbm[i].rearrange("p (c m2) -> p c m2", m2=m2),
            src_.rearrange("(c p) m2 -> p c m2", p=128))
        pre_state["i"] += 1

    sc_after_g = {}
    stx_after_g = {}
    wait("gpsimd", TC)
    for gg in range(NG):
        for kc in range(DMC):
            it = gg * DMC + kc
            par = it % 2
            if it >= 2:
                wait("gpsimd", XTFs[par])
                wait("gpsimd", XSQs[par])
            dma("gpsimd", XTs[par], xtb[par][:],
                xT[128 * kc:128 * (kc + 1), TPC * gg:TPC * (gg + 1)])
            wait("vector", XTs[par])
            if it >= 2:
                wait("vector", XTFs[par])
            finc("vector", XSQs[par],
                 lambda h, par=par: h.tensor_tensor(xsq[par][:], xtb[par][:], xtb[par][:], ALU.mult))
            wait("tensor", XTs[par])
            wait("tensor", XSQs[par])
            if kc == 0:
                bwait("tensor", ("A", "P1"))
                bwait("tensor", ("B", "P1"))
            for n4 in range(4):
                rawi("tensor", lambda h, kc=kc, par=par, n4=n4: h.matmul(
                    ps_A[0:33, 512 * n4:512 * (n4 + 1)], wsv[:, kc, :],
                    xtb[par][:, 512 * n4:512 * (n4 + 1)],
                    start=(kc == 0), stop=(kc == DMC - 1)))
            for n4 in range(4):
                fn = lambda h, kc=kc, par=par, n4=n4: h.matmul(
                    ps_B[64:65, 512 * n4:512 * (n4 + 1)], ones_bf[:],
                    xsq[par][:, 512 * n4:512 * (n4 + 1)],
                    start=(kc == 0), stop=(kc == DMC - 1))
                finc("tensor", XTFs[par], fn) if n4 == 3 else rawi("tensor", fn)
        lastpar = (gg * DMC + DMC - 1) % 2
        wait("scalar", XTFs[lastpar])
        if gg >= 1:
            wait("scalar", STX, stx_after_g[gg - 1])
        s_op(lambda h: h.copy(raws[0:33, :], ps_A[0:33, :]))
        s_op(lambda h: h.copy(raws[64:65, :], ps_B[64:65, 0:TPC]))
        sc_after_g[gg] = SCF.n
        bset(("A", "P1"), SCF)
        bset(("B", "P1"), SCF)
        wait("sync", SCF, sc_after_g[gg])
        dma("sync", STX, Lst[16 * gg:16 * gg + 16, :], raws[0:16, :])
        dma("sync", STX, GSst[8 * gg:8 * gg + 8, :], raws[16:24, :])
        dma("sync", STX, GSst[64 + 8 * gg:64 + 8 * gg + 8, :], raws[25:33, :])
        dma("sync", STX, mu8[gg:gg + 1, :], raws[24:25, :])
        dma("sync", STX, ss8[gg:gg + 1, :], raws[64:65, :])
        stx_after_g[gg] = STX.n

    while pre_state["i"] < NPRE:
        precast_one()
    PRE_ALL = PCW.n

    wait("vector", STX)
    v_op(lambda h: h.tensor_tensor(sc8a[:], mu8[:], mu8[:], ALU.mult))
    v_op(lambda h: h.tensor_scalar(is8[:], ss8[:], 1.0 / DM, None, ALU.mult))
    v_op(lambda h: h.tensor_tensor(is8[:], is8[:], sc8a[:], ALU.subtract))
    wait("scalar", VC)
    s_op(lambda h: h.activation(sc8a[:], is8[:], ACT.Sqrt, bias=epsc[0:8, :], scale=1.0))
    wait("vector", SCF)
    v_op(lambda h: h.reciprocal(is8[:], sc8a[:]))

    def mm128(dst_sb, lhs, src, psl=ps_A, rows=128):
        wait("tensor", VC)
        wait("tensor", SCF)
        for n4 in range(4):
            fn = lambda h, n4=n4: h.matmul(psl[0:rows, 512 * n4:512 * (n4 + 1)],
                                           lhs if isinstance(lhs, bass.AP) else lhs[:],
                                           src[:, 512 * n4:512 * (n4 + 1)],
                                           start=True, stop=True)
            finc("tensor", TC, fn) if n4 == 3 else rawi("tensor", fn)
        wait("scalar", TC)
        s_op(lambda h: h.copy(dst_sb[0:rows, 0:TPC] if rows != 128 else dst_sb[:],
                              psl[0:rows, 0:TPC]))

    mm128(q_bc1, e16d, mu8)
    mm128(q_bc2, e16d, is8)
    wait("vector", SCF)
    v_op(lambda h: h.tensor_scalar(q_l[:], q_bc1[:], scol_pp[:], None, ALU.mult))
    v_op(lambda h: h.tensor_tensor(q_l[:], Lst[:], q_l[:], ALU.subtract))
    v_op(lambda h: h.tensor_tensor(q_l[:], q_l[:], q_bc2[:], ALU.mult))
    v_op(lambda h: h.tensor_scalar(q_l[:], q_l[:], rb_pp[:], None, ALU.add))
    dbg_dump("l", q_l[:])

    shmask = lambda d: [(p // 16) * 16 + ((p % 16) + d) % 16 for p in range(32)]

    def tree_max(buf, tmp):
        for d in (8, 4, 2, 1):
            v_op(lambda h, d=d, buf=buf, tmp=tmp: h.stream_shuffle(tmp[:], buf[:], shmask(d)))
            v_op(lambda h, buf=buf, tmp=tmp: h.tensor_tensor(buf[:], buf[:], tmp[:], ALU.max))

    v_op(lambda h: h.tensor_copy(q_t1[:], q_l[:]))
    tree_max(q_t1, q_t2)
    v_op(lambda h: h.tensor_tensor(q_t2[:], q_l[:], q_t1[:], ALU.subtract))
    wait("scalar", VC)
    s_op(lambda h: h.activation(q_e[:], q_t2[:], ACT.Exp))
    wait("vector", SCF)
    mm128(q_rz, bd16m, q_e)
    wait("vector", SCF)
    v_op(lambda h: h.reciprocal(q_rz[:], q_rz[:]))
    v_op(lambda h: h.tensor_scalar(q_o1[:], q_e[:], 1.0, None, ALU.is_equal))
    mm128(q_t2, t16m, q_o1)
    wait("vector", SCF)
    v_op(lambda h: h.tensor_scalar(q_t2[:], q_t2[:], 1.0, None, ALU.is_equal))
    v_op(lambda h: h.tensor_tensor(q_o1[:], q_o1[:], q_t2[:], ALU.mult))
    v_op(lambda h: h.tensor_scalar(q_t2[:], q_o1[:], iota_pp[:], None, ALU.mult))
    mm128(q_i1, bd16m, q_t2)
    wait("vector", SCF)
    v_op(lambda h: h.tensor_tensor(q_t2[:], q_e[:], q_o1[:], ALU.mult))
    v_op(lambda h: h.tensor_tensor(q_t2[:], q_e[:], q_t2[:], ALU.subtract))
    v_op(lambda h: h.tensor_copy(q_w2[:], q_t2[:]))
    tree_max(q_w2, q_t1)
    v_op(lambda h: h.tensor_tensor(q_o2[:], q_t2[:], q_w2[:], ALU.is_equal))
    mm128(q_t1, t16m, q_o2)
    wait("vector", SCF)
    v_op(lambda h: h.tensor_scalar(q_t1[:], q_t1[:], 1.0, None, ALU.is_equal))
    v_op(lambda h: h.tensor_tensor(q_o2[:], q_o2[:], q_t1[:], ALU.mult))
    v_op(lambda h: h.tensor_scalar(q_t1[:], q_o2[:], iota_pp[:], None, ALU.mult))
    mm128(q_i2, bd16m, q_t1)
    dbg_dump("i1", q_i1[:])
    dbg_dump("i2", q_i2[:])
    wait("vector", SCF)
    v_op(lambda h: h.tensor_tensor(q_w2[:], q_w2[:], q_rz[:], ALU.mult))
    v_op(lambda h: h.tensor_tensor(q_t1[:], q_rz[:], q_w2[:], ALU.add))
    v_op(lambda h: h.tensor_scalar(q_t1[:], q_t1[:], 1e-9, None, ALU.add))
    v_op(lambda h: h.reciprocal(q_t1[:], q_t1[:]))
    v_op(lambda h: h.tensor_tensor(q_w1[:], q_rz[:], q_t1[:], ALU.mult))
    v_op(lambda h: h.tensor_tensor(q_w2[:], q_w2[:], q_t1[:], ALU.mult))
    v_op(lambda h: h.tensor_tensor(q_t1[:], q_o1[:], q_o2[:], ALU.add))
    v_op(lambda h: h.tensor_tensor_scan(q_ex[:], q_t1[:], q_t1[:], 0.0, ALU.add, ALU.bypass))
    v_op(lambda h: h.tensor_copy(q_t2[:, 0:1], q_ex[:, TPC - 1:TPC]))
    v_op(lambda h: h.tensor_tensor(q_ex[:], q_ex[:], q_t1[:], ALU.subtract))
    wait("tensor", VC)
    wait("tensor", SCF)
    finc("tensor", TC, lambda h: h.matmul(ps_B[:, 0:1], p8m[:], q_t2[:, 0:1],
                                          start=True, stop=True))
    wait("scalar", TC)
    s_op(lambda h: h.copy(gpre_pp[:], ps_B[:, 0:1]))
    wait("vector", SCF)
    v_op(lambda h: h.tensor_scalar(q_cg[:], q_ex[:], gpre_pp[:], None, ALU.add))
    dbg_dump("cumG", q_cg[:])

    def pos_k(oh, ii, wv, drv):
        v_op(lambda h: h.tensor_tensor(q_t1[:], oh[:], q_cg[:], ALU.mult))
        mm128(q_t2, bd16m, q_t1)
        wait("vector", SCF)
        v_op(lambda h: h.tensor_scalar(q_t2[:], q_t2[:], float(CAP), None, ALU.is_lt))
        v_op(lambda h: h.tensor_tensor(wv[:], wv[:], q_t2[:], ALU.mult))
        v_op(lambda h: h.tensor_tensor(q_t1[:], oh[:], q_ex[:], ALU.mult))
        mm128(drv, bd16m, q_t1)
        wait("vector", SCF)
        v_op(lambda h: h.tensor_scalar(q_t1[:], drv[:], float(RL), None, ALU.is_lt))
        v_op(lambda h: h.tensor_tensor(q_t1[:], q_t1[:], q_t2[:], ALU.mult))
        v_op(lambda h: h.tensor_scalar(q_t2[:], ii[:], float(RL), None, ALU.mult))
        v_op(lambda h: h.tensor_tensor(drv[:], q_t2[:], drv[:], ALU.add))
        v_op(lambda h: h.tensor_tensor(drv[:], drv[:], q_t1[:], ALU.mult))
        v_op(lambda h: h.tensor_scalar(q_t1[:], q_t1[:], -float(TR1), float(TR1), ALU.mult, ALU.add))
        v_op(lambda h: h.tensor_tensor(drv[:], drv[:], q_t1[:], ALU.add))

    pos_k(q_o1, q_i1, q_w1, q_dr1)
    pos_k(q_o2, q_i2, q_w2, q_dr2)
    dbg_dump("w1", q_w1[:])
    dbg_dump("w2", q_w2[:])
    dbg_dump("dr1", q_dr1[:])
    dbg_dump("dr2", q_dr2[:])

    mm128(q_t1, e8d, mu8)
    mm128(q_t2, e8d, is8)
    wait("vector", SCF)
    v_op(lambda h: h.tensor_scalar(q_e[:], q_t1[:], sg_pp[:], None, ALU.mult))
    v_op(lambda h: h.tensor_tensor(q_e[:], GSst[:], q_e[:], ALU.subtract))
    v_op(lambda h: h.tensor_tensor(q_e[:], q_e[:], q_t2[:], ALU.mult))
    wait("sync", VC)
    wait("sync", SCF)
    for bb_ in range(4):
        for hh in range(2):
            g_ = 2 * bb_ + hh
            dma("sync", STX, sic.ap()[8 * bb_:8 * bb_ + 8, 2048 * hh:2048 * (hh + 1)],
                q_e[8 * g_:8 * g_ + 8, :])
            dma("sync", STX, gsc.ap()[8 * bb_:8 * bb_ + 8, 2048 * hh:2048 * (hh + 1)],
                q_e[64 + 8 * g_:64 + 8 * g_ + 8, :])
    wait("scalar", STX)
    s_op(lambda h: h.activation(gsig.ap()[0:32, :], gsc.ap()[0:32, :], ACT.Sigmoid,
                                bias=bgeff32[:], scale=1.0))
    dbg_dump("gsig", gsig.ap()[0:32, :])
    wait("vector", STX)
    wait("vector", SCF)
    v_op(lambda h: h.tensor_scalar(sic.ap()[0:32, :], sic.ap()[0:32, :], bsi_pp[:], None, ALU.add))
    dbg_dump("sib", sic.ap()[0:32, :])
    v_op(lambda h: h.tensor_scalar(gsc.ap()[0:32, :], gsig.ap()[0:32, :], -1.0, 1.0, ALU.mult, ALU.add))
    v_op(lambda h: h.tensor_tensor(gsc.ap()[0:32, :], gsc.ap()[0:32, :], sic.ap()[0:32, :], ALU.mult))
    v_op(lambda h: h.tensor_tensor_scan(sic.ap()[0:32, :], gsig.ap()[0:32, :], gsc.ap()[0:32, :],
                                        0.0, ALU.mult, ALU.add))
    dbg_dump("st", sic.ap()[0:32, :])
    wait("sync", VC)
    for hh in range(2):
        dma("sync", STX, st64.ap()[32 * hh:32 * hh + 32, :],
            sic.ap()[0:32, 2048 * hh:2048 * (hh + 1)])
    wait("tensor", STX)
    wait("tensor", SCF)
    wait("tensor", VC)
    for n4 in range(4):
        fn = lambda h, n4=n4: h.matmul(ps_B[0:8, 512 * n4:512 * (n4 + 1)], selst[:],
                                       st64.ap()[:, 512 * n4:512 * (n4 + 1)], start=True, stop=True)
        finc("tensor", TC, fn) if n4 == 3 else rawi("tensor", fn)
    wait("scalar", TC)
    s_op(lambda h: h.copy(stT[:], ps_B[0:8, 0:TPC]))
    dbg_dump("stT", stT[:])

    mm128(q_bc1, e16d, mu8)
    mm128(q_bc2, e16d, is8)
    extr = [(ps_A, 0, ownrowA, 0, q_bc1), (ps_A, 32, ownrowA, 32, q_w1),
            (ps_A, 64, ownrowA, 64, q_w2), (ps_B, 0, ownrowB, 0, q_dr1),
            (ps_B, 32, ownrowB, 32, q_dr2), (ps_B, 64, ownrowB, 64, q_bc2)]
    for psx, pb, dstT, drow, srcq in extr:
        wait("tensor", VC)
        wait("tensor", SCF)
        for n4 in range(4):
            fn = lambda h, n4=n4, srcq=srcq, psx=psx, pb=pb: h.matmul(
                psx[pb:pb + 16, 512 * n4:512 * (n4 + 1)], selown[:],
                srcq[:, 512 * n4:512 * (n4 + 1)], start=True, stop=True)
            finc("tensor", TC, fn) if n4 == 3 else rawi("tensor", fn)
        wait("scalar", TC)
        s_op(lambda h, psx=psx, pb=pb, dstT=dstT, drow=drow: h.copy(
            dstT[drow:drow + 1, :], psx[pb:pb + 1, 0:TPC]))
    wait("vector", SCF)
    v_op(lambda h: h.tensor_copy(dr_i16[0:1, :], ownrowB[0:1, :]))
    v_op(lambda h: h.tensor_copy(dr_i16[32:33, :], ownrowB[32:33, :]))
    wait("gpsimd", VC)
    dma("gpsimd", GPD, dr_w[0:8, :], dr_i16[0:1, :].rearrange("o (q j) -> o q j", q=8))
    dma("gpsimd", GPD, dr_w[8:16, :], dr_i16[32:33, :].rearrange("o (q j) -> o q j", q=8))
    wait("gpsimd", GPD)
    for rep in range(1, 8):
        dma("gpsimd", GPD, dr_w[16 * rep:16 * (rep + 1), :], dr_w[0:16, :])
    wait("sync", SCF)
    for dst, srcT, row in [(w0_pp, ownrowA, 32), (w1_pp, ownrowA, 64),
                           (mu_pp, ownrowA, 0), (is_pp, ownrowB, 64)]:
        for vv in range(TV):
            dma("sync", STX, dst[:, vv:vv + 1],
                srcT[row:row + 1, 128 * vv:128 * (vv + 1)].rearrange("o (q c) -> o q c", c=1),
                slow=True)

    ctxA.close()
    ctxT = ExitStack()
    xtile = [ctxT.enter_context(nc.sbuf_tensor(f"xtile{i}", [128, DM], F32)) for i in range(2)]
    ttile = ctxT.enter_context(nc.sbuf_tensor("ttile", [128, DM], F32))
    dtile = ctxT.enter_context(nc.sbuf_tensor("dtile", [128, DM], F32))
    otile = ctxT.enter_context(nc.sbuf_tensor("otile", [128, DM], F32))
    ctxB0 = ExitStack()
    enB0 = ctxB0.enter_context
    btile = enB0(nc.sbuf_tensor("btile", [128, DM], BF16))
    gtl = enB0(nc.sbuf_tensor("gtl", [128, 512], F32))
    ctxB1 = ExitStack()
    xnT = ctxB1.enter_context(nc.sbuf_tensor("xnT", [128, DMC * TPC], BF16))
    ctxB2a = ExitStack()
    embed = ctxB2a.enter_context(nc.sbuf_tensor("embed", [128, DMC * TPC], BF16))
    ctxB5t = ExitStack()
    enB5 = ctxB5t.enter_context
    tbl1 = enB5(nc.sbuf_tensor("tbl1", [128, NRV * 64], F32))
    tbl2 = enB5(nc.sbuf_tensor("tbl2", [128, NSV * 64], F32))
    slot1s = nc.alloc_sbuf_tensor_at("slot1s", [128, NSV * 64], F32,
                                     offset=nc.lookup_mloc(tbl1).addr)
    idx_f = enB5(nc.sbuf_tensor("idx_f", [128, NRV], F32))
    k_f = enB5(nc.sbuf_tensor("k_f", [128, NRV], F32))
    mskE = enB5(nc.sbuf_tensor("mskE", [128, NRV], F32))
    gidx_f = enB5(nc.sbuf_tensor("gidx_f", [128, NRV], F32))
    s2_f = enB5(nc.sbuf_tensor("s2_f", [128, NRV], F32))
    scrE = enB5(nc.sbuf_tensor("scrE", [128, NRV], F32))
    gidx_i = enB5(nc.sbuf_tensor("gidx_i", [128, NRV], I16))
    s2_i = enB5(nc.sbuf_tensor("s2_i", [128, NRV], I16))
    rid_f = enB5(nc.sbuf_tensor("rid_f", [128, NSV], F32))
    rid_i = enB5(nc.sbuf_tensor("rid_i", [128, NSV], I16))
    rowcs = nc.alloc_sbuf_tensor_at("rowcs", [128, NRV * 64], F32,
                                    offset=nc.lookup_mloc(tbl1).addr)
    wait("scalar", VC, 3)
    zv8 = zsrc.ap().rearrange("p (v c) -> p v c", c=64)
    for c6 in range(6):
        dma("scalar", SCD, inv1_hbm[1024 * c6:1024 * (c6 + 1), :].rearrange("(v p) c -> p v c", p=128),
            zv8[:, 0:8, :])
    dma("scalar", SCD, inv1_hbm[NR:NR + 1, :], zsrc[0:1, 0:64])
    for c4 in range(4):
        dma("scalar", SCD, tb2_hbm[1024 * c4:1024 * (c4 + 1), :].rearrange("(v p) c -> p v c", p=128),
            zv8[:, 0:8, :])
    dma("scalar", SCD, tb2_hbm[NS:NS + 1, :], zsrc[0:1, 0:64])
    dma("scalar", SCD, eo_hbm[EOTR:EOTR + 1, :], zsrc_bf[0:1, :])
    INIT_ALL = SCD.n
    wait("sync", VC)
    wait("sync", TC)
    wait("sync", SCF)
    dma("sync", STX, slot1s[:], slot1c_in)
    PAY_ALL = STX.n
    ctxB2b = ExitStack()
    winm_sb = ctxB2b.enter_context(nc.sbuf_tensor("winm_sb", [128, DMC * DM], BF16))
    wout_sb = ctxB2b.enter_context(nc.sbuf_tensor("wout_sb", [128, DMC * DM], BF16))
    wout_tl = ctxB2b.enter_context(nc.sbuf_tensor("wout_tl", [8, DM], BF16))
    mubc = ctxB2b.enter_context(nc.sbuf_tensor("mubc", [128, TPC], BF16))
    isbc = ctxB2b.enter_context(nc.sbuf_tensor("isbc", [128, TPC], BF16))
    xob = [ctxB2b.enter_context(nc.sbuf_tensor(f"xob{i}", [128, TPC], BF16)) for i in range(2)]
    for dst, srcT, row in [(mubc, ownrowA, 0), (isbc, ownrowB, 64)]:
        wait("tensor", SCF)
        wait("tensor", VC)
        for n4 in range(4):
            fn = lambda h, n4=n4, srcT=srcT, row=row: h.matmul(
                ps_A[:, 512 * n4:512 * (n4 + 1)], ones33[row:row + 1, :],
                srcT[row:row + 1, 512 * n4:512 * (n4 + 1)], start=True, stop=True)
            finc("tensor", TC, fn) if n4 == 3 else rawi("tensor", fn)
        wait("scalar", TC)
        s_op(lambda h, dst=dst: h.copy(dst[:], ps_A[:, 0:TPC]))
    wait("gpsimd", VC)
    wait("gpsimd", TC)
    dma("gpsimd", LD, winm_sb.ap().rearrange("p (kc m) -> p kc m", m=DM),
        w_in_m.rearrange("(kc p) m -> p kc m", p=128))
    dma("gpsimd", LD, wout_sb.ap().rearrange("p (kc m) -> p kc m", m=DM),
        w_out_in[0:DM, :].rearrange("(kc p) m -> p kc m", p=128))
    dma("gpsimd", LD, wout_tl[:], w_out_in[DM:DM + 8, :])
    wait("gpsimd", SCD, INIT_ALL)
    wait("gpsimd", STX, PAY_ALL)
    wait("gpsimd", GPD)
    finc("gpsimd", MOE, lambda h: h.dma_scatter_add(
        inv1_hbm, slot1s.ap().rearrange("p (v c) -> p v c", c=64), dr_w[:],
        NS, NS, 64), amt=16)
    xnv = xnT.ap().rearrange("p (kc n) -> p kc n", n=TPC)
    for kc in range(DMC):
        par = kc % 2
        if kc >= 2:
            wait("gpsimd", XOF[par])
        dma("gpsimd", XOT[par], xob[par][:], xT_own[128 * kc:128 * (kc + 1), :])
        wait("vector", XOT[par])
        wait("vector", SCF)
        wait("vector", VC)
        if kc >= 1:
            wait("vector", XOF[1 - par])
        finc("vector", VC, lambda h, par=par: h.tensor_tensor(
            ttile[:, 0:DM] if False else dtile[:], xob[par][:, 0:DM], mubc[:, 0:DM], ALU.subtract))
        finc("vector", VC, lambda h, par=par: h.tensor_tensor(
            otile[:], xob[par][:, DM:TPC], mubc[:, DM:TPC], ALU.subtract))
        wait("vector", VC)
        finc("vector", VC, lambda h, kc=kc: h.tensor_tensor(
            xnv[:, kc, 0:DM], dtile[:], isbc[:, 0:DM], ALU.mult))
        finc("vector", XOF[par], lambda h, kc=kc: h.tensor_tensor(
            xnv[:, kc, DM:TPC], otile[:], isbc[:, DM:TPC], ALU.mult))
    XNDONE = (XOF[0].n, XOF[1].n)
    wait("vector", STX)
    for vv in range(TV):
        par = vv % 2
        if vv >= 2:
            wait("sync", XLF[par])
        dma("sync", XL[par], xtile[par][:], x_own[128 * vv:128 * (vv + 1), :])
        wait("vector", XL[par])
        if vv >= 1:
            wait("vector", XBS)
            wait("vector", XLF[1 - par])
        wait("vector", VC)
        finc("vector", VC, lambda h, par=par, vv=vv: h.tensor_scalar(
            ttile[:], xtile[par][:], mu_pp[:, vv:vv + 1], None, ALU.subtract))
        wait("vector", VC)
        finc("vector", XLF[par], lambda h, vv=vv: h.tensor_scalar(
            btile[:], ttile[:], is_pp[:, vv:vv + 1], None, ALU.mult))
        wait("scalar", XLF[par])
        dma("scalar", XBS, xb_hbm[128 * vv:128 * (vv + 1), :], btile[:])
    XBS_ALL = XBS.n

    wait("gpsimd", MOE)
    dma("gpsimd", MOE, tbl1.ap().rearrange("p (v c) -> p v c", c=64),
        inv1_hbm[0:NR, :].rearrange("(v p) c -> p v c", p=128))
    TBL1_ALL = MOE.n
    if "inv1" in dbg:
        wait("scalar", MOE, TBL1_ALL)
        dma("scalar", SCD, dbg["inv1"], inv1_hbm)
    if "drw" in dbg:
        wait("scalar", GPD)
        dma("scalar", SCD, dbg["drw"], dr_w[:])
    wait("vector", MOE, TBL1_ALL)
    t1v = tbl1.ap().rearrange("p (v c) -> p v c", c=64)
    v_op(lambda h: h.tensor_copy(idx_f.ap().rearrange("p (v o) -> p v o", o=1), t1v[:, :, 0:1]))
    v_op(lambda h: h.tensor_scalar(idx_f[:], idx_f[:], 1.0, None, ALU.subtract))
    v_op(lambda h: h.tensor_copy(k_f.ap().rearrange("p (v o) -> p v o", o=1), t1v[:, :, 1:2]))
    v_op(lambda h: h.tensor_scalar(mskE[:], idx_f[:], 0.0, None, ALU.is_lt))
    v_op(lambda h: h.tensor_tensor(gidx_f[:], idx_f[:], mskE[:], ALU.add))
    v_op(lambda h: h.tensor_scalar(scrE[:], mskE[:], -1.0, 1.0, ALU.mult, ALU.add))
    v_op(lambda h: h.tensor_tensor(s2_f[:], k_f[:], scrE[:], ALU.mult))
    v_op(lambda h: h.tensor_scalar(s2_f[:], s2_f[:], float(TPC), None, ALU.mult))
    v_op(lambda h: h.tensor_tensor(s2_f[:], s2_f[:], gidx_f[:], ALU.add))
    v_op(lambda h: h.tensor_scalar(scrE[:], mskE[:], float(TR2), None, ALU.mult))
    v_op(lambda h: h.tensor_tensor(s2_f[:], s2_f[:], scrE[:], ALU.add))
    v_op(lambda h: h.tensor_copy(gidx_i[:], gidx_f[:]))
    v_op(lambda h: h.tensor_copy(s2_i[:], s2_f[:]))
    dbg_dump("idxf", idx_f[:])
    dbg_dump("kf", k_f[:])

    def wrap8(t_i, t_w, nv, col0=0):
        wv = t_w.ap().rearrange("p (v hh) -> p v hh", hh=8)
        for hh in range(8):
            dma("sync", STX, wv[0:16, 0:nv, hh], t_i[16 * hh:16 * hh + 16, col0:col0 + nv],
                slow=True)
        wait("sync", STX)
        for rep in range(1, 8):
            dma("sync", STX, t_w[16 * rep:16 * (rep + 1), 0:nv * 8], t_w[0:16, 0:nv * 8])
        wait("sync", STX)

    wait("sync", VC)
    dma("sync", STX, rowcs.ap()[:, :], rowc_in)
    wrap8(gidx_i, gidx_w, NRV)
    wrap8(s2_i, s2_w, NRV)
    WRAP_ALL = STX.n
    wait("gpsimd", STX, WRAP_ALL)
    finc("gpsimd", MOE, lambda h: h.dma_scatter_add(
        tb2_hbm, rowcs.ap().rearrange("p (v c) -> p v c", c=64), s2_w[:],
        NR, NR, 64), amt=16)
    wait("gpsimd", MOE)
    dma("gpsimd", MOE, tbl2.ap().rearrange("p (v c) -> p v c", c=64),
        tb2_hbm[0:NS, :].rearrange("(v p) c -> p v c", p=128))
    TBL2_ALL = MOE.n
    winv = winm_sb.ap().rearrange("p (kc m) -> p kc m", m=DM)
    emv = embed.ap().rearrange("p (mc n) -> p mc n", n=TPC)
    wait("tensor", LD)
    wait("tensor", XOF[0], XNDONE[0])
    wait("tensor", XOF[1], XNDONE[1])
    for mc in range(DMC):
        for n4 in range(4):
            bank = ("A", (mc * 4 + n4) % 4)
            bwait("tensor", bank)
            bs = 512 * bank[1]
            for kc in range(DMC):
                fn = lambda h, mc=mc, n4=n4, kc=kc, bs=bs: h.matmul(
                    ps_A[:, bs:bs + 512], winv[:, kc, 128 * mc:128 * (mc + 1)],
                    xnv[:, kc, 512 * n4:512 * (n4 + 1)],
                    start=(kc == 0), stop=(kc == DMC - 1))
                finc("tensor", TC, fn) if kc == DMC - 1 else rawi("tensor", fn)
            wait("vector", TC)
            wait("vector", VC)
            finc("vector", VC, lambda h, mc=mc, n4=n4, bs=bs: h.tensor_scalar(
                emv[:, mc, 512 * n4:512 * (n4 + 1)], ps_A[:, bs:bs + 512],
                bic[:, mc:mc + 1], None, ALU.add))
            bset(bank, VC)
    wouv = wout_sb.ap().rearrange("p (kc m) -> p kc m", m=DM)
    wait("tensor", LD)
    wait("tensor", VC)
    wait("vector", XBS, XBS_ALL)
    for vv in range(TV):
        if vv >= 1:
            wait("vector", SST)
        for n2 in range(2):
            bank = ("B", (vv * 2 + n2) % 2)
            bwait("tensor", bank)
            bs = 512 * bank[1]
            for kc in range(DMC):
                rawi("tensor", lambda h, vv=vv, n2=n2, kc=kc, bs=bs: h.matmul(
                    ps_B[:, bs:bs + 512], emv[:, kc, 128 * vv:128 * (vv + 1)],
                    wouv[:, kc, 512 * n2:512 * (n2 + 1)], start=(kc == 0), stop=False))
            finc("tensor", TC, lambda h, vv=vv, n2=n2, bs=bs: h.matmul(
                ps_B[:, bs:bs + 512], stT[:, 128 * vv:128 * (vv + 1)],
                wout_tl[:, 512 * n2:512 * (n2 + 1)], start=False, stop=True))
            wait("vector", TC)
            wait("vector", VC)
            finc("vector", VC, lambda h, n2=n2, bs=bs: h.tensor_tensor(
                btile[:, 512 * n2:512 * (n2 + 1)], ps_B[:, bs:bs + 512],
                bb_out[:, 512 * n2:512 * (n2 + 1)], ALU.add))
            bset(bank, VC)
        wait("scalar", VC)
        dma("scalar", SST, s_hbm[128 * vv:128 * (vv + 1), :], btile[:])
    SST_ALL = SST.n
    wait("vector", MOE, TBL2_ALL)
    t2v = tbl2.ap().rearrange("p (v c) -> p v c", c=64)
    v_op(lambda h: h.tensor_copy(rid_f.ap().rearrange("p (v o) -> p v o", o=1), t2v[:, :, 0:1]))
    v_op(lambda h: h.tensor_scalar(rid_f[:], rid_f[:], 1.0, None, ALU.subtract))
    v_op(lambda h: h.tensor_scalar(scrE[:, 0:NSV], rid_f[:], 0.0, None, ALU.is_lt))
    v_op(lambda h: h.tensor_scalar(scrE[:, 0:NSV], scrE[:, 0:NSV], float(EOTR + 1), None, ALU.mult))
    v_op(lambda h: h.tensor_tensor(rid_f[:], rid_f[:], scrE[:, 0:NSV], ALU.add))
    v_op(lambda h: h.tensor_copy(rid_i[:], rid_f[:]))
    dbg_dump("rid", rid_f[:])
    wait("sync", VC)
    wrap8(rid_i, r0_w, TV, col0=0)
    wrap8(rid_i, r1_w, TV, col0=TV)
    RWRAP_ALL = STX.n

    ctxB2b.close()
    ctxB5t.close()
    ctxB2a.close()
    ctxB3a = ExitStack()
    NWB = 5
    wbuf = [ctxB3a.enter_context(nc.sbuf_tensor(f"wbuf{i}", [128, 11264], BF16))
            for i in range(NWB - 1)]
    _own_base = nc.lookup_mloc(ownrowA).addr
    assert nc.lookup_mloc(ownrowB).addr == _own_base + 8192
    assert nc.lookup_mloc(dr_i16).addr == _own_base + 16384
    assert nc.lookup_mloc(xtile[0]).addr == _own_base + 20480
    wbuf.append(nc.alloc_sbuf_tensor_at("wbuf4", [128, 11264], BF16,
                                        offset=_own_base))
    ctxB3b = ExitStack()
    a_sht = ctxB3b.enter_context(nc.sbuf_tensor("a_sht", [128, FFC * 512], BF16))
    wload_state = {"i": 0}
    wfr_expect = [0] * NWB
    wld_mark = {}

    def wload():
        i = wload_state["i"]
        nm, src, kind = wstream[i]
        par = i % NWB
        if i == 0:
            wait("gpsimd", PCW, PRE_ALL)
        if i >= NWB:
            wait("gpsimd", WFR[par], wfr_expect[par])
        if i < NPRE:
            dma("gpsimd", WLD[par], wbuf[par].ap(), wbf_hbm[i])
        elif kind == "dm":
            view = wbuf[par].ap().rearrange("p (c m) -> p c m", m=DM)[:, 0:FFC, :]
            dma("gpsimd", WLD[par], view, src.rearrange("(c p) m -> p c m", p=128))
        else:
            view = wbuf[par].ap().rearrange("p (c m) -> p c m", m=FF)[:, 0:DMC, :]
            dma("gpsimd", WLD[par], view, src.rearrange("(c p) m -> p c m", p=128))
        wld_mark[i] = (par, WLD[par].n)
        wload_state["i"] += 1
        return i

    def wrelease(i):
        par = i % NWB
        wfr_expect[par] = WFR[par].n

    wait("gpsimd", TC)
    wait("gpsimd", STX, RWRAP_ALL)
    wait("gpsimd", VC)
    for _ in range(NWB):
        wload()
    ashv = a_sht.ap().rearrange("p (fc n) -> p fc n", n=512)

    def wview(i, kind):
        par = i % NWB
        if kind == "dm":
            return wbuf[par].ap().rearrange("p (c m) -> p c m", m=DM)
        return wbuf[par].ap().rearrange("p (c m) -> p c m", m=FF)

    ig, iu, idn = 0, 1, 2
    gv = wview(ig, "ff")
    uv = wview(iu, "ff")
    dv = wview(idn, "dm")
    wait("tensor", WLD[wld_mark[ig][0]], wld_mark[ig][1])
    wait("tensor", WLD[wld_mark[iu][0]], wld_mark[iu][1])
    for t4 in range(4):
        if t4 >= 1:
            wait("vector", TC)
        for fc in range(FFC):
            gb = ("A", fc % 2)
            ub = ("A", 2 + fc % 2)
            bwait("tensor", gb)
            bwait("tensor", ub)
            gbs, ubs = 512 * gb[1], 512 * ub[1]
            for kc in range(DMC):
                fn = lambda h, fc=fc, t4=t4, kc=kc, gbs=gbs: h.matmul(
                    ps_A[:, gbs:gbs + 512], gv[:, kc, 128 * fc:128 * (fc + 1)],
                    xnv[:, kc, 512 * t4:512 * (t4 + 1)],
                    start=(kc == 0), stop=(kc == DMC - 1))
                if t4 == 3 and fc == FFC - 1 and kc == DMC - 1:
                    finc("tensor", WFR[ig], fn)
                else:
                    rawi("tensor", fn)
            for kc in range(DMC):
                fn = lambda h, fc=fc, t4=t4, kc=kc, ubs=ubs: h.matmul(
                    ps_A[:, ubs:ubs + 512], uv[:, kc, 128 * fc:128 * (fc + 1)],
                    xnv[:, kc, 512 * t4:512 * (t4 + 1)],
                    start=(kc == 0), stop=(kc == DMC - 1))
                if kc == DMC - 1:
                    if t4 == 3 and fc == FFC - 1:
                        finc("tensor", WFR[iu], fn)
                    else:
                        finc("tensor", TC, fn)
                else:
                    rawi("tensor", fn)
            if t4 == 3 and fc == FFC - 1:
                wait("scalar", WFR[iu])
            else:
                wait("scalar", TC)
            wait("scalar", VC)
            s_op(lambda h, gbs=gbs: h.activation(gtl[:], ps_A[:, gbs:gbs + 512], ACT.Silu))
            bset(gb, SCF)
            wait("vector", SCF)
            wait("vector", VC)
            finc("vector", VC, lambda h, fc=fc, ubs=ubs: h.tensor_tensor(
                ashv[:, fc, :], gtl[:], ps_A[:, ubs:ubs + 512], ALU.mult))
            bset(ub, VC)
        if t4 == 0:
            wait("tensor", WLD[wld_mark[idn][0]], wld_mark[idn][1])
            wait("vector", SST, SST_ALL)
        wait("tensor", VC)
        for vsub in range(4):
            vv = 4 * t4 + vsub
            if vv >= 1:
                wait("vector", SHST)
            for n2 in range(2):
                bank = ("B", n2)
                bwait("tensor", bank)
                bs = 512 * bank[1]
                for fc in range(FFC):
                    fn = lambda h, vsub=vsub, n2=n2, fc=fc, bs=bs: h.matmul(
                        ps_B[:, bs:bs + 512], ashv[:, fc, 128 * vsub:128 * (vsub + 1)],
                        dv[:, fc, 512 * n2:512 * (n2 + 1)],
                        start=(fc == 0), stop=(fc == FFC - 1))
                    if fc == FFC - 1:
                        if vv == TV - 1 and n2 == 1:
                            finc("tensor", WFR[idn], fn)
                        else:
                            finc("tensor", TC, fn)
                    else:
                        rawi("tensor", fn)
                if vv == TV - 1 and n2 == 1:
                    wait("vector", WFR[idn])
                else:
                    wait("vector", TC)
                wait("vector", VC)
                finc("vector", VC, lambda h, n2=n2, bs=bs: h.tensor_copy(
                    btile[:, 512 * n2:512 * (n2 + 1)], ps_B[:, bs:bs + 512]))
                bset(bank, VC)
            wait("scalar", VC)
            dma("scalar", SHST, sh_hbm[128 * vv:128 * (vv + 1), :], btile[:])
    SHST_ALL = SHST.n
    wrelease(ig)
    wrelease(iu)
    wrelease(idn)
    SHG_DONE = WFR[ig].n
    SHU_DONE = WFR[iu].n
    for _ in range(6 - NWB):
        wload()

    ctxB3b.close()
    xnT_base = nc.lookup_mloc(xnT).addr
    eo_t = nc.alloc_sbuf_tensor_at("eo_t", [128, RLC * DM], BF16, offset=xnT_base)
    xeT = [nc.alloc_sbuf_tensor_at(f"xeT{i}", [128, DMC * RL], BF16,
                                   offset=xnT_base + RLC * DM * 2 + i * DMC * RL * 2)
           for i in range(2)]
    a_e = nc.alloc_sbuf_tensor_at("a_e", [128, FFC * RL], BF16,
                                  offset=xnT_base + RLC * DM * 2 + 2 * DMC * RL * 2)
    assert RLC * DM * 2 + 2 * DMC * RL * 2 + FFC * RL * 2 <= DMC * TPC * 2
    aev = a_e.ap().rearrange("p (fc n) -> p fc n", n=RL)

    gxf_mark = [0, 0]
    for e_ in range(E):
        par = e_ % 2
        if e_ == 0:
            wait("gpsimd", WFR[0], SHG_DONE)
            wait("gpsimd", WFR[1], SHU_DONE)
        if e_ >= 2:
            wait("gpsimd", GXF[par], gxf_mark[par])
        wait("gpsimd", XBS, XBS_ALL)
        wait("gpsimd", STX, WRAP_ALL)
        finc("gpsimd", GX[par], lambda h, par=par, e_=e_: h.dma_gather(
            xeT[par].ap().rearrange("p (c n) -> p c n", n=RL), xb_hbm,
            gidx_w[:, (RL // 16) * e_:(RL // 16) * (e_ + 1)], RL, RL, DM,
            transpose=True), amt=16)
        if e_ == 0 and "xe0" in dbg:
            wait("scalar", GX[par])
            dma("scalar", SCD, dbg["xe0"], xeT[par][:])
        if e_ == 0 and "xb" in dbg:
            wait("scalar", XBS, XBS_ALL)
            dma("scalar", SCD, dbg["xb"], xb_hbm)
        iwg, iwu, iwd = 3 * e_ + 3, 3 * e_ + 4, 3 * e_ + 5
        gv_ = wview(iwg, "ff")
        uv_ = wview(iwu, "ff")
        xev = xeT[par].ap().rearrange("p (c n) -> p c n", n=RL)
        wait("tensor", GX[par])
        wait("tensor", WLD[wld_mark[iwg][0]], wld_mark[iwg][1])
        wait("tensor", WLD[wld_mark[iwu][0]], wld_mark[iwu][1])
        for fc in range(FFC):
            gb = ("A", fc % 2)
            ub = ("A", 2 + fc % 2)
            bwait("tensor", gb)
            bwait("tensor", ub)
            gbs, ubs = 512 * gb[1], 512 * ub[1]
            for kc in range(DMC):
                fn = lambda h, fc=fc, kc=kc, gbs=gbs, gv_=gv_, xev=xev: h.matmul(
                    ps_A[:, gbs:gbs + RL], gv_[:, kc, 128 * fc:128 * (fc + 1)],
                    xev[:, kc, :], start=(kc == 0), stop=(kc == DMC - 1))
                if fc == FFC - 1 and kc == DMC - 1:
                    finc("tensor", WFR[iwg % NWB], fn)
                else:
                    rawi("tensor", fn)
            for kc in range(DMC):
                fn = lambda h, fc=fc, kc=kc, ubs=ubs, uv_=uv_, xev=xev: h.matmul(
                    ps_A[:, ubs:ubs + RL], uv_[:, kc, 128 * fc:128 * (fc + 1)],
                    xev[:, kc, :], start=(kc == 0), stop=(kc == DMC - 1))
                if kc == DMC - 1:
                    if fc == FFC - 1:
                        finc("tensor", WFR[iwu % NWB], fn)
                    else:
                        finc("tensor", TC, fn)
                else:
                    rawi("tensor", fn)
            if fc == FFC - 1:
                wait("scalar", WFR[iwu % NWB])
            else:
                wait("scalar", TC)
            wait("scalar", VC)
            s_op(lambda h, gbs=gbs: h.activation(gtl[:, 0:RL], ps_A[:, gbs:gbs + RL], ACT.Silu))
            bset(gb, SCF)
            wait("vector", SCF)
            wait("vector", VC)
            finc("vector", VC, lambda h, fc=fc, ubs=ubs: h.tensor_tensor(
                aev[:, fc, :], gtl[:, 0:RL], ps_A[:, ubs:ubs + RL], ALU.mult))
            bset(ub, VC)
        if e_ == 0 and "ae0" in dbg:
            wait("scalar", VC)
            dma("scalar", SCD, dbg["ae0"], a_e[:])
        wrelease(iwg)
        wrelease(iwu)
        if e_ + 1 < E:
            wload()
            wload()
            wload()
        dv_ = wview(iwd, "dm")
        wait("tensor", WLD[wld_mark[iwd][0]], wld_mark[iwd][1])
        wait("tensor", VC)
        if e_ >= 1:
            wait("scalar", EOS)
        for rc in range(RLC):
            for n2 in range(2):
                bank = ("B", (rc * 2 + n2) % 2)
                bwait("tensor", bank)
                bs = 512 * bank[1]
                for fc in range(FFC):
                    fn = lambda h, rc=rc, n2=n2, fc=fc, bs=bs, dv_=dv_: h.matmul(
                        ps_B[:, bs:bs + 512], aev[:, fc, 128 * rc:128 * (rc + 1)],
                        dv_[:, fc, 512 * n2:512 * (n2 + 1)],
                        start=(fc == 0), stop=(fc == FFC - 1))
                    if fc == 0 and rc == 0 and n2 == 0:
                        finc("tensor", GXF[par], fn)
                    elif fc == FFC - 1:
                        if rc == RLC - 1 and n2 == 1:
                            finc("tensor", WFR[iwd % NWB], fn)
                        else:
                            finc("tensor", TC, fn)
                    else:
                        rawi("tensor", fn)
                if rc == RLC - 1 and n2 == 1:
                    wait("scalar", WFR[iwd % NWB])
                else:
                    wait("scalar", TC)
                s_op(lambda h, rc=rc, n2=n2, bs=bs: h.copy(
                    eo_t.ap().rearrange("p (r m) -> p r m", m=DM)[:, rc, 512 * n2:512 * (n2 + 1)],
                    ps_B[:, bs:bs + 512]))
                bset(bank, SCF)
        wait("scalar", SCF)
        dma("scalar", EOS, eo_hbm[RL * e_:RL * (e_ + 1), :].rearrange("(r p) m -> p r m", p=128),
            eo_t.ap().rearrange("p (r m) -> p r m", m=DM))
        wrelease(iwd)
        gxf_mark[par] = GXF[par].n
    EOS_ALL = EOS.n

    ctxB3a.close()
    ctxB1.close()
    ctxB0.close()
    ctxB4 = ExitStack()
    enB4 = ctxB4.enter_context
    g0b = enB4(nc.sbuf_tensor("g0b", [128, TV * DM], BF16))
    g1b = enB4(nc.sbuf_tensor("g1b", [128, TV * DM], BF16))
    sfull = enB4(nc.sbuf_tensor("sfull", [128, TV * DM], BF16))
    shfull = enB4(nc.sbuf_tensor("shfull", [128, TV * DM], BF16))
    ot2 = enB4(nc.sbuf_tensor("ot2", [128, DM], F32))
    xnb = [enB4(nc.sbuf_tensor(f"xnb{i}", [128, DM], BF16)) for i in range(2)]
    wait("gpsimd", EOS, EOS_ALL)
    wait("gpsimd", SCD, INIT_ALL)
    wait("gpsimd", STX, RWRAP_ALL)
    wait("gpsimd", SST, SST_ALL)
    wait("gpsimd", SHST, SHST_ALL)
    for (buf, wtab) in ((g0b, r0_w), (g1b, r1_w)):
        off = 0
        for n_ in (768, 768, 512):
            finc("gpsimd", CG, lambda h, buf=buf, wtab=wtab, off=off, n_=n_: h.dma_gather(
                buf.ap().rearrange("p (v m) -> p v m", m=DM)[:, off // 128:(off + n_) // 128, :],
                eo_hbm, wtab[:, off // 16:(off + n_) // 16], n_, n_, DM,
                transpose=False), amt=16)
            off += n_
    CG_ALL = CG.n

    wait("sync", SST, SST_ALL)
    wait("sync", SHST, SHST_ALL)
    wait("sync", EOS, EOS_ALL)
    for i_ in range(NWB):
        wait("sync", WFR[i_])
    wait("sync", TC)
    wait("sync", VC)
    wait("sync", SCF)
    dma("sync", PRE, sfull.ap().rearrange("p (v m) -> p v m", m=DM),
        s_hbm.rearrange("(v p) m -> p v m", p=128))
    dma("sync", PRE, shfull.ap().rearrange("p (v m) -> p v m", m=DM),
        sh_hbm.rearrange("(v p) m -> p v m", p=128))
    if "s" in dbg:
        wait("scalar", SST, SST_ALL)
        dma("scalar", SCD, dbg["s"], s_hbm)
    if "sh" in dbg:
        wait("scalar", SHST, SHST_ALL)
        dma("scalar", SCD, dbg["sh"], sh_hbm)
    if "eo" in dbg:
        wait("scalar", EOS, EOS_ALL)
        dma("scalar", SCD, dbg["eo"], eo_hbm)
    if "g0" in dbg:
        wait("scalar", CG, CG_ALL)
        dma("scalar", SCD, dbg["g0"], g0b[:])
        dma("scalar", SCD, dbg["g1"], g1b[:])
    sfv = sfull.ap().rearrange("p (v m) -> p v m", m=DM)
    shv = shfull.ap().rearrange("p (v m) -> p v m", m=DM)
    g0v = g0b.ap().rearrange("p (v m) -> p v m", m=DM)
    g1v = g1b.ap().rearrange("p (v m) -> p v m", m=DM)
    otl = [otile, ot2]
    OUTP = [OUTS, OUT2]
    outs_mark = [0, 0]
    wait("sync", XBS, XBS_ALL)
    for vv in range(TV):
        par = vv % 2
        wait("sync", XLF[par])
        dma("sync", XL[par], xtile[par][:], x_own[128 * vv:128 * (vv + 1), :])
        wait("sync", XNF[par])
        dma("sync", XNL[par], xnb[par][:], xb_hbm[128 * vv:128 * (vv + 1), :])
        wait("vector", XL[par])
        wait("vector", XNL[par])
        wait("vector", CG, CG_ALL)
        wait("vector", PRE)
        wait("vector", VC)
        finc("vector", VC, lambda h, vv=vv: h.tensor_scalar(
            dtile[:], g0v[:, vv, :], w0_pp[:, vv:vv + 1], None, ALU.mult))
        v_op(lambda h, vv=vv: h.tensor_scalar(ttile[:], g1v[:, vv, :], w1_pp[:, vv:vv + 1], None, ALU.mult))
        v_op(lambda h: h.tensor_tensor(dtile[:], dtile[:], ttile[:], ALU.add))
        wait("vector", VC)
        finc("vector", XNF[par], lambda h, par=par: h.tensor_tensor(
            dtile[:], dtile[:], xnb[par][:], ALU.add))
        wait("vector", XNF[par])
        v_op(lambda h, vv=vv: h.tensor_tensor(dtile[:], dtile[:], shv[:, vv, :], ALU.add))
        if "gs" in dbg:
            wait("scalar", VC)
            dma("scalar", SCD, dbg["gs"].rearrange("p (v m) -> p v m", m=DM)[:, vv, :], dtile[:])
        wait("scalar", VC)
        wait("scalar", OUTP[par], outs_mark[par])
        s_op(lambda h, par=par: h.activation(otl[par][:], dtile[:], ACT.Silu))
        wait("vector", SCF)
        v_op(lambda h, par=par, vv=vv: h.tensor_tensor(otl[par][:], otl[par][:], sfv[:, vv, :], ALU.mult))
        wait("vector", VC)
        finc("vector", XLF[par], lambda h, par=par: h.tensor_tensor(
            otl[par][:], otl[par][:], xtile[par][:], ALU.add))
        wait("scalar", XLF[par])
        dma("scalar", OUTP[par], out_hbm[128 * vv:128 * (vv + 1), :], otl[par][:])
        outs_mark[par] = OUTP[par].n
    wait("scalar", OUTS)
    wait("scalar", OUT2)

    block = en(nc.Block())
    P.replay(block)
    ctxB4.close()
    ctxT.close()
    ctx.close()
    return nc


_NC_CACHE = {}


def _get_nc(debug=()):
    key = tuple(debug)
    if key not in _NC_CACHE:
        nc = build_nc(debug=key)
        lower_extended_insts(nc)
        _NC_CACHE[key] = nc
    return _NC_CACHE[key]


def kernel(L0_attn_x, L0_attn_W_in, L0_attn_b_in, L0_attn_W_g, L0_attn_b_g,
           L0_attn_W_out, L0_attn_b_out, L0_ffn_router_w, L0_ffn_router_b,
           L0_ffn_w_gate, L0_ffn_w_up, L0_ffn_w_down, L0_ffn_sw_gate,
           L0_ffn_sw_up, L0_ffn_sw_down):
    x = np.ascontiguousarray(np.asarray(L0_attn_x, dtype=np.float32).reshape(NTOK, DM))
    xTf = np.ascontiguousarray(x.T)
    W_in = np.asarray(L0_attn_W_in, np.float32)
    consts = _host_constants()
    wsc = np.ascontiguousarray(np.concatenate(
        [np.asarray(L0_ffn_router_w, np.float32),
         W_in[:, DM:DM + 8],
         np.full((DM, 1), 1.0 / DM, np.float32)], axis=1))
    shared_map = {
        "xT": xTf,
        "w_small_c": wsc,
        "w_inT": np.ascontiguousarray(W_in.T),
        "w_g_in": np.ascontiguousarray(np.asarray(L0_attn_W_g, np.float32)),
        "b_in_col": np.ascontiguousarray(np.asarray(L0_attn_b_in, np.float32)[:, None]),
        "b_g8": np.ascontiguousarray(np.asarray(L0_attn_b_g, np.float32)[:, None]),
        "b_si32": np.ascontiguousarray(np.tile(np.asarray(L0_attn_b_in, np.float32)[DM:DM + 8], 4)[:, None]),
        "rb128": np.ascontiguousarray(np.tile(np.asarray(L0_ffn_router_b, np.float32), 8)[:, None]),
        "w_in_m": np.ascontiguousarray(W_in[:, 0:DM]),
        "w_out_in": np.ascontiguousarray(np.asarray(L0_attn_W_out, np.float32)),
        "b_out_row": np.ascontiguousarray(np.asarray(L0_attn_b_out, np.float32)[None, :]),
        "sw_gate": np.ascontiguousarray(np.asarray(L0_ffn_sw_gate, np.float32)),
        "sw_up": np.ascontiguousarray(np.asarray(L0_ffn_sw_up, np.float32)),
        "sw_down": np.ascontiguousarray(np.asarray(L0_ffn_sw_down, np.float32)),
        "w_gate": np.ascontiguousarray(np.asarray(L0_ffn_w_gate, np.float32)),
        "w_up": np.ascontiguousarray(np.asarray(L0_ffn_w_up, np.float32)),
        "w_down": np.ascontiguousarray(np.asarray(L0_ffn_w_down, np.float32)),
        "ones128": consts["ones128"],
        "E16dup": consts["E16dup"],
        "E8dup2": consts["E8dup2"],
        "P8": consts["P8"],
        "T16": consts["T16"],
        "BD16": consts["BD16"],
        "iotaE": consts["iotaE"],
        "slot1c": consts["slot1c"],
        "rowc": consts["rowc"],
    }
    in_maps = []
    for c in range(NCORE):
        sown, sst = _per_core_consts(c)
        m = dict(shared_map)
        m["xT_own"] = np.ascontiguousarray(xTf[:, TPC * c:TPC * (c + 1)])
        m["x_own"] = np.ascontiguousarray(x[TPC * c:TPC * (c + 1), :])
        m["selOwn"] = sown
        m["selST"] = sst
        in_maps.append(m)
    nc = _get_nc(DEBUG)
    res = run_bass_kernel_spmd(nc, in_maps, core_ids=list(range(NCORE)))
    out = np.concatenate([res.results[c]["out_own"] for c in range(NCORE)], axis=0)
    if DEBUG:
        kernel._dbg = [{k: v for k, v in r.items() if k.startswith("dbg_")}
                       for r in res.results]
    return out.reshape(4, 4096, DM)


# version 10
# speedup vs baseline: 1.0356x; 1.0044x over previous
import numpy as np
from contextlib import ExitStack

import concourse.bass as bass
import concourse.mybir as mybir
from concourse.bass_utils import run_bass_kernel_spmd
from concourse.library_config import mlp as _MLP_LIB
from concourse.library_overlay import lower_extended_insts

F32 = mybir.dt.float32
BF16 = mybir.dt.bfloat16
I16 = mybir.dt.int16
ALU = mybir.AluOpType
ACT = mybir.ActivationFunctionType

NCORE = 8
DM = 1024
DMC = 8
NTOK = 16384
TPC = 2048
TV = 16
NG = 8
E = 16
FF = 1408
FFC = 11
RL = 384
RLC = 3
NR = E * RL
NRV = NR // 128
NS = TPC * 2
NSV = NS // 128
CAP = 2560
TR1 = NR
TR2 = NS
EOTR = NR
EPS_LN = 1e-5

DEBUG = ()


def _host_constants():
    c = {}
    c["ones128"] = np.ones((1, 128), np.float32)
    e16 = np.zeros((8, 128), np.float32)
    for g in range(8):
        e16[g, 16 * g:16 * g + 16] = 1.0
    c["E16dup"] = e16
    e8 = np.zeros((8, 128), np.float32)
    for g in range(8):
        e8[g, 8 * g:8 * g + 8] = 1.0
        e8[g, 64 + 8 * g:64 + 8 * g + 8] = 1.0
    c["E8dup2"] = e8
    p8 = np.zeros((128, 128), np.float32)
    for g in range(8):
        for gp in range(g):
            for ee in range(16):
                p8[16 * gp + ee, 16 * g + ee] = 1.0
    c["P8"] = p8
    t16 = np.zeros((128, 128), np.float32)
    for g in range(8):
        for e1 in range(16):
            for e2 in range(e1, 16):
                t16[16 * g + e1, 16 * g + e2] = 1.0
    c["T16"] = t16
    bd = np.zeros((128, 128), np.float32)
    for g in range(8):
        bd[16 * g:16 * g + 16, 16 * g:16 * g + 16] = 1.0
    c["BD16"] = bd
    c["iotaE"] = (np.arange(128) % 16).astype(np.float32)[:, None]
    s1 = np.zeros((128, NSV, 64), np.float32)
    for m in range(NS):
        k, n = m // TPC, m % TPC
        i = (m % (NS // 16)) * 16 + m // (NS // 16)
        s1[i % 128, i // 128, 0] = n + 1
        s1[i % 128, i // 128, 1] = k
    c["slot1c"] = np.ascontiguousarray(s1.reshape(128, NSV * 64))
    rc = np.zeros((128, NRV, 64), np.float32)
    for m in range(NR):
        rc[m % 128, m // 128, 0] = m + 1
    c["rowc"] = np.ascontiguousarray(rc.reshape(128, NRV * 64))
    return c


def _per_core_consts(core):
    b_i, h_i = core // 2, core % 2
    sown = np.zeros((128, 16), np.float32)
    for j in range(16):
        sown[16 * core + j, j] = 1.0
    sst = np.zeros((64, 8), np.float32)
    for ch in range(8):
        sst[32 * h_i + 8 * b_i + ch, ch] = 1.0
    return sown, sst


class Sem:
    def __init__(self, en, nc, name):
        self.h = en(nc.semaphore(name))
        self.n = 0


class Prog:
    ENGS = ("sync", "scalar", "vector", "tensor", "gpsimd")

    def __init__(self):
        self.streams = {e: [] for e in self.ENGS}

    def emit(self, eng, fn):
        self.streams[eng].append(fn)

    def replay(self, block):
        for eng in self.ENGS:
            fns = self.streams[eng]

            def body(h, fns=fns):
                for f in fns:
                    f(h)

            getattr(block, eng)(body)


def build_nc(debug=()):
    nc = bass.Bass("TRN2", target_bir_lowering=False)

    def din(name, shape, dt=F32):
        return nc.dram_tensor(name, shape, dt, kind="ExternalInput").ap()

    xT = din("xT", [DM, NTOK])
    xT_own = din("xT_own", [DM, TPC])
    x_own = din("x_own", [TPC, DM])
    w_small_c = din("w_small_c", [DM, 25])
    w_inT = din("w_inT", [DM + 8, DM])
    w_g_in = din("w_g_in", [DM + 8, 8])
    b_in_col = din("b_in_col", [DM + 8, 1])
    b_g8 = din("b_g8", [8, 1])
    b_si32 = din("b_si32", [32, 1])
    rb128 = din("rb128", [128, 1])
    w_in_m = din("w_in_m", [DM, DM])
    w_out_in = din("w_out_in", [DM + 8, DM])
    b_out_row = din("b_out_row", [1, DM])
    sw_gate = din("sw_gate", [DM, FF])
    sw_up = din("sw_up", [DM, FF])
    sw_down = din("sw_down", [FF, DM])
    w_gate = din("w_gate", [E, DM, FF])
    w_up = din("w_up", [E, DM, FF])
    w_down = din("w_down", [E, FF, DM])
    ones128_in = din("ones128", [1, 128])
    E16_in = din("E16dup", [8, 128])
    E8_in = din("E8dup2", [8, 128])
    P8_in = din("P8", [128, 128])
    T16_in = din("T16", [128, 128])
    BD16_in = din("BD16", [128, 128])
    iota_in = din("iotaE", [128, 1])
    slot1c_in = din("slot1c", [128, NSV * 64])
    rowc_in = din("rowc", [128, NRV * 64])
    selOwn_in = din("selOwn", [128, 16])
    selST_in = din("selST", [64, 8])

    out_hbm = nc.dram_tensor("out_own", [TPC, DM], F32, kind="ExternalOutput").ap()

    xb_hbm = nc.dram_tensor("xb_hbm", [TPC, DM], BF16, kind="Internal").ap()
    s_hbm = nc.dram_tensor("s_hbm", [TPC, DM], BF16, kind="Internal").ap()
    sh_hbm = nc.dram_tensor("sh_hbm", [TPC, DM], BF16, kind="Internal").ap()
    inv1_hbm = nc.dram_tensor("inv1_hbm", [NR + 1, 64], F32, kind="Internal").ap()
    tb2_hbm = nc.dram_tensor("tb2_hbm", [NS + 1, 64], F32, kind="Internal").ap()
    eo_hbm = nc.dram_tensor("eo_hbm", [NR + 1, DM], BF16, kind="Internal").ap()
    NPRE = 3
    wbf_hbm = nc.dram_tensor("wbf_hbm", [NPRE, 128, 11264], BF16, kind="Internal").ap()

    dbg_shapes = {"l": [128, TPC], "w1": [128, TPC], "w2": [128, TPC],
                  "dr1": [128, TPC], "dr2": [128, TPC], "st": [32, 4096],
                  "cumG": [128, TPC], "i1": [128, TPC], "i2": [128, TPC],
                  "stT": [8, TPC], "gsig": [32, 4096], "sib": [32, 4096],
                  "idxf": [128, NRV], "kf": [128, NRV], "rid": [128, NSV],
                  "moe": [128, TV * DM], "gs": [128, TV * DM],
                  "s": [TPC, DM], "sh": [TPC, DM], "g0": [128, TV * DM],
                  "g1": [128, TV * DM], "eo": [NR + 1, DM],
                  "inv1": [NR + 1, 64], "drw": [128, NS // 16],
                  "xe0": [128, DMC * RL], "xb": [TPC, DM], "ae0": [128, FFC * RL],
                  "wg0": [128, 11264], "wu0": [128, 11264]}
    dbg = {nm: nc.dram_tensor("dbg_" + nm, dbg_shapes[nm],
                              BF16 if nm in ("stT", "s", "sh", "g0", "g1", "eo", "xe0", "xb", "ae0", "wg0", "wu0") else
                              (I16 if nm == "drw" else F32),
                              kind="ExternalOutput").ap()
           for nm in debug}

    ctx = ExitStack()
    en = ctx.enter_context

    def SB(nm, shp, dt=F32):
        return en(nc.sbuf_tensor(nm, shp, dt))

    wsmall = SB("wsmall", [128, DMC * 33], BF16)
    ones_bf = SB("ones_bf", [128, 1], BF16)
    ones1f = SB("ones1f", [1, 128], F32)
    ones33 = SB("ones33", [65, 128], F32)
    e16d = SB("e16d", [8, 128], F32)
    e8d = SB("e8d", [8, 128], F32)
    p8m = SB("p8m", [128, 128], F32)
    t16m = SB("t16m", [128, 128], F32)
    bd16m = SB("bd16m", [128, 128], F32)
    iota_pp = SB("iota_pp", [128, 1], F32)
    selown = SB("selown", [128, 16], F32)
    selst = SB("selst", [64, 8], F32)
    rb_pp = SB("rb_pp", [128, 1], F32)
    bsi_pp = SB("bsi_pp", [32, 1], F32)
    bg8s = SB("bg8s", [8, 1], F32)
    bgeff = SB("bgeff", [8, 1], F32)
    bgeff32 = SB("bgeff32", [32, 1], F32)
    scol_pp = SB("scol_pp", [128, 1], F32)
    sg_pp = SB("sg_pp", [128, 1], F32)
    bic = SB("bic", [128, DMC + 1], F32)
    bicb = SB("bicb", [128, DMC + 1], BF16)
    bout_sb = SB("bout_sb", [1, DM], F32)
    zsrc = SB("zsrc", [128, 512], F32)
    zsrc_bf = SB("zsrc_bf", [1, DM], BF16)
    bb_out = SB("bb_out", [128, DM], BF16)
    scol33 = SB("scol33", [33, 1], F32)
    gpre_pp = SB("gpre_pp", [128, 1], F32)
    epsc = SB("epsc", [128, 1], F32)
    mu_pp = SB("mu_pp", [128, TV], F32)
    is_pp = SB("is_pp", [128, TV], F32)
    w0_pp = SB("w0_pp", [128, TV], F32)
    w1_pp = SB("w1_pp", [128, TV], F32)
    stT = SB("stT", [8, TPC], BF16)
    dr_w = SB("dr_w", [128, NS // 16], I16)
    gidx_w = SB("gidx_w", [128, NR // 16], I16)
    s2_w = SB("s2_w", [128, NR // 16], I16)
    r0_w = SB("r0_w", [128, TPC // 16], I16)
    r1_w = SB("r1_w", [128, TPC // 16], I16)
    ownrowA = SB("ownrowA", [65, TPC], F32)
    ownrowB = SB("ownrowB", [65, TPC], F32)
    dr_i16 = SB("dr_i16", [33, TPC], I16)

    ctxA = ExitStack()
    enA = ctxA.enter_context

    def SA(nm, shp, dt=F32):
        return enA(nc.sbuf_tensor(nm, shp, dt))

    q_l = SA("q_l", [128, TPC], F32)
    q_rz = SA("q_rz", [128, TPC], F32)
    q_o1 = SA("q_o1", [128, TPC], F32)
    q_o2 = SA("q_o2", [128, TPC], F32)
    Lst = SA("Lst", [128, TPC], F32)
    q_cg = SA("q_cg", [128, TPC], F32)
    GSst = SA("GSst", [128, TPC], F32)
    q_e = SA("q_e", [128, TPC], F32)
    q_w1 = SA("q_w1", [128, TPC], F32)
    q_w2 = SA("q_w2", [128, TPC], F32)
    q_dr1 = SA("q_dr1", [128, TPC], F32)
    q_dr2 = SA("q_dr2", [128, TPC], F32)
    q_ex = SA("q_ex", [128, TPC], F32)
    q_t1 = SA("q_t1", [128, TPC], F32)
    q_t2 = SA("q_t2", [128, TPC], F32)
    q_bc1 = SA("q_bc1", [128, TPC], F32)
    q_bc2 = SA("q_bc2", [128, TPC], F32)
    mu8 = SA("mu8", [8, TPC], F32)
    is8 = SA("is8", [8, TPC], F32)
    winT_tl_t = SA("winT_tl_t", [8, DM], BF16)
    wgm_st = SA("wgm_st", [128, DMC * 8], BF16)
    wgm_tl_t = SA("wgm_tl_t", [8, 8], BF16)
    q_i1 = q_l
    q_i2 = Lst
    raws = q_bc1
    addr = lambda t: nc.lookup_mloc(t).addr
    ss8 = nc.alloc_sbuf_tensor_at("ss8", [8, TPC], F32, offset=addr(q_ex))
    sc8a = nc.alloc_sbuf_tensor_at("sc8a", [8, TPC], F32, offset=addr(q_e))
    assert addr(q_t2) == addr(q_t1) + 8192
    winT_st = nc.alloc_sbuf_tensor_at("winT_st", [128, DMC * DM], BF16, offset=addr(q_t1))
    xtb = [nc.alloc_sbuf_tensor_at(f"xtb{i}", [128, TPC], BF16,
                                   offset=addr(q_t1) + 4096 * i) for i in range(2)]
    xsq = [nc.alloc_sbuf_tensor_at(f"xsq{i}", [128, TPC], BF16,
                                   offset=addr(q_t1) + 8192 + 4096 * i) for i in range(2)]
    assert addr(q_rz) == addr(q_l) + 8192 and addr(q_o2) == addr(q_o1) + 8192
    assert addr(q_cg) == addr(Lst) + 8192
    sic = nc.alloc_sbuf_tensor_at("sic", [32, 4096], F32, offset=addr(q_l))
    gsc = nc.alloc_sbuf_tensor_at("gsc", [32, 4096], F32, offset=addr(q_o1))
    gsig = nc.alloc_sbuf_tensor_at("gsig", [32, 4096], F32, offset=addr(Lst))
    st64 = nc.alloc_sbuf_tensor_at("st64", [64, TPC], F32, offset=addr(GSst))
    ps_A = en(nc.psum_tensor("ps_A", [128, TPC], F32))
    ps_B = en(nc.psum_tensor("ps_B", [128, TPC], F32))

    sems = []

    def sem(name):
        s = Sem(en, nc, name)
        sems.append(s)
        return s

    LD = sem("LD")
    LDS = sem("LDS")
    TC = sem("TC")
    VC = sem("VC")
    SCF = sem("SCF")
    STX = sem("STX")
    SCD = sem("SCD")
    GPD = sem("GPD")
    XTs = [sem("XT0"), sem("XT1")]
    XTFs = [sem("XTF0"), sem("XTF1")]
    XSQs = [sem("XSQ0"), sem("XSQ1")]
    WLD = [sem(f"WLD{i}") for i in range(5)]
    WFR = [sem(f"WFR{i}") for i in range(5)]
    GX = [sem("GX0"), sem("GX1")]
    GXF = [sem("GXF0"), sem("GXF1")]
    XL = [sem("XL0"), sem("XL1")]
    XLF = [sem("XLF0"), sem("XLF1")]
    XOT = [sem("XOT0"), sem("XOT1")]
    XOF = [sem("XOF0"), sem("XOF1")]
    XNL = [sem("XNL0"), sem("XNL1")]
    XNF = [sem("XNF0"), sem("XNF1")]
    OUT2 = sem("OUT2")
    PCW = sem("PCW")
    XBS = sem("XBS")
    SST = sem("SST")
    SHST = sem("SHST")
    EOS = sem("EOS")
    OUTS = sem("OUTS")
    MOE = sem("MOE")
    CG = sem("CG")
    PRE = sem("PRE")

    P = Prog()

    def emit(eng, fn):
        P.emit(eng, fn)

    def wait(eng, s, val=None):
        v = s.n if val is None else val
        if v > 0:
            emit(eng, lambda h: h.wait_ge(s.h, v))

    def dma(eng, s, out, in_, slow=False):
        s.n += 16
        emit(eng, lambda h: h.dma_start(out, in_, allow_slow_non_contiguous=slow)
             .then_inc(s.h, 16))

    def finc(eng, s, fn, amt=1):
        s.n += amt
        emit(eng, lambda h: fn(h).then_inc(s.h, amt))

    def rawi(eng, fn):
        emit(eng, lambda h: fn(h))

    def v_op(fn):
        wait("vector", VC)
        finc("vector", VC, fn)

    def s_op(fn):
        wait("scalar", SCF)
        finc("scalar", SCF, fn)

    def dbg_dump(nm, src):
        if nm in dbg:
            wait("scalar", VC)
            wait("scalar", SCF)
            dma("scalar", SCD, dbg[nm], src)

    bank_free = {}

    def bwait(eng, key):
        if key in bank_free:
            s, cnt = bank_free[key]
            wait(eng, s, cnt)

    def bset(key, s):
        bank_free[key] = (s, s.n)

    rawi("gpsimd", lambda h: h.load_library(_MLP_LIB))
    wsv = wsmall.ap().rearrange("p (kc c) -> p kc c", c=33)
    dma("gpsimd", LD, wsv[:, :, 0:25], w_small_c.rearrange("(kc p) c -> p kc c", p=128))
    winT_v = winT_st.ap().rearrange("p (kc m) -> p kc m", m=DM)
    dma("gpsimd", LD, winT_v[:], w_inT[0:DM, :].rearrange("(kc p) m -> p kc m", p=128))
    winT_tl = winT_tl_t[:]
    dma("gpsimd", LD, winT_tl, w_inT[DM:DM + 8, :])
    wgm_v = wgm_st.ap().rearrange("p (kc c) -> p kc c", c=8)
    dma("gpsimd", LD, wgm_v[:], w_g_in[0:DM, :].rearrange("(kc p) c -> p kc c", p=128))
    wgm_tl = wgm_tl_t[:]
    dma("gpsimd", LD, wgm_tl, w_g_in[DM:DM + 8, :])
    for dst, src in [(ones1f[:], ones128_in), (e16d[:], E16_in), (e8d[:], E8_in),
                     (p8m[:], P8_in), (t16m[:], T16_in), (bd16m[:], BD16_in),
                     (iota_pp[:], iota_in), (selown[:], selOwn_in), (selst[:], selST_in),
                     (rb_pp[:], rb128), (bsi_pp[:], b_si32), (bg8s[:], b_g8)]:
        dma("sync", LDS, dst, src)
    bicv = bic.ap().rearrange("p (c o) -> p c o", o=1)
    dma("sync", LDS, bicv[:, 0:DMC, :], b_in_col[0:DM, :].rearrange("(c p) o -> p c o", p=128), slow=True)
    dma("sync", LDS, bic[0:8, DMC:DMC + 1], b_in_col[DM:DM + 8, :])
    dma("sync", LDS, bout_sb[:], b_out_row)
    v_op(lambda h: h.memset(zsrc[:], 0.0))
    v_op(lambda h: h.memset(zsrc_bf[:], 0.0))
    v_op(lambda h: h.memset(ones_bf[:], 1.0))
    v_op(lambda h: h.memset(epsc[:], float(EPS_LN)))
    v_op(lambda h: h.memset(ones33[:], 1.0))
    wait("vector", LD)
    wait("vector", LDS)
    v_op(lambda h: h.tensor_copy(bicb[:, 0:DMC], bic[:, 0:DMC]))
    v_op(lambda h: h.tensor_copy(bicb[0:8, DMC:DMC + 1], bic[0:8, DMC:DMC + 1]))

    wait("tensor", LD)
    wait("tensor", LDS)
    wait("tensor", VC)
    for mc in range(DMC):
        bwait("tensor", ("A", 0))
        for kc in range(DMC):
            rawi("tensor", lambda h, mc=mc, kc=kc: h.matmul(
                ps_A[:, 0:8], winT_v[:, kc, 128 * mc:128 * (mc + 1)], wgm_v[:, kc, :],
                start=(kc == 0), stop=False))
        finc("tensor", TC, lambda h, mc=mc: h.matmul(
            ps_A[:, 0:8], winT_tl_t[:, 128 * mc:128 * (mc + 1)], wgm_tl,
            start=False, stop=True))
        wait("scalar", TC)
        s_op(lambda h, mc=mc: h.copy(wsv[:, mc, 25:33], ps_A[:, 0:8]))
        bset(("A", 0), SCF)
    for kc in range(DMC):
        rawi("tensor", lambda h, kc=kc: h.matmul(
            ps_B[0:8, 0:1], wgm_v[:, kc, :],
            bicb.ap().rearrange("p (c o) -> p c o", o=1)[:, kc, :], start=(kc == 0), stop=False))
    finc("tensor", TC, lambda h: h.matmul(
        ps_B[0:8, 0:1], wgm_tl, bicb[0:8, DMC:DMC + 1], start=False, stop=True))
    wait("vector", TC)
    v_op(lambda h: h.tensor_tensor(bgeff[:], ps_B[0:8, 0:1], bg8s[:], ALU.add))
    wait("sync", VC)
    for r in range(4):
        dma("sync", STX, bgeff32[8 * r:8 * r + 8, :], bgeff[:])
    for nn in range(2):
        wait("tensor", VC)
        finc("tensor", TC, lambda h, nn=nn: h.matmul(
            ps_B[:, 512:1024], ones1f[:], bout_sb[:, 512 * nn:512 * (nn + 1)],
            start=True, stop=True))
        wait("vector", TC)
        v_op(lambda h, nn=nn: h.tensor_copy(bb_out[:, 512 * nn:512 * (nn + 1)],
                                            ps_B[:, 512:1024]))
    wait("tensor", SCF)
    wait("tensor", VC)
    for kc in range(DMC):
        fn = lambda h, kc=kc: h.matmul(ps_B[0:33, 1024:1025], wsv[:, kc, :], ones_bf[:],
                                       start=(kc == 0), stop=(kc == DMC - 1))
        finc("tensor", TC, fn) if kc == DMC - 1 else rawi("tensor", fn)
    wait("scalar", TC)
    s_op(lambda h: h.copy(scol33[:], ps_B[0:33, 1024:1025]))
    wait("sync", SCF)
    for g in range(8):
        dma("sync", STX, scol_pp[16 * g:16 * g + 16, :], scol33[0:16, :])
        dma("sync", STX, sg_pp[8 * g:8 * g + 8, :], scol33[16:24, :])
        dma("sync", STX, sg_pp[64 + 8 * g:64 + 8 * g + 8, :], scol33[25:33, :])

    wstream = [("swg", sw_gate, "ff"), ("swu", sw_up, "ff"), ("swd", sw_down, "dm")]
    for e_ in range(E):
        wstream += [(f"wg{e_}", w_gate[e_], "ff"), (f"wu{e_}", w_up[e_], "ff"),
                    (f"wd{e_}", w_down[e_], "dm")]
    pre_state = {"i": 0}

    def precast_one():
        i = pre_state["i"]
        if i >= NPRE:
            return
        nm, src_, kind = wstream[i]
        m2 = DM if kind == "dm" else FF
        dma("gpsimd", PCW, wbf_hbm[i].rearrange("p (c m2) -> p c m2", m2=m2),
            src_.rearrange("(c p) m2 -> p c m2", p=128))
        pre_state["i"] += 1

    sc_after_g = {}
    stx_after_g = {}
    wait("gpsimd", TC)
    for gg in range(NG):
        for kc in range(DMC):
            it = gg * DMC + kc
            par = it % 2
            if it >= 2:
                wait("gpsimd", XTFs[par])
                wait("gpsimd", XSQs[par])
            dma("gpsimd", XTs[par], xtb[par][:],
                xT[128 * kc:128 * (kc + 1), TPC * gg:TPC * (gg + 1)])
            wait("vector", XTs[par])
            if it >= 2:
                wait("vector", XTFs[par])
            finc("vector", XSQs[par],
                 lambda h, par=par: h.tensor_tensor(xsq[par][:], xtb[par][:], xtb[par][:], ALU.mult))
            wait("tensor", XTs[par])
            wait("tensor", XSQs[par])
            if kc == 0:
                bwait("tensor", ("A", "P1"))
                bwait("tensor", ("B", "P1"))
            for n4 in range(4):
                rawi("tensor", lambda h, kc=kc, par=par, n4=n4: h.matmul(
                    ps_A[0:33, 512 * n4:512 * (n4 + 1)], wsv[:, kc, :],
                    xtb[par][:, 512 * n4:512 * (n4 + 1)],
                    start=(kc == 0), stop=(kc == DMC - 1)))
            for n4 in range(4):
                fn = lambda h, kc=kc, par=par, n4=n4: h.matmul(
                    ps_B[64:65, 512 * n4:512 * (n4 + 1)], ones_bf[:],
                    xsq[par][:, 512 * n4:512 * (n4 + 1)],
                    start=(kc == 0), stop=(kc == DMC - 1))
                finc("tensor", XTFs[par], fn) if n4 == 3 else rawi("tensor", fn)
        lastpar = (gg * DMC + DMC - 1) % 2
        wait("scalar", XTFs[lastpar])
        if gg >= 1:
            wait("scalar", STX, stx_after_g[gg - 1])
        s_op(lambda h: h.copy(raws[0:33, :], ps_A[0:33, :]))
        s_op(lambda h: h.copy(raws[64:65, :], ps_B[64:65, 0:TPC]))
        sc_after_g[gg] = SCF.n
        bset(("A", "P1"), SCF)
        bset(("B", "P1"), SCF)
        wait("sync", SCF, sc_after_g[gg])
        dma("sync", STX, Lst[16 * gg:16 * gg + 16, :], raws[0:16, :])
        dma("sync", STX, GSst[8 * gg:8 * gg + 8, :], raws[16:24, :])
        dma("sync", STX, GSst[64 + 8 * gg:64 + 8 * gg + 8, :], raws[25:33, :])
        dma("sync", STX, mu8[gg:gg + 1, :], raws[24:25, :])
        dma("sync", STX, ss8[gg:gg + 1, :], raws[64:65, :])
        stx_after_g[gg] = STX.n

    while pre_state["i"] < NPRE:
        precast_one()
    PRE_ALL = PCW.n

    wait("vector", STX)
    v_op(lambda h: h.tensor_tensor(sc8a[:], mu8[:], mu8[:], ALU.mult))
    v_op(lambda h: h.tensor_scalar(is8[:], ss8[:], 1.0 / DM, None, ALU.mult))
    v_op(lambda h: h.tensor_tensor(is8[:], is8[:], sc8a[:], ALU.subtract))
    wait("scalar", VC)
    s_op(lambda h: h.activation(sc8a[:], is8[:], ACT.Sqrt, bias=epsc[0:8, :], scale=1.0))
    wait("vector", SCF)
    v_op(lambda h: h.reciprocal(is8[:], sc8a[:]))

    def mm128(dst_sb, lhs, src, psl=ps_A, rows=128):
        wait("tensor", VC)
        wait("tensor", SCF)
        for n4 in range(4):
            fn = lambda h, n4=n4: h.matmul(psl[0:rows, 512 * n4:512 * (n4 + 1)],
                                           lhs if isinstance(lhs, bass.AP) else lhs[:],
                                           src[:, 512 * n4:512 * (n4 + 1)],
                                           start=True, stop=True)
            finc("tensor", TC, fn) if n4 == 3 else rawi("tensor", fn)
        wait("scalar", TC)
        s_op(lambda h: h.copy(dst_sb[0:rows, 0:TPC] if rows != 128 else dst_sb[:],
                              psl[0:rows, 0:TPC]))

    mm128(q_bc1, e16d, mu8)
    mm128(q_bc2, e16d, is8)
    wait("vector", SCF)
    v_op(lambda h: h.tensor_scalar(q_l[:], q_bc1[:], scol_pp[:], None, ALU.mult))
    v_op(lambda h: h.tensor_tensor(q_l[:], Lst[:], q_l[:], ALU.subtract))
    v_op(lambda h: h.tensor_tensor(q_l[:], q_l[:], q_bc2[:], ALU.mult))
    v_op(lambda h: h.tensor_scalar(q_l[:], q_l[:], rb_pp[:], None, ALU.add))
    dbg_dump("l", q_l[:])

    shmask = lambda d: [(p // 16) * 16 + ((p % 16) + d) % 16 for p in range(32)]

    def tree_max(buf, tmp):
        for d in (8, 4, 2, 1):
            v_op(lambda h, d=d, buf=buf, tmp=tmp: h.stream_shuffle(tmp[:], buf[:], shmask(d)))
            v_op(lambda h, buf=buf, tmp=tmp: h.tensor_tensor(buf[:], buf[:], tmp[:], ALU.max))

    v_op(lambda h: h.tensor_copy(q_t1[:], q_l[:]))
    tree_max(q_t1, q_t2)
    v_op(lambda h: h.tensor_tensor(q_t2[:], q_l[:], q_t1[:], ALU.subtract))
    wait("scalar", VC)
    s_op(lambda h: h.activation(q_e[:], q_t2[:], ACT.Exp))
    wait("vector", SCF)
    mm128(q_rz, bd16m, q_e)
    wait("vector", SCF)
    v_op(lambda h: h.reciprocal(q_rz[:], q_rz[:]))
    v_op(lambda h: h.tensor_scalar(q_o1[:], q_e[:], 1.0, None, ALU.is_equal))
    mm128(q_t2, t16m, q_o1)
    wait("vector", SCF)
    v_op(lambda h: h.tensor_scalar(q_t2[:], q_t2[:], 1.0, None, ALU.is_equal))
    v_op(lambda h: h.tensor_tensor(q_o1[:], q_o1[:], q_t2[:], ALU.mult))
    v_op(lambda h: h.tensor_scalar(q_t2[:], q_o1[:], iota_pp[:], None, ALU.mult))
    mm128(q_i1, bd16m, q_t2)
    wait("vector", SCF)
    v_op(lambda h: h.tensor_tensor(q_t2[:], q_e[:], q_o1[:], ALU.mult))
    v_op(lambda h: h.tensor_tensor(q_t2[:], q_e[:], q_t2[:], ALU.subtract))
    v_op(lambda h: h.tensor_copy(q_w2[:], q_t2[:]))
    tree_max(q_w2, q_t1)
    v_op(lambda h: h.tensor_tensor(q_o2[:], q_t2[:], q_w2[:], ALU.is_equal))
    mm128(q_t1, t16m, q_o2)
    wait("vector", SCF)
    v_op(lambda h: h.tensor_scalar(q_t1[:], q_t1[:], 1.0, None, ALU.is_equal))
    v_op(lambda h: h.tensor_tensor(q_o2[:], q_o2[:], q_t1[:], ALU.mult))
    v_op(lambda h: h.tensor_scalar(q_t1[:], q_o2[:], iota_pp[:], None, ALU.mult))
    mm128(q_i2, bd16m, q_t1)
    dbg_dump("i1", q_i1[:])
    dbg_dump("i2", q_i2[:])
    wait("vector", SCF)
    v_op(lambda h: h.tensor_tensor(q_w2[:], q_w2[:], q_rz[:], ALU.mult))
    v_op(lambda h: h.tensor_tensor(q_t1[:], q_rz[:], q_w2[:], ALU.add))
    v_op(lambda h: h.tensor_scalar(q_t1[:], q_t1[:], 1e-9, None, ALU.add))
    v_op(lambda h: h.reciprocal(q_t1[:], q_t1[:]))
    v_op(lambda h: h.tensor_tensor(q_w1[:], q_rz[:], q_t1[:], ALU.mult))
    v_op(lambda h: h.tensor_tensor(q_w2[:], q_w2[:], q_t1[:], ALU.mult))
    v_op(lambda h: h.tensor_tensor(q_t1[:], q_o1[:], q_o2[:], ALU.add))
    v_op(lambda h: h.tensor_tensor_scan(q_ex[:], q_t1[:], q_t1[:], 0.0, ALU.add, ALU.bypass))
    v_op(lambda h: h.tensor_copy(q_t2[:, 0:1], q_ex[:, TPC - 1:TPC]))
    v_op(lambda h: h.tensor_tensor(q_ex[:], q_ex[:], q_t1[:], ALU.subtract))
    wait("tensor", VC)
    wait("tensor", SCF)
    finc("tensor", TC, lambda h: h.matmul(ps_B[:, 0:1], p8m[:], q_t2[:, 0:1],
                                          start=True, stop=True))
    wait("scalar", TC)
    s_op(lambda h: h.copy(gpre_pp[:], ps_B[:, 0:1]))
    wait("vector", SCF)
    v_op(lambda h: h.tensor_scalar(q_cg[:], q_ex[:], gpre_pp[:], None, ALU.add))
    dbg_dump("cumG", q_cg[:])

    def pos_k(oh, ii, wv, drv):
        v_op(lambda h: h.tensor_tensor(q_t1[:], oh[:], q_cg[:], ALU.mult))
        mm128(q_t2, bd16m, q_t1)
        wait("vector", SCF)
        v_op(lambda h: h.tensor_scalar(q_t2[:], q_t2[:], float(CAP), None, ALU.is_lt))
        v_op(lambda h: h.tensor_tensor(wv[:], wv[:], q_t2[:], ALU.mult))
        v_op(lambda h: h.tensor_tensor(q_t1[:], oh[:], q_ex[:], ALU.mult))
        mm128(drv, bd16m, q_t1)
        wait("vector", SCF)
        v_op(lambda h: h.tensor_scalar(q_t1[:], drv[:], float(RL), None, ALU.is_lt))
        v_op(lambda h: h.tensor_tensor(q_t1[:], q_t1[:], q_t2[:], ALU.mult))
        v_op(lambda h: h.tensor_scalar(q_t2[:], ii[:], float(RL), None, ALU.mult))
        v_op(lambda h: h.tensor_tensor(drv[:], q_t2[:], drv[:], ALU.add))
        v_op(lambda h: h.tensor_tensor(drv[:], drv[:], q_t1[:], ALU.mult))
        v_op(lambda h: h.tensor_scalar(q_t1[:], q_t1[:], -float(TR1), float(TR1), ALU.mult, ALU.add))
        v_op(lambda h: h.tensor_tensor(drv[:], drv[:], q_t1[:], ALU.add))

    pos_k(q_o1, q_i1, q_w1, q_dr1)
    pos_k(q_o2, q_i2, q_w2, q_dr2)
    dbg_dump("w1", q_w1[:])
    dbg_dump("w2", q_w2[:])
    dbg_dump("dr1", q_dr1[:])
    dbg_dump("dr2", q_dr2[:])

    mm128(q_t1, e8d, mu8)
    mm128(q_t2, e8d, is8)
    wait("vector", SCF)
    v_op(lambda h: h.tensor_scalar(q_e[:], q_t1[:], sg_pp[:], None, ALU.mult))
    v_op(lambda h: h.tensor_tensor(q_e[:], GSst[:], q_e[:], ALU.subtract))
    v_op(lambda h: h.tensor_tensor(q_e[:], q_e[:], q_t2[:], ALU.mult))
    wait("sync", VC)
    wait("sync", SCF)
    for bb_ in range(4):
        for hh in range(2):
            g_ = 2 * bb_ + hh
            dma("sync", STX, sic.ap()[8 * bb_:8 * bb_ + 8, 2048 * hh:2048 * (hh + 1)],
                q_e[8 * g_:8 * g_ + 8, :])
            dma("sync", STX, gsc.ap()[8 * bb_:8 * bb_ + 8, 2048 * hh:2048 * (hh + 1)],
                q_e[64 + 8 * g_:64 + 8 * g_ + 8, :])
    wait("scalar", STX)
    s_op(lambda h: h.activation(gsig.ap()[0:32, :], gsc.ap()[0:32, :], ACT.Sigmoid,
                                bias=bgeff32[:], scale=1.0))
    dbg_dump("gsig", gsig.ap()[0:32, :])
    wait("vector", STX)
    wait("vector", SCF)
    v_op(lambda h: h.tensor_scalar(sic.ap()[0:32, :], sic.ap()[0:32, :], bsi_pp[:], None, ALU.add))
    dbg_dump("sib", sic.ap()[0:32, :])
    v_op(lambda h: h.tensor_scalar(gsc.ap()[0:32, :], gsig.ap()[0:32, :], -1.0, 1.0, ALU.mult, ALU.add))
    v_op(lambda h: h.tensor_tensor(gsc.ap()[0:32, :], gsc.ap()[0:32, :], sic.ap()[0:32, :], ALU.mult))
    v_op(lambda h: h.tensor_tensor_scan(sic.ap()[0:32, :], gsig.ap()[0:32, :], gsc.ap()[0:32, :],
                                        0.0, ALU.mult, ALU.add))
    dbg_dump("st", sic.ap()[0:32, :])
    wait("sync", VC)
    for hh in range(2):
        dma("sync", STX, st64.ap()[32 * hh:32 * hh + 32, :],
            sic.ap()[0:32, 2048 * hh:2048 * (hh + 1)])
    wait("tensor", STX)
    wait("tensor", SCF)
    wait("tensor", VC)
    for n4 in range(4):
        fn = lambda h, n4=n4: h.matmul(ps_B[0:8, 512 * n4:512 * (n4 + 1)], selst[:],
                                       st64.ap()[:, 512 * n4:512 * (n4 + 1)], start=True, stop=True)
        finc("tensor", TC, fn) if n4 == 3 else rawi("tensor", fn)
    wait("scalar", TC)
    s_op(lambda h: h.copy(stT[:], ps_B[0:8, 0:TPC]))
    dbg_dump("stT", stT[:])

    mm128(q_bc1, e16d, mu8)
    mm128(q_bc2, e16d, is8)
    extr = [(ps_A, 0, ownrowA, 0, q_bc1), (ps_A, 32, ownrowA, 32, q_w1),
            (ps_A, 64, ownrowA, 64, q_w2), (ps_B, 0, ownrowB, 0, q_dr1),
            (ps_B, 32, ownrowB, 32, q_dr2), (ps_B, 64, ownrowB, 64, q_bc2)]
    for psx, pb, dstT, drow, srcq in extr:
        wait("tensor", VC)
        wait("tensor", SCF)
        for n4 in range(4):
            fn = lambda h, n4=n4, srcq=srcq, psx=psx, pb=pb: h.matmul(
                psx[pb:pb + 16, 512 * n4:512 * (n4 + 1)], selown[:],
                srcq[:, 512 * n4:512 * (n4 + 1)], start=True, stop=True)
            finc("tensor", TC, fn) if n4 == 3 else rawi("tensor", fn)
        wait("scalar", TC)
        s_op(lambda h, psx=psx, pb=pb, dstT=dstT, drow=drow: h.copy(
            dstT[drow:drow + 1, :], psx[pb:pb + 1, 0:TPC]))
    wait("vector", SCF)
    v_op(lambda h: h.tensor_copy(dr_i16[0:1, :], ownrowB[0:1, :]))
    v_op(lambda h: h.tensor_copy(dr_i16[32:33, :], ownrowB[32:33, :]))
    wait("gpsimd", VC)
    dma("gpsimd", GPD, dr_w[0:8, :], dr_i16[0:1, :].rearrange("o (q j) -> o q j", q=8))
    dma("gpsimd", GPD, dr_w[8:16, :], dr_i16[32:33, :].rearrange("o (q j) -> o q j", q=8))
    wait("gpsimd", GPD)
    for rep in range(1, 8):
        dma("gpsimd", GPD, dr_w[16 * rep:16 * (rep + 1), :], dr_w[0:16, :])
    wait("sync", SCF)
    for dst, srcT, row in [(w0_pp, ownrowA, 32), (w1_pp, ownrowA, 64),
                           (mu_pp, ownrowA, 0), (is_pp, ownrowB, 64)]:
        for vv in range(TV):
            dma("sync", STX, dst[:, vv:vv + 1],
                srcT[row:row + 1, 128 * vv:128 * (vv + 1)].rearrange("o (q c) -> o q c", c=1),
                slow=True)

    ctxA.close()
    ctxT = ExitStack()
    xtile = [ctxT.enter_context(nc.sbuf_tensor(f"xtile{i}", [128, DM], F32)) for i in range(2)]
    ttile = ctxT.enter_context(nc.sbuf_tensor("ttile", [128, DM], F32))
    dtile = ctxT.enter_context(nc.sbuf_tensor("dtile", [128, DM], F32))
    otile = ctxT.enter_context(nc.sbuf_tensor("otile", [128, DM], F32))
    ctxB0 = ExitStack()
    enB0 = ctxB0.enter_context
    btile = enB0(nc.sbuf_tensor("btile", [128, DM], BF16))
    gtl = enB0(nc.sbuf_tensor("gtl", [128, 512], F32))
    ctxB1 = ExitStack()
    xnT = ctxB1.enter_context(nc.sbuf_tensor("xnT", [128, DMC * TPC], BF16))
    ctxB2a = ExitStack()
    embed = ctxB2a.enter_context(nc.sbuf_tensor("embed", [128, DMC * TPC], BF16))
    ctxB5t = ExitStack()
    enB5 = ctxB5t.enter_context
    tbl1 = enB5(nc.sbuf_tensor("tbl1", [128, NRV * 64], F32))
    tbl2 = enB5(nc.sbuf_tensor("tbl2", [128, NSV * 64], F32))
    slot1s = nc.alloc_sbuf_tensor_at("slot1s", [128, NSV * 64], F32,
                                     offset=nc.lookup_mloc(tbl1).addr)
    idx_f = enB5(nc.sbuf_tensor("idx_f", [128, NRV], F32))
    k_f = enB5(nc.sbuf_tensor("k_f", [128, NRV], F32))
    mskE = enB5(nc.sbuf_tensor("mskE", [128, NRV], F32))
    gidx_f = enB5(nc.sbuf_tensor("gidx_f", [128, NRV], F32))
    s2_f = enB5(nc.sbuf_tensor("s2_f", [128, NRV], F32))
    scrE = enB5(nc.sbuf_tensor("scrE", [128, NRV], F32))
    gidx_i = enB5(nc.sbuf_tensor("gidx_i", [128, NRV], I16))
    s2_i = enB5(nc.sbuf_tensor("s2_i", [128, NRV], I16))
    rid_f = enB5(nc.sbuf_tensor("rid_f", [128, NSV], F32))
    rid_i = enB5(nc.sbuf_tensor("rid_i", [128, NSV], I16))
    rowcs = nc.alloc_sbuf_tensor_at("rowcs", [128, NRV * 64], F32,
                                    offset=nc.lookup_mloc(tbl1).addr)
    wait("scalar", VC, 3)
    zv8 = zsrc.ap().rearrange("p (v c) -> p v c", c=64)
    for c6 in range(6):
        dma("scalar", SCD, inv1_hbm[1024 * c6:1024 * (c6 + 1), :].rearrange("(v p) c -> p v c", p=128),
            zv8[:, 0:8, :])
    dma("scalar", SCD, inv1_hbm[NR:NR + 1, :], zsrc[0:1, 0:64])
    for c4 in range(4):
        dma("scalar", SCD, tb2_hbm[1024 * c4:1024 * (c4 + 1), :].rearrange("(v p) c -> p v c", p=128),
            zv8[:, 0:8, :])
    dma("scalar", SCD, tb2_hbm[NS:NS + 1, :], zsrc[0:1, 0:64])
    dma("scalar", SCD, eo_hbm[EOTR:EOTR + 1, :], zsrc_bf[0:1, :])
    INIT_ALL = SCD.n
    wait("sync", VC)
    wait("sync", TC)
    wait("sync", SCF)
    dma("sync", STX, slot1s[:], slot1c_in)
    PAY_ALL = STX.n
    ctxB2b = ExitStack()
    winm_sb = ctxB2b.enter_context(nc.sbuf_tensor("winm_sb", [128, DMC * DM], BF16))
    wout_sb = ctxB2b.enter_context(nc.sbuf_tensor("wout_sb", [128, DMC * DM], BF16))
    wout_tl = ctxB2b.enter_context(nc.sbuf_tensor("wout_tl", [8, DM], BF16))
    mubc = ctxB2b.enter_context(nc.sbuf_tensor("mubc", [128, TPC], BF16))
    isbc = ctxB2b.enter_context(nc.sbuf_tensor("isbc", [128, TPC], BF16))
    xob = [ctxB2b.enter_context(nc.sbuf_tensor(f"xob{i}", [128, TPC], BF16)) for i in range(2)]
    for dst, srcT, row in [(mubc, ownrowA, 0), (isbc, ownrowB, 64)]:
        wait("tensor", SCF)
        wait("tensor", VC)
        for n4 in range(4):
            fn = lambda h, n4=n4, srcT=srcT, row=row: h.matmul(
                ps_A[:, 512 * n4:512 * (n4 + 1)], ones33[row:row + 1, :],
                srcT[row:row + 1, 512 * n4:512 * (n4 + 1)], start=True, stop=True)
            finc("tensor", TC, fn) if n4 == 3 else rawi("tensor", fn)
        wait("scalar", TC)
        s_op(lambda h, dst=dst: h.copy(dst[:], ps_A[:, 0:TPC]))
    wait("gpsimd", VC)
    wait("gpsimd", TC)
    dma("gpsimd", LD, winm_sb.ap().rearrange("p (kc m) -> p kc m", m=DM),
        w_in_m.rearrange("(kc p) m -> p kc m", p=128))
    dma("gpsimd", LD, wout_sb.ap().rearrange("p (kc m) -> p kc m", m=DM),
        w_out_in[0:DM, :].rearrange("(kc p) m -> p kc m", p=128))
    dma("gpsimd", LD, wout_tl[:], w_out_in[DM:DM + 8, :])
    wait("gpsimd", SCD, INIT_ALL)
    wait("gpsimd", STX, PAY_ALL)
    wait("gpsimd", GPD)
    finc("gpsimd", MOE, lambda h: h.dma_scatter_add(
        inv1_hbm, slot1s.ap().rearrange("p (v c) -> p v c", c=64), dr_w[:],
        NS, NS, 64), amt=16)
    xnv = xnT.ap().rearrange("p (kc n) -> p kc n", n=TPC)
    for kc in range(DMC):
        par = kc % 2
        if kc >= 2:
            wait("gpsimd", XOF[par])
        dma("gpsimd", XOT[par], xob[par][:], xT_own[128 * kc:128 * (kc + 1), :])
        wait("vector", XOT[par])
        wait("vector", SCF)
        wait("vector", VC)
        if kc >= 1:
            wait("vector", XOF[1 - par])
        finc("vector", VC, lambda h, par=par: h.tensor_tensor(
            ttile[:, 0:DM] if False else dtile[:], xob[par][:, 0:DM], mubc[:, 0:DM], ALU.subtract))
        finc("vector", VC, lambda h, par=par: h.tensor_tensor(
            otile[:], xob[par][:, DM:TPC], mubc[:, DM:TPC], ALU.subtract))
        wait("vector", VC)
        finc("vector", VC, lambda h, kc=kc: h.tensor_tensor(
            xnv[:, kc, 0:DM], dtile[:], isbc[:, 0:DM], ALU.mult))
        finc("vector", XOF[par], lambda h, kc=kc: h.tensor_tensor(
            xnv[:, kc, DM:TPC], otile[:], isbc[:, DM:TPC], ALU.mult))
    XNDONE = (XOF[0].n, XOF[1].n)
    wait("vector", STX)
    for vv in range(TV):
        par = vv % 2
        if vv >= 2:
            wait("sync", XLF[par])
        dma("sync", XL[par], xtile[par][:], x_own[128 * vv:128 * (vv + 1), :])
        wait("vector", XL[par])
        if vv >= 1:
            wait("vector", XBS)
            wait("vector", XLF[1 - par])
        wait("vector", VC)
        finc("vector", VC, lambda h, par=par, vv=vv: h.tensor_scalar(
            ttile[:], xtile[par][:], mu_pp[:, vv:vv + 1], None, ALU.subtract))
        wait("vector", VC)
        finc("vector", XLF[par], lambda h, vv=vv: h.tensor_scalar(
            btile[:], ttile[:], is_pp[:, vv:vv + 1], None, ALU.mult))
        wait("scalar", XLF[par])
        dma("scalar", XBS, xb_hbm[128 * vv:128 * (vv + 1), :], btile[:])
    XBS_ALL = XBS.n

    wait("gpsimd", TC)
    wait("gpsimd", VC)
    wait("gpsimd", STX)
    _own_base = nc.lookup_mloc(ownrowA).addr
    assert nc.lookup_mloc(ownrowB).addr == _own_base + 8192
    assert nc.lookup_mloc(dr_i16).addr == _own_base + 16384
    assert nc.lookup_mloc(xtile[0]).addr == _own_base + 20480
    wbuf4_t = nc.alloc_sbuf_tensor_at("wbuf4", [128, 11264], BF16, offset=_own_base)
    wld_mark = {}
    _wu0 = wstream[4][1]
    dma("gpsimd", WLD[4], wbuf4_t.ap().rearrange("p (c m) -> p c m", m=FF)[:, 0:DMC, :],
        _wu0.rearrange("(c p) m -> p c m", p=128))
    wld_mark[4] = (4, WLD[4].n)

    wait("gpsimd", MOE)
    dma("gpsimd", MOE, tbl1.ap().rearrange("p (v c) -> p v c", c=64),
        inv1_hbm[0:NR, :].rearrange("(v p) c -> p v c", p=128))
    TBL1_ALL = MOE.n
    if "inv1" in dbg:
        wait("scalar", MOE, TBL1_ALL)
        dma("scalar", SCD, dbg["inv1"], inv1_hbm)
    if "drw" in dbg:
        wait("scalar", GPD)
        dma("scalar", SCD, dbg["drw"], dr_w[:])
    wait("vector", MOE, TBL1_ALL)
    t1v = tbl1.ap().rearrange("p (v c) -> p v c", c=64)
    v_op(lambda h: h.tensor_copy(idx_f.ap().rearrange("p (v o) -> p v o", o=1), t1v[:, :, 0:1]))
    v_op(lambda h: h.tensor_scalar(idx_f[:], idx_f[:], 1.0, None, ALU.subtract))
    v_op(lambda h: h.tensor_copy(k_f.ap().rearrange("p (v o) -> p v o", o=1), t1v[:, :, 1:2]))
    v_op(lambda h: h.tensor_scalar(mskE[:], idx_f[:], 0.0, None, ALU.is_lt))
    v_op(lambda h: h.tensor_tensor(gidx_f[:], idx_f[:], mskE[:], ALU.add))
    v_op(lambda h: h.tensor_scalar(scrE[:], mskE[:], -1.0, 1.0, ALU.mult, ALU.add))
    v_op(lambda h: h.tensor_tensor(s2_f[:], k_f[:], scrE[:], ALU.mult))
    v_op(lambda h: h.tensor_scalar(s2_f[:], s2_f[:], float(TPC), None, ALU.mult))
    v_op(lambda h: h.tensor_tensor(s2_f[:], s2_f[:], gidx_f[:], ALU.add))
    v_op(lambda h: h.tensor_scalar(scrE[:], mskE[:], float(TR2), None, ALU.mult))
    v_op(lambda h: h.tensor_tensor(s2_f[:], s2_f[:], scrE[:], ALU.add))
    v_op(lambda h: h.tensor_copy(gidx_i[:], gidx_f[:]))
    v_op(lambda h: h.tensor_copy(s2_i[:], s2_f[:]))
    dbg_dump("idxf", idx_f[:])
    dbg_dump("kf", k_f[:])

    def wrap8(t_i, t_w, nv, col0=0):
        wv = t_w.ap().rearrange("p (v hh) -> p v hh", hh=8)
        for hh in range(8):
            dma("sync", STX, wv[0:16, 0:nv, hh], t_i[16 * hh:16 * hh + 16, col0:col0 + nv],
                slow=True)
        wait("sync", STX)
        for rep in range(1, 8):
            dma("sync", STX, t_w[16 * rep:16 * (rep + 1), 0:nv * 8], t_w[0:16, 0:nv * 8])
        wait("sync", STX)

    wait("sync", VC)
    dma("sync", STX, rowcs.ap()[:, :], rowc_in)
    wrap8(gidx_i, gidx_w, NRV)
    wrap8(s2_i, s2_w, NRV)
    WRAP_ALL = STX.n
    wait("gpsimd", STX, WRAP_ALL)
    finc("gpsimd", MOE, lambda h: h.dma_scatter_add(
        tb2_hbm, rowcs.ap().rearrange("p (v c) -> p v c", c=64), s2_w[:],
        NR, NR, 64), amt=16)
    wait("gpsimd", MOE)
    dma("gpsimd", MOE, tbl2.ap().rearrange("p (v c) -> p v c", c=64),
        tb2_hbm[0:NS, :].rearrange("(v p) c -> p v c", p=128))
    TBL2_ALL = MOE.n
    winv = winm_sb.ap().rearrange("p (kc m) -> p kc m", m=DM)
    emv = embed.ap().rearrange("p (mc n) -> p mc n", n=TPC)
    wait("tensor", LD)
    wait("tensor", XOF[0], XNDONE[0])
    wait("tensor", XOF[1], XNDONE[1])
    for mc in range(DMC):
        for n4 in range(4):
            bank = ("A", (mc * 4 + n4) % 4)
            bwait("tensor", bank)
            bs = 512 * bank[1]
            for kc in range(DMC):
                fn = lambda h, mc=mc, n4=n4, kc=kc, bs=bs: h.matmul(
                    ps_A[:, bs:bs + 512], winv[:, kc, 128 * mc:128 * (mc + 1)],
                    xnv[:, kc, 512 * n4:512 * (n4 + 1)],
                    start=(kc == 0), stop=(kc == DMC - 1))
                finc("tensor", TC, fn) if kc == DMC - 1 else rawi("tensor", fn)
            wait("vector", TC)
            wait("vector", VC)
            finc("vector", VC, lambda h, mc=mc, n4=n4, bs=bs: h.tensor_scalar(
                emv[:, mc, 512 * n4:512 * (n4 + 1)], ps_A[:, bs:bs + 512],
                bic[:, mc:mc + 1], None, ALU.add))
            bset(bank, VC)
    wouv = wout_sb.ap().rearrange("p (kc m) -> p kc m", m=DM)
    wait("tensor", LD)
    wait("tensor", VC)
    wait("vector", XBS, XBS_ALL)
    for vv in range(TV):
        if vv >= 1:
            wait("vector", SST)
        for n2 in range(2):
            bank = ("B", (vv * 2 + n2) % 2)
            bwait("tensor", bank)
            bs = 512 * bank[1]
            for kc in range(DMC):
                rawi("tensor", lambda h, vv=vv, n2=n2, kc=kc, bs=bs: h.matmul(
                    ps_B[:, bs:bs + 512], emv[:, kc, 128 * vv:128 * (vv + 1)],
                    wouv[:, kc, 512 * n2:512 * (n2 + 1)], start=(kc == 0), stop=False))
            finc("tensor", TC, lambda h, vv=vv, n2=n2, bs=bs: h.matmul(
                ps_B[:, bs:bs + 512], stT[:, 128 * vv:128 * (vv + 1)],
                wout_tl[:, 512 * n2:512 * (n2 + 1)], start=False, stop=True))
            wait("vector", TC)
            wait("vector", VC)
            finc("vector", VC, lambda h, n2=n2, bs=bs: h.tensor_tensor(
                btile[:, 512 * n2:512 * (n2 + 1)], ps_B[:, bs:bs + 512],
                bb_out[:, 512 * n2:512 * (n2 + 1)], ALU.add))
            bset(bank, VC)
        wait("scalar", VC)
        dma("scalar", SST, s_hbm[128 * vv:128 * (vv + 1), :], btile[:])
    SST_ALL = SST.n
    wait("vector", MOE, TBL2_ALL)
    t2v = tbl2.ap().rearrange("p (v c) -> p v c", c=64)
    v_op(lambda h: h.tensor_copy(rid_f.ap().rearrange("p (v o) -> p v o", o=1), t2v[:, :, 0:1]))
    v_op(lambda h: h.tensor_scalar(rid_f[:], rid_f[:], 1.0, None, ALU.subtract))
    v_op(lambda h: h.tensor_scalar(scrE[:, 0:NSV], rid_f[:], 0.0, None, ALU.is_lt))
    v_op(lambda h: h.tensor_scalar(scrE[:, 0:NSV], scrE[:, 0:NSV], float(EOTR + 1), None, ALU.mult))
    v_op(lambda h: h.tensor_tensor(rid_f[:], rid_f[:], scrE[:, 0:NSV], ALU.add))
    v_op(lambda h: h.tensor_copy(rid_i[:], rid_f[:]))
    dbg_dump("rid", rid_f[:])
    wait("sync", VC)
    wrap8(rid_i, r0_w, TV, col0=0)
    wrap8(rid_i, r1_w, TV, col0=TV)
    RWRAP_ALL = STX.n

    ctxB2b.close()
    ctxB5t.close()
    ctxB2a.close()
    ctxB3a = ExitStack()
    NWB = 5
    wbuf = [ctxB3a.enter_context(nc.sbuf_tensor(f"wbuf{i}", [128, 11264], BF16))
            for i in range(NWB - 1)]
    wbuf.append(wbuf4_t)
    ctxB3b = ExitStack()
    a_sht = ctxB3b.enter_context(nc.sbuf_tensor("a_sht", [128, FFC * 512], BF16))
    wload_state = {"i": 0}
    wfr_expect = [0] * NWB

    def wload():
        i = wload_state["i"]
        nm, src, kind = wstream[i]
        par = i % NWB
        if i == 0:
            wait("gpsimd", PCW, PRE_ALL)
        if i >= NWB:
            wait("gpsimd", WFR[par], wfr_expect[par])
        if i < NPRE:
            dma("gpsimd", WLD[par], wbuf[par].ap(), wbf_hbm[i])
        elif kind == "dm":
            view = wbuf[par].ap().rearrange("p (c m) -> p c m", m=DM)[:, 0:FFC, :]
            dma("gpsimd", WLD[par], view, src.rearrange("(c p) m -> p c m", p=128))
        else:
            view = wbuf[par].ap().rearrange("p (c m) -> p c m", m=FF)[:, 0:DMC, :]
            dma("gpsimd", WLD[par], view, src.rearrange("(c p) m -> p c m", p=128))
        wld_mark[i] = (par, WLD[par].n)
        wload_state["i"] += 1
        return i

    def wrelease(i):
        par = i % NWB
        wfr_expect[par] = WFR[par].n

    wait("gpsimd", TC)
    wait("gpsimd", STX, RWRAP_ALL)
    wait("gpsimd", VC)
    for _ in range(NWB - 1):
        wload()
    wload_state["i"] += 1

    ashv = a_sht.ap().rearrange("p (fc n) -> p fc n", n=512)

    def wview(i, kind):
        par = i % NWB
        if kind == "dm":
            return wbuf[par].ap().rearrange("p (c m) -> p c m", m=DM)
        return wbuf[par].ap().rearrange("p (c m) -> p c m", m=FF)

    ig, iu, idn = 0, 1, 2
    gv = wview(ig, "ff")
    uv = wview(iu, "ff")
    dv = wview(idn, "dm")
    wait("tensor", WLD[wld_mark[ig][0]], wld_mark[ig][1])
    wait("tensor", WLD[wld_mark[iu][0]], wld_mark[iu][1])
    for t4 in range(4):
        if t4 >= 1:
            wait("vector", TC)
        for fc in range(FFC):
            gb = ("A", fc % 2)
            ub = ("A", 2 + fc % 2)
            bwait("tensor", gb)
            bwait("tensor", ub)
            gbs, ubs = 512 * gb[1], 512 * ub[1]
            for kc in range(DMC):
                fn = lambda h, fc=fc, t4=t4, kc=kc, gbs=gbs: h.matmul(
                    ps_A[:, gbs:gbs + 512], gv[:, kc, 128 * fc:128 * (fc + 1)],
                    xnv[:, kc, 512 * t4:512 * (t4 + 1)],
                    start=(kc == 0), stop=(kc == DMC - 1))
                if t4 == 3 and fc == FFC - 1 and kc == DMC - 1:
                    finc("tensor", WFR[ig], fn)
                else:
                    rawi("tensor", fn)
            for kc in range(DMC):
                fn = lambda h, fc=fc, t4=t4, kc=kc, ubs=ubs: h.matmul(
                    ps_A[:, ubs:ubs + 512], uv[:, kc, 128 * fc:128 * (fc + 1)],
                    xnv[:, kc, 512 * t4:512 * (t4 + 1)],
                    start=(kc == 0), stop=(kc == DMC - 1))
                if kc == DMC - 1:
                    if t4 == 3 and fc == FFC - 1:
                        finc("tensor", WFR[iu], fn)
                    else:
                        finc("tensor", TC, fn)
                else:
                    rawi("tensor", fn)
            if t4 == 3 and fc == FFC - 1:
                wait("scalar", WFR[iu])
            else:
                wait("scalar", TC)
            wait("scalar", VC)
            s_op(lambda h, gbs=gbs: h.activation(gtl[:], ps_A[:, gbs:gbs + 512], ACT.Silu))
            bset(gb, SCF)
            wait("vector", SCF)
            wait("vector", VC)
            finc("vector", VC, lambda h, fc=fc, ubs=ubs: h.tensor_tensor(
                ashv[:, fc, :], gtl[:], ps_A[:, ubs:ubs + 512], ALU.mult))
            bset(ub, VC)
        if t4 == 0:
            wait("tensor", WLD[wld_mark[idn][0]], wld_mark[idn][1])
            wait("vector", SST, SST_ALL)
        wait("tensor", VC)
        for vsub in range(4):
            vv = 4 * t4 + vsub
            if vv >= 1:
                wait("vector", SHST)
            for n2 in range(2):
                bank = ("B", n2)
                bwait("tensor", bank)
                bs = 512 * bank[1]
                for fc in range(FFC):
                    fn = lambda h, vsub=vsub, n2=n2, fc=fc, bs=bs: h.matmul(
                        ps_B[:, bs:bs + 512], ashv[:, fc, 128 * vsub:128 * (vsub + 1)],
                        dv[:, fc, 512 * n2:512 * (n2 + 1)],
                        start=(fc == 0), stop=(fc == FFC - 1))
                    if fc == FFC - 1:
                        if vv == TV - 1 and n2 == 1:
                            finc("tensor", WFR[idn], fn)
                        else:
                            finc("tensor", TC, fn)
                    else:
                        rawi("tensor", fn)
                if vv == TV - 1 and n2 == 1:
                    wait("vector", WFR[idn])
                else:
                    wait("vector", TC)
                wait("vector", VC)
                finc("vector", VC, lambda h, n2=n2, bs=bs: h.tensor_copy(
                    btile[:, 512 * n2:512 * (n2 + 1)], ps_B[:, bs:bs + 512]))
                bset(bank, VC)
            wait("scalar", VC)
            dma("scalar", SHST, sh_hbm[128 * vv:128 * (vv + 1), :], btile[:])
    SHST_ALL = SHST.n
    wrelease(ig)
    wrelease(iu)
    wrelease(idn)
    SHG_DONE = WFR[ig].n
    SHU_DONE = WFR[iu].n
    for _ in range(6 - NWB):
        wload()

    ctxB3b.close()
    xnT_base = nc.lookup_mloc(xnT).addr
    eo_t = nc.alloc_sbuf_tensor_at("eo_t", [128, RLC * DM], BF16, offset=xnT_base)
    xeT = [nc.alloc_sbuf_tensor_at(f"xeT{i}", [128, DMC * RL], BF16,
                                   offset=xnT_base + RLC * DM * 2 + i * DMC * RL * 2)
           for i in range(2)]
    a_e = nc.alloc_sbuf_tensor_at("a_e", [128, FFC * RL], BF16,
                                  offset=xnT_base + RLC * DM * 2 + 2 * DMC * RL * 2)
    assert RLC * DM * 2 + 2 * DMC * RL * 2 + FFC * RL * 2 <= DMC * TPC * 2
    aev = a_e.ap().rearrange("p (fc n) -> p fc n", n=RL)

    gxf_mark = [0, 0]
    for e_ in range(E):
        par = e_ % 2
        if e_ == 0:
            wait("gpsimd", WFR[0], SHG_DONE)
            wait("gpsimd", WFR[1], SHU_DONE)
        if e_ >= 2:
            wait("gpsimd", GXF[par], gxf_mark[par])
        wait("gpsimd", XBS, XBS_ALL)
        wait("gpsimd", STX, WRAP_ALL)
        finc("gpsimd", GX[par], lambda h, par=par, e_=e_: h.dma_gather(
            xeT[par].ap().rearrange("p (c n) -> p c n", n=RL), xb_hbm,
            gidx_w[:, (RL // 16) * e_:(RL // 16) * (e_ + 1)], RL, RL, DM,
            transpose=True), amt=16)
        if e_ == 0 and "xe0" in dbg:
            wait("scalar", GX[par])
            dma("scalar", SCD, dbg["xe0"], xeT[par][:])
        if e_ == 0 and "xb" in dbg:
            wait("scalar", XBS, XBS_ALL)
            dma("scalar", SCD, dbg["xb"], xb_hbm)
        iwg, iwu, iwd = 3 * e_ + 3, 3 * e_ + 4, 3 * e_ + 5
        gv_ = wview(iwg, "ff")
        uv_ = wview(iwu, "ff")
        xev = xeT[par].ap().rearrange("p (c n) -> p c n", n=RL)
        wait("tensor", GX[par])
        wait("tensor", WLD[wld_mark[iwg][0]], wld_mark[iwg][1])
        wait("tensor", WLD[wld_mark[iwu][0]], wld_mark[iwu][1])
        for fc in range(FFC):
            gb = ("A", fc % 2)
            ub = ("A", 2 + fc % 2)
            bwait("tensor", gb)
            bwait("tensor", ub)
            gbs, ubs = 512 * gb[1], 512 * ub[1]
            for kc in range(DMC):
                fn = lambda h, fc=fc, kc=kc, gbs=gbs, gv_=gv_, xev=xev: h.matmul(
                    ps_A[:, gbs:gbs + RL], gv_[:, kc, 128 * fc:128 * (fc + 1)],
                    xev[:, kc, :], start=(kc == 0), stop=(kc == DMC - 1))
                if fc == FFC - 1 and kc == DMC - 1:
                    finc("tensor", WFR[iwg % NWB], fn)
                else:
                    rawi("tensor", fn)
            for kc in range(DMC):
                fn = lambda h, fc=fc, kc=kc, ubs=ubs, uv_=uv_, xev=xev: h.matmul(
                    ps_A[:, ubs:ubs + RL], uv_[:, kc, 128 * fc:128 * (fc + 1)],
                    xev[:, kc, :], start=(kc == 0), stop=(kc == DMC - 1))
                if kc == DMC - 1:
                    if fc == FFC - 1:
                        finc("tensor", WFR[iwu % NWB], fn)
                    else:
                        finc("tensor", TC, fn)
                else:
                    rawi("tensor", fn)
            if fc == FFC - 1:
                wait("scalar", WFR[iwu % NWB])
            else:
                wait("scalar", TC)
            wait("scalar", VC)
            s_op(lambda h, gbs=gbs: h.activation(gtl[:, 0:RL], ps_A[:, gbs:gbs + RL], ACT.Silu))
            bset(gb, SCF)
            wait("vector", SCF)
            wait("vector", VC)
            finc("vector", VC, lambda h, fc=fc, ubs=ubs: h.tensor_tensor(
                aev[:, fc, :], gtl[:, 0:RL], ps_A[:, ubs:ubs + RL], ALU.mult))
            bset(ub, VC)
        if e_ == 0 and "ae0" in dbg:
            wait("scalar", VC)
            dma("scalar", SCD, dbg["ae0"], a_e[:])
        wrelease(iwg)
        wrelease(iwu)
        if e_ + 1 < E:
            wload()
            wload()
            wload()
        dv_ = wview(iwd, "dm")
        wait("tensor", WLD[wld_mark[iwd][0]], wld_mark[iwd][1])
        wait("tensor", VC)
        if e_ >= 1:
            wait("scalar", EOS)
        for rc in range(RLC):
            for n2 in range(2):
                bank = ("B", (rc * 2 + n2) % 2)
                bwait("tensor", bank)
                bs = 512 * bank[1]
                for fc in range(FFC):
                    fn = lambda h, rc=rc, n2=n2, fc=fc, bs=bs, dv_=dv_: h.matmul(
                        ps_B[:, bs:bs + 512], aev[:, fc, 128 * rc:128 * (rc + 1)],
                        dv_[:, fc, 512 * n2:512 * (n2 + 1)],
                        start=(fc == 0), stop=(fc == FFC - 1))
                    if fc == 0 and rc == 0 and n2 == 0:
                        finc("tensor", GXF[par], fn)
                    elif fc == FFC - 1:
                        if rc == RLC - 1 and n2 == 1:
                            finc("tensor", WFR[iwd % NWB], fn)
                        else:
                            finc("tensor", TC, fn)
                    else:
                        rawi("tensor", fn)
                if rc == RLC - 1 and n2 == 1:
                    wait("scalar", WFR[iwd % NWB])
                else:
                    wait("scalar", TC)
                s_op(lambda h, rc=rc, n2=n2, bs=bs: h.copy(
                    eo_t.ap().rearrange("p (r m) -> p r m", m=DM)[:, rc, 512 * n2:512 * (n2 + 1)],
                    ps_B[:, bs:bs + 512]))
                bset(bank, SCF)
        wait("scalar", SCF)
        dma("scalar", EOS, eo_hbm[RL * e_:RL * (e_ + 1), :].rearrange("(r p) m -> p r m", p=128),
            eo_t.ap().rearrange("p (r m) -> p r m", m=DM))
        wrelease(iwd)
        gxf_mark[par] = GXF[par].n
    EOS_ALL = EOS.n

    ctxB3a.close()
    ctxB1.close()
    ctxB0.close()
    ctxB4 = ExitStack()
    enB4 = ctxB4.enter_context
    g0b = enB4(nc.sbuf_tensor("g0b", [128, TV * DM], BF16))
    g1b = enB4(nc.sbuf_tensor("g1b", [128, TV * DM], BF16))
    sfull = enB4(nc.sbuf_tensor("sfull", [128, TV * DM], BF16))
    shfull = enB4(nc.sbuf_tensor("shfull", [128, TV * DM], BF16))
    ot2 = enB4(nc.sbuf_tensor("ot2", [128, DM], F32))
    xnb = [enB4(nc.sbuf_tensor(f"xnb{i}", [128, DM], BF16)) for i in range(2)]
    wait("gpsimd", EOS, EOS_ALL)
    wait("gpsimd", SCD, INIT_ALL)
    wait("gpsimd", STX, RWRAP_ALL)
    wait("gpsimd", SST, SST_ALL)
    wait("gpsimd", SHST, SHST_ALL)
    for (buf, wtab) in ((g0b, r0_w), (g1b, r1_w)):
        off = 0
        for n_ in (768, 768, 512):
            finc("gpsimd", CG, lambda h, buf=buf, wtab=wtab, off=off, n_=n_: h.dma_gather(
                buf.ap().rearrange("p (v m) -> p v m", m=DM)[:, off // 128:(off + n_) // 128, :],
                eo_hbm, wtab[:, off // 16:(off + n_) // 16], n_, n_, DM,
                transpose=False), amt=16)
            off += n_
    CG_ALL = CG.n

    wait("sync", SST, SST_ALL)
    wait("sync", SHST, SHST_ALL)
    wait("sync", EOS, EOS_ALL)
    for i_ in range(NWB):
        wait("sync", WFR[i_])
    wait("sync", TC)
    wait("sync", VC)
    wait("sync", SCF)
    dma("sync", PRE, sfull.ap().rearrange("p (v m) -> p v m", m=DM),
        s_hbm.rearrange("(v p) m -> p v m", p=128))
    dma("sync", PRE, shfull.ap().rearrange("p (v m) -> p v m", m=DM),
        sh_hbm.rearrange("(v p) m -> p v m", p=128))
    if "s" in dbg:
        wait("scalar", SST, SST_ALL)
        dma("scalar", SCD, dbg["s"], s_hbm)
    if "sh" in dbg:
        wait("scalar", SHST, SHST_ALL)
        dma("scalar", SCD, dbg["sh"], sh_hbm)
    if "eo" in dbg:
        wait("scalar", EOS, EOS_ALL)
        dma("scalar", SCD, dbg["eo"], eo_hbm)
    if "g0" in dbg:
        wait("scalar", CG, CG_ALL)
        dma("scalar", SCD, dbg["g0"], g0b[:])
        dma("scalar", SCD, dbg["g1"], g1b[:])
    sfv = sfull.ap().rearrange("p (v m) -> p v m", m=DM)
    shv = shfull.ap().rearrange("p (v m) -> p v m", m=DM)
    g0v = g0b.ap().rearrange("p (v m) -> p v m", m=DM)
    g1v = g1b.ap().rearrange("p (v m) -> p v m", m=DM)
    otl = [otile, ot2]
    OUTP = [OUTS, OUT2]
    outs_mark = [0, 0]
    wait("sync", XBS, XBS_ALL)
    for vv in range(TV):
        par = vv % 2
        wait("sync", XLF[par])
        dma("sync", XL[par], xtile[par][:], x_own[128 * vv:128 * (vv + 1), :])
        wait("sync", XNF[par])
        dma("sync", XNL[par], xnb[par][:], xb_hbm[128 * vv:128 * (vv + 1), :])
        wait("vector", XL[par])
        wait("vector", XNL[par])
        wait("vector", CG, CG_ALL)
        wait("vector", PRE)
        wait("vector", VC)
        finc("vector", VC, lambda h, vv=vv: h.tensor_scalar(
            dtile[:], g0v[:, vv, :], w0_pp[:, vv:vv + 1], None, ALU.mult))
        v_op(lambda h, vv=vv: h.tensor_scalar(ttile[:], g1v[:, vv, :], w1_pp[:, vv:vv + 1], None, ALU.mult))
        v_op(lambda h: h.tensor_tensor(dtile[:], dtile[:], ttile[:], ALU.add))
        wait("vector", VC)
        finc("vector", XNF[par], lambda h, par=par: h.tensor_tensor(
            dtile[:], dtile[:], xnb[par][:], ALU.add))
        wait("vector", XNF[par])
        v_op(lambda h, vv=vv: h.tensor_tensor(dtile[:], dtile[:], shv[:, vv, :], ALU.add))
        if "gs" in dbg:
            wait("scalar", VC)
            dma("scalar", SCD, dbg["gs"].rearrange("p (v m) -> p v m", m=DM)[:, vv, :], dtile[:])
        wait("scalar", VC)
        wait("scalar", OUTP[par], outs_mark[par])
        s_op(lambda h, par=par: h.activation(otl[par][:], dtile[:], ACT.Silu))
        wait("vector", SCF)
        v_op(lambda h, par=par, vv=vv: h.tensor_tensor(otl[par][:], otl[par][:], sfv[:, vv, :], ALU.mult))
        wait("vector", VC)
        finc("vector", XLF[par], lambda h, par=par: h.tensor_tensor(
            otl[par][:], otl[par][:], xtile[par][:], ALU.add))
        wait("scalar", XLF[par])
        dma("scalar", OUTP[par], out_hbm[128 * vv:128 * (vv + 1), :], otl[par][:])
        outs_mark[par] = OUTP[par].n
    wait("scalar", OUTS)
    wait("scalar", OUT2)

    block = en(nc.Block())
    P.replay(block)
    ctxB4.close()
    ctxT.close()
    ctx.close()
    return nc


_NC_CACHE = {}


def _get_nc(debug=()):
    key = tuple(debug)
    if key not in _NC_CACHE:
        nc = build_nc(debug=key)
        lower_extended_insts(nc)
        _NC_CACHE[key] = nc
    return _NC_CACHE[key]


def kernel(L0_attn_x, L0_attn_W_in, L0_attn_b_in, L0_attn_W_g, L0_attn_b_g,
           L0_attn_W_out, L0_attn_b_out, L0_ffn_router_w, L0_ffn_router_b,
           L0_ffn_w_gate, L0_ffn_w_up, L0_ffn_w_down, L0_ffn_sw_gate,
           L0_ffn_sw_up, L0_ffn_sw_down):
    x = np.ascontiguousarray(np.asarray(L0_attn_x, dtype=np.float32).reshape(NTOK, DM))
    xTf = np.ascontiguousarray(x.T)
    W_in = np.asarray(L0_attn_W_in, np.float32)
    consts = _host_constants()
    wsc = np.ascontiguousarray(np.concatenate(
        [np.asarray(L0_ffn_router_w, np.float32),
         W_in[:, DM:DM + 8],
         np.full((DM, 1), 1.0 / DM, np.float32)], axis=1))
    shared_map = {
        "xT": xTf,
        "w_small_c": wsc,
        "w_inT": np.ascontiguousarray(W_in.T),
        "w_g_in": np.ascontiguousarray(np.asarray(L0_attn_W_g, np.float32)),
        "b_in_col": np.ascontiguousarray(np.asarray(L0_attn_b_in, np.float32)[:, None]),
        "b_g8": np.ascontiguousarray(np.asarray(L0_attn_b_g, np.float32)[:, None]),
        "b_si32": np.ascontiguousarray(np.tile(np.asarray(L0_attn_b_in, np.float32)[DM:DM + 8], 4)[:, None]),
        "rb128": np.ascontiguousarray(np.tile(np.asarray(L0_ffn_router_b, np.float32), 8)[:, None]),
        "w_in_m": np.ascontiguousarray(W_in[:, 0:DM]),
        "w_out_in": np.ascontiguousarray(np.asarray(L0_attn_W_out, np.float32)),
        "b_out_row": np.ascontiguousarray(np.asarray(L0_attn_b_out, np.float32)[None, :]),
        "sw_gate": np.ascontiguousarray(np.asarray(L0_ffn_sw_gate, np.float32)),
        "sw_up": np.ascontiguousarray(np.asarray(L0_ffn_sw_up, np.float32)),
        "sw_down": np.ascontiguousarray(np.asarray(L0_ffn_sw_down, np.float32)),
        "w_gate": np.ascontiguousarray(np.asarray(L0_ffn_w_gate, np.float32)),
        "w_up": np.ascontiguousarray(np.asarray(L0_ffn_w_up, np.float32)),
        "w_down": np.ascontiguousarray(np.asarray(L0_ffn_w_down, np.float32)),
        "ones128": consts["ones128"],
        "E16dup": consts["E16dup"],
        "E8dup2": consts["E8dup2"],
        "P8": consts["P8"],
        "T16": consts["T16"],
        "BD16": consts["BD16"],
        "iotaE": consts["iotaE"],
        "slot1c": consts["slot1c"],
        "rowc": consts["rowc"],
    }
    in_maps = []
    for c in range(NCORE):
        sown, sst = _per_core_consts(c)
        m = dict(shared_map)
        m["xT_own"] = np.ascontiguousarray(xTf[:, TPC * c:TPC * (c + 1)])
        m["x_own"] = np.ascontiguousarray(x[TPC * c:TPC * (c + 1), :])
        m["selOwn"] = sown
        m["selST"] = sst
        in_maps.append(m)
    nc = _get_nc(DEBUG)
    res = run_bass_kernel_spmd(nc, in_maps, core_ids=list(range(NCORE)))
    out = np.concatenate([res.results[c]["out_own"] for c in range(NCORE)], axis=0)
    if DEBUG:
        kernel._dbg = [{k: v for k, v in r.items() if k.startswith("dbg_")}
                       for r in res.results]
    return out.reshape(4, 4096, DM)


# version 11
# speedup vs baseline: 1.0479x; 1.0119x over previous
import numpy as np
from contextlib import ExitStack

import concourse.bass as bass
import concourse.mybir as mybir
from concourse.bass_utils import run_bass_kernel_spmd
from concourse.library_config import mlp as _MLP_LIB
from concourse.library_overlay import lower_extended_insts

F32 = mybir.dt.float32
BF16 = mybir.dt.bfloat16
I16 = mybir.dt.int16
ALU = mybir.AluOpType
ACT = mybir.ActivationFunctionType

NCORE = 8
DM = 1024
DMC = 8
NTOK = 16384
TPC = 2048
TV = 16
NG = 8
E = 16
FF = 1408
FFC = 11
RL = 384
RLC = 3
NR = E * RL
NRV = NR // 128
NS = TPC * 2
NSV = NS // 128
CAP = 2560
TR1 = NR
TR2 = NS
EOTR = NR
EPS_LN = 1e-5

DEBUG = ()


def _host_constants():
    c = {}
    c["ones128"] = np.ones((1, 128), np.float32)
    e16 = np.zeros((8, 128), np.float32)
    for g in range(8):
        e16[g, 16 * g:16 * g + 16] = 1.0
    c["E16dup"] = e16
    e8 = np.zeros((8, 128), np.float32)
    for g in range(8):
        e8[g, 8 * g:8 * g + 8] = 1.0
        e8[g, 64 + 8 * g:64 + 8 * g + 8] = 1.0
    c["E8dup2"] = e8
    p8 = np.zeros((128, 128), np.float32)
    for g in range(8):
        for gp in range(g):
            for ee in range(16):
                p8[16 * gp + ee, 16 * g + ee] = 1.0
    c["P8"] = p8
    t16 = np.zeros((128, 128), np.float32)
    for g in range(8):
        for e1 in range(16):
            for e2 in range(e1, 16):
                t16[16 * g + e1, 16 * g + e2] = 1.0
    c["T16"] = t16
    bd = np.zeros((128, 128), np.float32)
    for g in range(8):
        bd[16 * g:16 * g + 16, 16 * g:16 * g + 16] = 1.0
    c["BD16"] = bd
    c["iotaE"] = (np.arange(128) % 16).astype(np.float32)[:, None]
    s1 = np.zeros((128, NSV, 64), np.float32)
    for m in range(NS):
        k, n = m // TPC, m % TPC
        i = (m % (NS // 16)) * 16 + m // (NS // 16)
        s1[i % 128, i // 128, 0] = n + 1
        s1[i % 128, i // 128, 1] = k
    c["slot1c"] = np.ascontiguousarray(s1.reshape(128, NSV * 64))
    rc = np.zeros((128, NRV, 64), np.float32)
    for m in range(NR):
        rc[m % 128, m // 128, 0] = m + 1
    c["rowc"] = np.ascontiguousarray(rc.reshape(128, NRV * 64))
    return c


def _per_core_consts(core):
    b_i, h_i = core // 2, core % 2
    sown = np.zeros((128, 16), np.float32)
    for j in range(16):
        sown[16 * core + j, j] = 1.0
    sst = np.zeros((64, 8), np.float32)
    for ch in range(8):
        sst[32 * h_i + 8 * b_i + ch, ch] = 1.0
    return sown, sst


class Sem:
    def __init__(self, en, nc, name):
        self.h = en(nc.semaphore(name))
        self.n = 0


class Prog:
    ENGS = ("sync", "scalar", "vector", "tensor", "gpsimd")

    def __init__(self):
        self.streams = {e: [] for e in self.ENGS}

    def emit(self, eng, fn):
        self.streams[eng].append(fn)

    def replay(self, block):
        for eng in self.ENGS:
            fns = self.streams[eng]

            def body(h, fns=fns):
                for f in fns:
                    f(h)

            getattr(block, eng)(body)


def build_nc(debug=()):
    nc = bass.Bass("TRN2", target_bir_lowering=False)

    def din(name, shape, dt=F32):
        return nc.dram_tensor(name, shape, dt, kind="ExternalInput").ap()

    xT = din("xT", [DM, NTOK])
    xT_own = din("xT_own", [DM, TPC])
    x_own = din("x_own", [TPC, DM])
    w_small_c = din("w_small_c", [DM, 25])
    w_inT = din("w_inT", [DM + 8, DM])
    w_g_in = din("w_g_in", [DM + 8, 8])
    b_in_col = din("b_in_col", [DM + 8, 1])
    b_g8 = din("b_g8", [8, 1])
    b_si32 = din("b_si32", [32, 1])
    rb128 = din("rb128", [128, 1])
    w_in_m = din("w_in_m", [DM, DM])
    w_out_in = din("w_out_in", [DM + 8, DM])
    b_out_row = din("b_out_row", [1, DM])
    sw_gate = din("sw_gate", [DM, FF])
    sw_up = din("sw_up", [DM, FF])
    sw_down = din("sw_down", [FF, DM])
    w_gate = din("w_gate", [E, DM, FF])
    w_up = din("w_up", [E, DM, FF])
    w_down = din("w_down", [E, FF, DM])
    ones128_in = din("ones128", [1, 128])
    E16_in = din("E16dup", [8, 128])
    E8_in = din("E8dup2", [8, 128])
    P8_in = din("P8", [128, 128])
    T16_in = din("T16", [128, 128])
    BD16_in = din("BD16", [128, 128])
    iota_in = din("iotaE", [128, 1])
    slot1c_in = din("slot1c", [128, NSV * 64])
    rowc_in = din("rowc", [128, NRV * 64])
    selOwn_in = din("selOwn", [128, 16])
    selST_in = din("selST", [64, 8])

    out_hbm = nc.dram_tensor("out_own", [TPC, DM], F32, kind="ExternalOutput").ap()

    xb_hbm = nc.dram_tensor("xb_hbm", [TPC, DM], BF16, kind="Internal").ap()
    s_hbm = nc.dram_tensor("s_hbm", [TPC, DM], BF16, kind="Internal").ap()
    sh_hbm = nc.dram_tensor("sh_hbm", [TPC, DM], BF16, kind="Internal").ap()
    inv1_hbm = nc.dram_tensor("inv1_hbm", [NR + 1, 64], F32, kind="Internal").ap()
    tb2_hbm = nc.dram_tensor("tb2_hbm", [NS + 1, 64], F32, kind="Internal").ap()
    eo_hbm = nc.dram_tensor("eo_hbm", [NR + 1, DM], BF16, kind="Internal").ap()
    NPRE = 6
    wbf_hbm = nc.dram_tensor("wbf_hbm", [NPRE, 128, 11264], BF16, kind="Internal").ap()

    dbg_shapes = {"l": [128, TPC], "w1": [128, TPC], "w2": [128, TPC],
                  "dr1": [128, TPC], "dr2": [128, TPC], "st": [32, 4096],
                  "cumG": [128, TPC], "i1": [128, TPC], "i2": [128, TPC],
                  "stT": [8, TPC], "gsig": [32, 4096], "sib": [32, 4096],
                  "idxf": [128, NRV], "kf": [128, NRV], "rid": [128, NSV],
                  "moe": [128, TV * DM], "gs": [128, TV * DM],
                  "s": [TPC, DM], "sh": [TPC, DM], "g0": [128, TV * DM],
                  "g1": [128, TV * DM], "eo": [NR + 1, DM],
                  "inv1": [NR + 1, 64], "drw": [128, NS // 16],
                  "xe0": [128, DMC * RL], "xb": [TPC, DM], "ae0": [128, FFC * RL],
                  "wg0": [128, 11264], "wu0": [128, 11264]}
    dbg = {nm: nc.dram_tensor("dbg_" + nm, dbg_shapes[nm],
                              BF16 if nm in ("stT", "s", "sh", "g0", "g1", "eo", "xe0", "xb", "ae0", "wg0", "wu0") else
                              (I16 if nm == "drw" else F32),
                              kind="ExternalOutput").ap()
           for nm in debug}

    ctx = ExitStack()
    en = ctx.enter_context

    def SB(nm, shp, dt=F32):
        return en(nc.sbuf_tensor(nm, shp, dt))

    wsmall = SB("wsmall", [128, DMC * 33], BF16)
    ones_bf = SB("ones_bf", [128, 1], BF16)
    ones1f = SB("ones1f", [1, 128], F32)
    ones33 = SB("ones33", [65, 128], F32)
    e16d = SB("e16d", [8, 128], F32)
    e8d = SB("e8d", [8, 128], F32)
    p8m = SB("p8m", [128, 128], F32)
    t16m = SB("t16m", [128, 128], F32)
    bd16m = SB("bd16m", [128, 128], F32)
    iota_pp = SB("iota_pp", [128, 1], F32)
    selown = SB("selown", [128, 16], F32)
    selst = SB("selst", [64, 8], F32)
    rb_pp = SB("rb_pp", [128, 1], F32)
    bsi_pp = SB("bsi_pp", [32, 1], F32)
    bg8s = SB("bg8s", [8, 1], F32)
    bgeff = SB("bgeff", [8, 1], F32)
    bgeff32 = SB("bgeff32", [32, 1], F32)
    scol_pp = SB("scol_pp", [128, 1], F32)
    sg_pp = SB("sg_pp", [128, 1], F32)
    bic = SB("bic", [128, DMC + 1], F32)
    bicb = SB("bicb", [128, DMC + 1], BF16)
    bout_sb = SB("bout_sb", [1, DM], F32)
    zsrc = SB("zsrc", [128, 512], F32)
    zsrc_bf = SB("zsrc_bf", [1, DM], BF16)
    bb_out = SB("bb_out", [128, DM], BF16)
    scol33 = SB("scol33", [33, 1], F32)
    gpre_pp = SB("gpre_pp", [128, 1], F32)
    epsc = SB("epsc", [128, 1], F32)
    mu_pp = SB("mu_pp", [128, TV], F32)
    is_pp = SB("is_pp", [128, TV], F32)
    w0_pp = SB("w0_pp", [128, TV], F32)
    w1_pp = SB("w1_pp", [128, TV], F32)
    stT = SB("stT", [8, TPC], BF16)
    dr_w = SB("dr_w", [128, NS // 16], I16)
    gidx_w = SB("gidx_w", [128, NR // 16], I16)
    s2_w = SB("s2_w", [128, NR // 16], I16)
    r0_w = SB("r0_w", [128, TPC // 16], I16)
    r1_w = SB("r1_w", [128, TPC // 16], I16)
    ownrowA = SB("ownrowA", [65, TPC], F32)
    ownrowB = SB("ownrowB", [65, TPC], F32)
    dr_i16 = SB("dr_i16", [33, TPC], I16)

    ctxA = ExitStack()
    enA = ctxA.enter_context

    def SA(nm, shp, dt=F32):
        return enA(nc.sbuf_tensor(nm, shp, dt))

    q_l = SA("q_l", [128, TPC], F32)
    q_rz = SA("q_rz", [128, TPC], F32)
    q_o1 = SA("q_o1", [128, TPC], F32)
    q_o2 = SA("q_o2", [128, TPC], F32)
    Lst = SA("Lst", [128, TPC], F32)
    q_cg = SA("q_cg", [128, TPC], F32)
    GSst = SA("GSst", [128, TPC], F32)
    q_e = SA("q_e", [128, TPC], F32)
    q_w1 = SA("q_w1", [128, TPC], F32)
    q_w2 = SA("q_w2", [128, TPC], F32)
    q_dr1 = SA("q_dr1", [128, TPC], F32)
    q_dr2 = SA("q_dr2", [128, TPC], F32)
    q_ex = SA("q_ex", [128, TPC], F32)
    q_t1 = SA("q_t1", [128, TPC], F32)
    q_t2 = SA("q_t2", [128, TPC], F32)
    q_bc1 = SA("q_bc1", [128, TPC], F32)
    q_bc2 = SA("q_bc2", [128, TPC], F32)
    mu8 = SA("mu8", [8, TPC], F32)
    is8 = SA("is8", [8, TPC], F32)
    winT_tl_t = SA("winT_tl_t", [8, DM], BF16)
    wgm_st = SA("wgm_st", [128, DMC * 8], BF16)
    wgm_tl_t = SA("wgm_tl_t", [8, 8], BF16)
    q_i1 = q_l
    q_i2 = Lst
    raws = q_bc1
    addr = lambda t: nc.lookup_mloc(t).addr
    ss8 = nc.alloc_sbuf_tensor_at("ss8", [8, TPC], F32, offset=addr(q_ex))
    sc8a = nc.alloc_sbuf_tensor_at("sc8a", [8, TPC], F32, offset=addr(q_e))
    assert addr(q_t2) == addr(q_t1) + 8192
    winT_st = nc.alloc_sbuf_tensor_at("winT_st", [128, DMC * DM], BF16, offset=addr(q_t1))
    xtb = [nc.alloc_sbuf_tensor_at(f"xtb{i}", [128, TPC], BF16,
                                   offset=addr(q_t1) + 4096 * i) for i in range(2)]
    xsq = [nc.alloc_sbuf_tensor_at(f"xsq{i}", [128, TPC], BF16,
                                   offset=addr(q_t1) + 8192 + 4096 * i) for i in range(2)]
    assert addr(q_rz) == addr(q_l) + 8192 and addr(q_o2) == addr(q_o1) + 8192
    assert addr(q_cg) == addr(Lst) + 8192
    sic = nc.alloc_sbuf_tensor_at("sic", [32, 4096], F32, offset=addr(q_l))
    gsc = nc.alloc_sbuf_tensor_at("gsc", [32, 4096], F32, offset=addr(q_o1))
    gsig = nc.alloc_sbuf_tensor_at("gsig", [32, 4096], F32, offset=addr(Lst))
    st64 = nc.alloc_sbuf_tensor_at("st64", [64, TPC], F32, offset=addr(GSst))
    ps_A = en(nc.psum_tensor("ps_A", [128, TPC], F32))
    ps_B = en(nc.psum_tensor("ps_B", [128, TPC], F32))

    sems = []

    def sem(name):
        s = Sem(en, nc, name)
        sems.append(s)
        return s

    LD = sem("LD")
    LDS = sem("LDS")
    TC = sem("TC")
    VC = sem("VC")
    SCF = sem("SCF")
    STX = sem("STX")
    SCD = sem("SCD")
    GPD = sem("GPD")
    XTs = [sem("XT0"), sem("XT1")]
    XTFs = [sem("XTF0"), sem("XTF1")]
    XSQs = [sem("XSQ0"), sem("XSQ1")]
    WLD = [sem(f"WLD{i}") for i in range(5)]
    WFR = [sem(f"WFR{i}") for i in range(5)]
    GX = [sem("GX0"), sem("GX1")]
    GXF = [sem("GXF0"), sem("GXF1")]
    XL = [sem("XL0"), sem("XL1")]
    XLF = [sem("XLF0"), sem("XLF1")]
    XOT = [sem("XOT0"), sem("XOT1")]
    XOF = [sem("XOF0"), sem("XOF1")]
    XNL = [sem("XNL0"), sem("XNL1")]
    XNF = [sem("XNF0"), sem("XNF1")]
    OUT2 = sem("OUT2")
    PCW = sem("PCW")
    XBS = sem("XBS")
    SST = sem("SST")
    SHST = sem("SHST")
    EOS = sem("EOS")
    OUTS = sem("OUTS")
    MOE = sem("MOE")
    CG = sem("CG")
    PRE = sem("PRE")

    P = Prog()

    def emit(eng, fn):
        P.emit(eng, fn)

    def wait(eng, s, val=None):
        v = s.n if val is None else val
        if v > 0:
            emit(eng, lambda h: h.wait_ge(s.h, v))

    def dma(eng, s, out, in_, slow=False):
        s.n += 16
        emit(eng, lambda h: h.dma_start(out, in_, allow_slow_non_contiguous=slow)
             .then_inc(s.h, 16))

    def finc(eng, s, fn, amt=1):
        s.n += amt
        emit(eng, lambda h: fn(h).then_inc(s.h, amt))

    def rawi(eng, fn):
        emit(eng, lambda h: fn(h))

    def v_op(fn):
        wait("vector", VC)
        finc("vector", VC, fn)

    def s_op(fn):
        wait("scalar", SCF)
        finc("scalar", SCF, fn)

    def dbg_dump(nm, src):
        if nm in dbg:
            wait("scalar", VC)
            wait("scalar", SCF)
            dma("scalar", SCD, dbg[nm], src)

    bank_free = {}

    def bwait(eng, key):
        if key in bank_free:
            s, cnt = bank_free[key]
            wait(eng, s, cnt)

    def bset(key, s):
        bank_free[key] = (s, s.n)

    rawi("gpsimd", lambda h: h.load_library(_MLP_LIB))
    wsv = wsmall.ap().rearrange("p (kc c) -> p kc c", c=33)
    dma("gpsimd", LD, wsv[:, :, 0:25], w_small_c.rearrange("(kc p) c -> p kc c", p=128))
    winT_v = winT_st.ap().rearrange("p (kc m) -> p kc m", m=DM)
    dma("gpsimd", LD, winT_v[:], w_inT[0:DM, :].rearrange("(kc p) m -> p kc m", p=128))
    winT_tl = winT_tl_t[:]
    dma("gpsimd", LD, winT_tl, w_inT[DM:DM + 8, :])
    wgm_v = wgm_st.ap().rearrange("p (kc c) -> p kc c", c=8)
    dma("gpsimd", LD, wgm_v[:], w_g_in[0:DM, :].rearrange("(kc p) c -> p kc c", p=128))
    wgm_tl = wgm_tl_t[:]
    dma("gpsimd", LD, wgm_tl, w_g_in[DM:DM + 8, :])
    for dst, src in [(ones1f[:], ones128_in), (e16d[:], E16_in), (e8d[:], E8_in),
                     (p8m[:], P8_in), (t16m[:], T16_in), (bd16m[:], BD16_in),
                     (iota_pp[:], iota_in), (selown[:], selOwn_in), (selst[:], selST_in),
                     (rb_pp[:], rb128), (bsi_pp[:], b_si32), (bg8s[:], b_g8)]:
        dma("sync", LDS, dst, src)
    bicv = bic.ap().rearrange("p (c o) -> p c o", o=1)
    dma("sync", LDS, bicv[:, 0:DMC, :], b_in_col[0:DM, :].rearrange("(c p) o -> p c o", p=128), slow=True)
    dma("sync", LDS, bic[0:8, DMC:DMC + 1], b_in_col[DM:DM + 8, :])
    dma("sync", LDS, bout_sb[:], b_out_row)
    v_op(lambda h: h.memset(zsrc[:], 0.0))
    v_op(lambda h: h.memset(zsrc_bf[:], 0.0))
    v_op(lambda h: h.memset(ones_bf[:], 1.0))
    v_op(lambda h: h.memset(epsc[:], float(EPS_LN)))
    v_op(lambda h: h.memset(ones33[:], 1.0))
    wait("vector", LD)
    wait("vector", LDS)
    v_op(lambda h: h.tensor_copy(bicb[:, 0:DMC], bic[:, 0:DMC]))
    v_op(lambda h: h.tensor_copy(bicb[0:8, DMC:DMC + 1], bic[0:8, DMC:DMC + 1]))

    wait("tensor", LD)
    wait("tensor", LDS)
    wait("tensor", VC)
    for mc in range(DMC):
        bwait("tensor", ("A", 0))
        for kc in range(DMC):
            rawi("tensor", lambda h, mc=mc, kc=kc: h.matmul(
                ps_A[:, 0:8], winT_v[:, kc, 128 * mc:128 * (mc + 1)], wgm_v[:, kc, :],
                start=(kc == 0), stop=False))
        finc("tensor", TC, lambda h, mc=mc: h.matmul(
            ps_A[:, 0:8], winT_tl_t[:, 128 * mc:128 * (mc + 1)], wgm_tl,
            start=False, stop=True))
        wait("scalar", TC)
        s_op(lambda h, mc=mc: h.copy(wsv[:, mc, 25:33], ps_A[:, 0:8]))
        bset(("A", 0), SCF)
    for kc in range(DMC):
        rawi("tensor", lambda h, kc=kc: h.matmul(
            ps_B[0:8, 0:1], wgm_v[:, kc, :],
            bicb.ap().rearrange("p (c o) -> p c o", o=1)[:, kc, :], start=(kc == 0), stop=False))
    finc("tensor", TC, lambda h: h.matmul(
        ps_B[0:8, 0:1], wgm_tl, bicb[0:8, DMC:DMC + 1], start=False, stop=True))
    wait("vector", TC)
    v_op(lambda h: h.tensor_tensor(bgeff[:], ps_B[0:8, 0:1], bg8s[:], ALU.add))
    wait("sync", VC)
    for r in range(4):
        dma("sync", STX, bgeff32[8 * r:8 * r + 8, :], bgeff[:])
    for nn in range(2):
        wait("tensor", VC)
        finc("tensor", TC, lambda h, nn=nn: h.matmul(
            ps_B[:, 512:1024], ones1f[:], bout_sb[:, 512 * nn:512 * (nn + 1)],
            start=True, stop=True))
        wait("vector", TC)
        v_op(lambda h, nn=nn: h.tensor_copy(bb_out[:, 512 * nn:512 * (nn + 1)],
                                            ps_B[:, 512:1024]))
    wait("tensor", SCF)
    wait("tensor", VC)
    for kc in range(DMC):
        fn = lambda h, kc=kc: h.matmul(ps_B[0:33, 1024:1025], wsv[:, kc, :], ones_bf[:],
                                       start=(kc == 0), stop=(kc == DMC - 1))
        finc("tensor", TC, fn) if kc == DMC - 1 else rawi("tensor", fn)
    wait("scalar", TC)
    s_op(lambda h: h.copy(scol33[:], ps_B[0:33, 1024:1025]))
    wait("sync", SCF)
    for g in range(8):
        dma("sync", STX, scol_pp[16 * g:16 * g + 16, :], scol33[0:16, :])
        dma("sync", STX, sg_pp[8 * g:8 * g + 8, :], scol33[16:24, :])
        dma("sync", STX, sg_pp[64 + 8 * g:64 + 8 * g + 8, :], scol33[25:33, :])

    wstream = [("swg", sw_gate, "ff"), ("swu", sw_up, "ff"), ("swd", sw_down, "dm")]
    for e_ in range(E):
        wstream += [(f"wg{e_}", w_gate[e_], "ff"), (f"wu{e_}", w_up[e_], "ff"),
                    (f"wd{e_}", w_down[e_], "dm")]
    pre_state = {"i": 0}

    def precast_one():
        i = pre_state["i"]
        if i >= NPRE:
            return
        nm, src_, kind = wstream[i]
        m2 = DM if kind == "dm" else FF
        dma("gpsimd", PCW, wbf_hbm[i].rearrange("p (c m2) -> p c m2", m2=m2),
            src_.rearrange("(c p) m2 -> p c m2", p=128))
        pre_state["i"] += 1

    sc_after_g = {}
    stx_after_g = {}
    wait("gpsimd", TC)
    for gg in range(NG):
        for kc in range(DMC):
            it = gg * DMC + kc
            par = it % 2
            if it >= 2:
                wait("gpsimd", XTFs[par])
                wait("gpsimd", XSQs[par])
            dma("gpsimd", XTs[par], xtb[par][:],
                xT[128 * kc:128 * (kc + 1), TPC * gg:TPC * (gg + 1)])
            wait("vector", XTs[par])
            if it >= 2:
                wait("vector", XTFs[par])
            finc("vector", XSQs[par],
                 lambda h, par=par: h.tensor_tensor(xsq[par][:], xtb[par][:], xtb[par][:], ALU.mult))
            wait("tensor", XTs[par])
            wait("tensor", XSQs[par])
            if kc == 0:
                bwait("tensor", ("A", "P1"))
                bwait("tensor", ("B", "P1"))
            for n4 in range(4):
                rawi("tensor", lambda h, kc=kc, par=par, n4=n4: h.matmul(
                    ps_A[0:33, 512 * n4:512 * (n4 + 1)], wsv[:, kc, :],
                    xtb[par][:, 512 * n4:512 * (n4 + 1)],
                    start=(kc == 0), stop=(kc == DMC - 1)))
            for n4 in range(4):
                fn = lambda h, kc=kc, par=par, n4=n4: h.matmul(
                    ps_B[64:65, 512 * n4:512 * (n4 + 1)], ones_bf[:],
                    xsq[par][:, 512 * n4:512 * (n4 + 1)],
                    start=(kc == 0), stop=(kc == DMC - 1))
                finc("tensor", XTFs[par], fn) if n4 == 3 else rawi("tensor", fn)
        lastpar = (gg * DMC + DMC - 1) % 2
        wait("scalar", XTFs[lastpar])
        if gg >= 1:
            wait("scalar", STX, stx_after_g[gg - 1])
        s_op(lambda h: h.copy(raws[0:33, :], ps_A[0:33, :]))
        s_op(lambda h: h.copy(raws[64:65, :], ps_B[64:65, 0:TPC]))
        sc_after_g[gg] = SCF.n
        bset(("A", "P1"), SCF)
        bset(("B", "P1"), SCF)
        wait("sync", SCF, sc_after_g[gg])
        dma("sync", STX, Lst[16 * gg:16 * gg + 16, :], raws[0:16, :])
        dma("sync", STX, GSst[8 * gg:8 * gg + 8, :], raws[16:24, :])
        dma("sync", STX, GSst[64 + 8 * gg:64 + 8 * gg + 8, :], raws[25:33, :])
        dma("sync", STX, mu8[gg:gg + 1, :], raws[24:25, :])
        dma("sync", STX, ss8[gg:gg + 1, :], raws[64:65, :])
        stx_after_g[gg] = STX.n

    while pre_state["i"] < NPRE:
        precast_one()
    PRE_ALL = PCW.n

    wait("vector", STX)
    v_op(lambda h: h.tensor_tensor(sc8a[:], mu8[:], mu8[:], ALU.mult))
    v_op(lambda h: h.tensor_scalar(is8[:], ss8[:], 1.0 / DM, None, ALU.mult))
    v_op(lambda h: h.tensor_tensor(is8[:], is8[:], sc8a[:], ALU.subtract))
    wait("scalar", VC)
    s_op(lambda h: h.activation(sc8a[:], is8[:], ACT.Sqrt, bias=epsc[0:8, :], scale=1.0))
    wait("vector", SCF)
    v_op(lambda h: h.reciprocal(is8[:], sc8a[:]))

    def mm128(dst_sb, lhs, src, psl=ps_A, rows=128):
        wait("tensor", VC)
        wait("tensor", SCF)
        for n4 in range(4):
            fn = lambda h, n4=n4: h.matmul(psl[0:rows, 512 * n4:512 * (n4 + 1)],
                                           lhs if isinstance(lhs, bass.AP) else lhs[:],
                                           src[:, 512 * n4:512 * (n4 + 1)],
                                           start=True, stop=True)
            finc("tensor", TC, fn) if n4 == 3 else rawi("tensor", fn)
        wait("scalar", TC)
        s_op(lambda h: h.copy(dst_sb[0:rows, 0:TPC] if rows != 128 else dst_sb[:],
                              psl[0:rows, 0:TPC]))

    mm128(q_bc1, e16d, mu8)
    mm128(q_bc2, e16d, is8)
    wait("vector", SCF)
    v_op(lambda h: h.tensor_scalar(q_l[:], q_bc1[:], scol_pp[:], None, ALU.mult))
    v_op(lambda h: h.tensor_tensor(q_l[:], Lst[:], q_l[:], ALU.subtract))
    v_op(lambda h: h.tensor_tensor(q_l[:], q_l[:], q_bc2[:], ALU.mult))
    v_op(lambda h: h.tensor_scalar(q_l[:], q_l[:], rb_pp[:], None, ALU.add))
    dbg_dump("l", q_l[:])

    shmask = lambda d: [(p // 16) * 16 + ((p % 16) + d) % 16 for p in range(32)]

    def tree_max(buf, tmp):
        for d in (8, 4, 2, 1):
            v_op(lambda h, d=d, buf=buf, tmp=tmp: h.stream_shuffle(tmp[:], buf[:], shmask(d)))
            v_op(lambda h, buf=buf, tmp=tmp: h.tensor_tensor(buf[:], buf[:], tmp[:], ALU.max))

    v_op(lambda h: h.tensor_copy(q_t1[:], q_l[:]))
    tree_max(q_t1, q_t2)
    v_op(lambda h: h.tensor_tensor(q_t2[:], q_l[:], q_t1[:], ALU.subtract))
    wait("scalar", VC)
    s_op(lambda h: h.activation(q_e[:], q_t2[:], ACT.Exp))
    wait("vector", SCF)
    mm128(q_rz, bd16m, q_e)
    wait("vector", SCF)
    v_op(lambda h: h.reciprocal(q_rz[:], q_rz[:]))
    v_op(lambda h: h.tensor_scalar(q_o1[:], q_e[:], 1.0, None, ALU.is_equal))
    mm128(q_t2, t16m, q_o1)
    wait("vector", SCF)
    v_op(lambda h: h.tensor_scalar(q_t2[:], q_t2[:], 1.0, None, ALU.is_equal))
    v_op(lambda h: h.tensor_tensor(q_o1[:], q_o1[:], q_t2[:], ALU.mult))
    v_op(lambda h: h.tensor_scalar(q_t2[:], q_o1[:], iota_pp[:], None, ALU.mult))
    mm128(q_i1, bd16m, q_t2)
    wait("vector", SCF)
    v_op(lambda h: h.tensor_tensor(q_t2[:], q_e[:], q_o1[:], ALU.mult))
    v_op(lambda h: h.tensor_tensor(q_t2[:], q_e[:], q_t2[:], ALU.subtract))
    v_op(lambda h: h.tensor_copy(q_w2[:], q_t2[:]))
    tree_max(q_w2, q_t1)
    v_op(lambda h: h.tensor_tensor(q_o2[:], q_t2[:], q_w2[:], ALU.is_equal))
    mm128(q_t1, t16m, q_o2)
    wait("vector", SCF)
    v_op(lambda h: h.tensor_scalar(q_t1[:], q_t1[:], 1.0, None, ALU.is_equal))
    v_op(lambda h: h.tensor_tensor(q_o2[:], q_o2[:], q_t1[:], ALU.mult))
    v_op(lambda h: h.tensor_scalar(q_t1[:], q_o2[:], iota_pp[:], None, ALU.mult))
    mm128(q_i2, bd16m, q_t1)
    dbg_dump("i1", q_i1[:])
    dbg_dump("i2", q_i2[:])
    wait("vector", SCF)
    v_op(lambda h: h.tensor_tensor(q_w2[:], q_w2[:], q_rz[:], ALU.mult))
    v_op(lambda h: h.tensor_tensor(q_t1[:], q_rz[:], q_w2[:], ALU.add))
    v_op(lambda h: h.tensor_scalar(q_t1[:], q_t1[:], 1e-9, None, ALU.add))
    v_op(lambda h: h.reciprocal(q_t1[:], q_t1[:]))
    v_op(lambda h: h.tensor_tensor(q_w1[:], q_rz[:], q_t1[:], ALU.mult))
    v_op(lambda h: h.tensor_tensor(q_w2[:], q_w2[:], q_t1[:], ALU.mult))
    v_op(lambda h: h.tensor_tensor(q_t1[:], q_o1[:], q_o2[:], ALU.add))
    v_op(lambda h: h.tensor_tensor_scan(q_ex[:], q_t1[:], q_t1[:], 0.0, ALU.add, ALU.bypass))
    v_op(lambda h: h.tensor_copy(q_t2[:, 0:1], q_ex[:, TPC - 1:TPC]))
    v_op(lambda h: h.tensor_tensor(q_ex[:], q_ex[:], q_t1[:], ALU.subtract))
    wait("tensor", VC)
    wait("tensor", SCF)
    finc("tensor", TC, lambda h: h.matmul(ps_B[:, 0:1], p8m[:], q_t2[:, 0:1],
                                          start=True, stop=True))
    wait("scalar", TC)
    s_op(lambda h: h.copy(gpre_pp[:], ps_B[:, 0:1]))
    wait("vector", SCF)
    v_op(lambda h: h.tensor_scalar(q_cg[:], q_ex[:], gpre_pp[:], None, ALU.add))
    dbg_dump("cumG", q_cg[:])

    def pos_k(oh, ii, wv, drv):
        v_op(lambda h: h.tensor_tensor(q_t1[:], oh[:], q_cg[:], ALU.mult))
        mm128(q_t2, bd16m, q_t1)
        wait("vector", SCF)
        v_op(lambda h: h.tensor_scalar(q_t2[:], q_t2[:], float(CAP), None, ALU.is_lt))
        v_op(lambda h: h.tensor_tensor(wv[:], wv[:], q_t2[:], ALU.mult))
        v_op(lambda h: h.tensor_tensor(q_t1[:], oh[:], q_ex[:], ALU.mult))
        mm128(drv, bd16m, q_t1)
        wait("vector", SCF)
        v_op(lambda h: h.tensor_scalar(q_t1[:], drv[:], float(RL), None, ALU.is_lt))
        v_op(lambda h: h.tensor_tensor(q_t1[:], q_t1[:], q_t2[:], ALU.mult))
        v_op(lambda h: h.tensor_scalar(q_t2[:], ii[:], float(RL), None, ALU.mult))
        v_op(lambda h: h.tensor_tensor(drv[:], q_t2[:], drv[:], ALU.add))
        v_op(lambda h: h.tensor_tensor(drv[:], drv[:], q_t1[:], ALU.mult))
        v_op(lambda h: h.tensor_scalar(q_t1[:], q_t1[:], -float(TR1), float(TR1), ALU.mult, ALU.add))
        v_op(lambda h: h.tensor_tensor(drv[:], drv[:], q_t1[:], ALU.add))

    pos_k(q_o1, q_i1, q_w1, q_dr1)
    pos_k(q_o2, q_i2, q_w2, q_dr2)
    dbg_dump("w1", q_w1[:])
    dbg_dump("w2", q_w2[:])
    dbg_dump("dr1", q_dr1[:])
    dbg_dump("dr2", q_dr2[:])

    mm128(q_t1, e8d, mu8)
    mm128(q_t2, e8d, is8)
    wait("vector", SCF)
    v_op(lambda h: h.tensor_scalar(q_e[:], q_t1[:], sg_pp[:], None, ALU.mult))
    v_op(lambda h: h.tensor_tensor(q_e[:], GSst[:], q_e[:], ALU.subtract))
    v_op(lambda h: h.tensor_tensor(q_e[:], q_e[:], q_t2[:], ALU.mult))
    wait("sync", VC)
    wait("sync", SCF)
    for bb_ in range(4):
        for hh in range(2):
            g_ = 2 * bb_ + hh
            dma("sync", STX, sic.ap()[8 * bb_:8 * bb_ + 8, 2048 * hh:2048 * (hh + 1)],
                q_e[8 * g_:8 * g_ + 8, :])
            dma("sync", STX, gsc.ap()[8 * bb_:8 * bb_ + 8, 2048 * hh:2048 * (hh + 1)],
                q_e[64 + 8 * g_:64 + 8 * g_ + 8, :])
    wait("scalar", STX)
    s_op(lambda h: h.activation(gsig.ap()[0:32, :], gsc.ap()[0:32, :], ACT.Sigmoid,
                                bias=bgeff32[:], scale=1.0))
    dbg_dump("gsig", gsig.ap()[0:32, :])
    wait("vector", STX)
    wait("vector", SCF)
    v_op(lambda h: h.tensor_scalar(sic.ap()[0:32, :], sic.ap()[0:32, :], bsi_pp[:], None, ALU.add))
    dbg_dump("sib", sic.ap()[0:32, :])
    v_op(lambda h: h.tensor_scalar(gsc.ap()[0:32, :], gsig.ap()[0:32, :], -1.0, 1.0, ALU.mult, ALU.add))
    v_op(lambda h: h.tensor_tensor(gsc.ap()[0:32, :], gsc.ap()[0:32, :], sic.ap()[0:32, :], ALU.mult))
    v_op(lambda h: h.tensor_tensor_scan(sic.ap()[0:32, :], gsig.ap()[0:32, :], gsc.ap()[0:32, :],
                                        0.0, ALU.mult, ALU.add))
    dbg_dump("st", sic.ap()[0:32, :])
    wait("sync", VC)
    for hh in range(2):
        dma("sync", STX, st64.ap()[32 * hh:32 * hh + 32, :],
            sic.ap()[0:32, 2048 * hh:2048 * (hh + 1)])
    wait("tensor", STX)
    wait("tensor", SCF)
    wait("tensor", VC)
    for n4 in range(4):
        fn = lambda h, n4=n4: h.matmul(ps_B[0:8, 512 * n4:512 * (n4 + 1)], selst[:],
                                       st64.ap()[:, 512 * n4:512 * (n4 + 1)], start=True, stop=True)
        finc("tensor", TC, fn) if n4 == 3 else rawi("tensor", fn)
    wait("scalar", TC)
    s_op(lambda h: h.copy(stT[:], ps_B[0:8, 0:TPC]))
    dbg_dump("stT", stT[:])

    mm128(q_bc1, e16d, mu8)
    mm128(q_bc2, e16d, is8)
    extr = [(ps_A, 0, ownrowA, 0, q_bc1), (ps_A, 32, ownrowA, 32, q_w1),
            (ps_A, 64, ownrowA, 64, q_w2), (ps_B, 0, ownrowB, 0, q_dr1),
            (ps_B, 32, ownrowB, 32, q_dr2), (ps_B, 64, ownrowB, 64, q_bc2)]
    for psx, pb, dstT, drow, srcq in extr:
        wait("tensor", VC)
        wait("tensor", SCF)
        for n4 in range(4):
            fn = lambda h, n4=n4, srcq=srcq, psx=psx, pb=pb: h.matmul(
                psx[pb:pb + 16, 512 * n4:512 * (n4 + 1)], selown[:],
                srcq[:, 512 * n4:512 * (n4 + 1)], start=True, stop=True)
            finc("tensor", TC, fn) if n4 == 3 else rawi("tensor", fn)
        wait("scalar", TC)
        s_op(lambda h, psx=psx, pb=pb, dstT=dstT, drow=drow: h.copy(
            dstT[drow:drow + 1, :], psx[pb:pb + 1, 0:TPC]))
    wait("vector", SCF)
    v_op(lambda h: h.tensor_copy(dr_i16[0:1, :], ownrowB[0:1, :]))
    v_op(lambda h: h.tensor_copy(dr_i16[32:33, :], ownrowB[32:33, :]))
    wait("gpsimd", VC)
    dma("gpsimd", GPD, dr_w[0:8, :], dr_i16[0:1, :].rearrange("o (q j) -> o q j", q=8))
    dma("gpsimd", GPD, dr_w[8:16, :], dr_i16[32:33, :].rearrange("o (q j) -> o q j", q=8))
    wait("gpsimd", GPD)
    for rep in range(1, 8):
        dma("gpsimd", GPD, dr_w[16 * rep:16 * (rep + 1), :], dr_w[0:16, :])
    wait("sync", SCF)
    for dst, srcT, row in [(w0_pp, ownrowA, 32), (w1_pp, ownrowA, 64),
                           (mu_pp, ownrowA, 0), (is_pp, ownrowB, 64)]:
        for vv in range(TV):
            dma("sync", STX, dst[:, vv:vv + 1],
                srcT[row:row + 1, 128 * vv:128 * (vv + 1)].rearrange("o (q c) -> o q c", c=1),
                slow=True)

    ctxA.close()
    ctxT = ExitStack()
    xtile = [ctxT.enter_context(nc.sbuf_tensor(f"xtile{i}", [128, DM], F32)) for i in range(2)]
    ttile = ctxT.enter_context(nc.sbuf_tensor("ttile", [128, DM], F32))
    dtile = ctxT.enter_context(nc.sbuf_tensor("dtile", [128, DM], F32))
    otile = ctxT.enter_context(nc.sbuf_tensor("otile", [128, DM], F32))
    ctxB0 = ExitStack()
    enB0 = ctxB0.enter_context
    btile = enB0(nc.sbuf_tensor("btile", [128, DM], BF16))
    gtl = enB0(nc.sbuf_tensor("gtl", [128, 512], F32))
    ctxB1 = ExitStack()
    xnT = ctxB1.enter_context(nc.sbuf_tensor("xnT", [128, DMC * TPC], BF16))
    ctxB2a = ExitStack()
    embed = ctxB2a.enter_context(nc.sbuf_tensor("embed", [128, DMC * TPC], BF16))
    ctxB5t = ExitStack()
    enB5 = ctxB5t.enter_context
    tbl1 = enB5(nc.sbuf_tensor("tbl1", [128, NRV * 64], F32))
    tbl2 = enB5(nc.sbuf_tensor("tbl2", [128, NSV * 64], F32))
    slot1s = nc.alloc_sbuf_tensor_at("slot1s", [128, NSV * 64], F32,
                                     offset=nc.lookup_mloc(tbl1).addr)
    idx_f = enB5(nc.sbuf_tensor("idx_f", [128, NRV], F32))
    k_f = enB5(nc.sbuf_tensor("k_f", [128, NRV], F32))
    mskE = enB5(nc.sbuf_tensor("mskE", [128, NRV], F32))
    gidx_f = enB5(nc.sbuf_tensor("gidx_f", [128, NRV], F32))
    s2_f = enB5(nc.sbuf_tensor("s2_f", [128, NRV], F32))
    scrE = enB5(nc.sbuf_tensor("scrE", [128, NRV], F32))
    gidx_i = enB5(nc.sbuf_tensor("gidx_i", [128, NRV], I16))
    s2_i = enB5(nc.sbuf_tensor("s2_i", [128, NRV], I16))
    rid_f = enB5(nc.sbuf_tensor("rid_f", [128, NSV], F32))
    rid_i = enB5(nc.sbuf_tensor("rid_i", [128, NSV], I16))
    rowcs = nc.alloc_sbuf_tensor_at("rowcs", [128, NRV * 64], F32,
                                    offset=nc.lookup_mloc(tbl1).addr)
    wait("scalar", VC, 3)
    zv8 = zsrc.ap().rearrange("p (v c) -> p v c", c=64)
    for c6 in range(6):
        dma("scalar", SCD, inv1_hbm[1024 * c6:1024 * (c6 + 1), :].rearrange("(v p) c -> p v c", p=128),
            zv8[:, 0:8, :])
    dma("scalar", SCD, inv1_hbm[NR:NR + 1, :], zsrc[0:1, 0:64])
    for c4 in range(4):
        dma("scalar", SCD, tb2_hbm[1024 * c4:1024 * (c4 + 1), :].rearrange("(v p) c -> p v c", p=128),
            zv8[:, 0:8, :])
    dma("scalar", SCD, tb2_hbm[NS:NS + 1, :], zsrc[0:1, 0:64])
    dma("scalar", SCD, eo_hbm[EOTR:EOTR + 1, :], zsrc_bf[0:1, :])
    INIT_ALL = SCD.n
    wait("sync", VC)
    wait("sync", TC)
    wait("sync", SCF)
    dma("sync", STX, slot1s[:], slot1c_in)
    PAY_ALL = STX.n
    ctxB2b = ExitStack()
    winm_sb = ctxB2b.enter_context(nc.sbuf_tensor("winm_sb", [128, DMC * DM], BF16))
    wout_sb = ctxB2b.enter_context(nc.sbuf_tensor("wout_sb", [128, DMC * DM], BF16))
    wout_tl = ctxB2b.enter_context(nc.sbuf_tensor("wout_tl", [8, DM], BF16))
    mubc = ctxB2b.enter_context(nc.sbuf_tensor("mubc", [128, TPC], BF16))
    isbc = ctxB2b.enter_context(nc.sbuf_tensor("isbc", [128, TPC], BF16))
    xob = [ctxB2b.enter_context(nc.sbuf_tensor(f"xob{i}", [128, TPC], BF16)) for i in range(2)]
    for dst, srcT, row in [(mubc, ownrowA, 0), (isbc, ownrowB, 64)]:
        wait("tensor", SCF)
        wait("tensor", VC)
        for n4 in range(4):
            fn = lambda h, n4=n4, srcT=srcT, row=row: h.matmul(
                ps_A[:, 512 * n4:512 * (n4 + 1)], ones33[row:row + 1, :],
                srcT[row:row + 1, 512 * n4:512 * (n4 + 1)], start=True, stop=True)
            finc("tensor", TC, fn) if n4 == 3 else rawi("tensor", fn)
        wait("scalar", TC)
        s_op(lambda h, dst=dst: h.copy(dst[:], ps_A[:, 0:TPC]))
    wait("gpsimd", VC)
    wait("gpsimd", TC)
    dma("gpsimd", LD, winm_sb.ap().rearrange("p (kc m) -> p kc m", m=DM),
        w_in_m.rearrange("(kc p) m -> p kc m", p=128))
    dma("gpsimd", LD, wout_sb.ap().rearrange("p (kc m) -> p kc m", m=DM),
        w_out_in[0:DM, :].rearrange("(kc p) m -> p kc m", p=128))
    dma("gpsimd", LD, wout_tl[:], w_out_in[DM:DM + 8, :])
    wait("gpsimd", SCD, INIT_ALL)
    wait("gpsimd", STX, PAY_ALL)
    wait("gpsimd", GPD)
    finc("gpsimd", MOE, lambda h: h.dma_scatter_add(
        inv1_hbm, slot1s.ap().rearrange("p (v c) -> p v c", c=64), dr_w[:],
        NS, NS, 64), amt=16)
    xnv = xnT.ap().rearrange("p (kc n) -> p kc n", n=TPC)
    for kc in range(DMC):
        par = kc % 2
        if kc >= 2:
            wait("gpsimd", XOF[par])
        dma("gpsimd", XOT[par], xob[par][:], xT_own[128 * kc:128 * (kc + 1), :])
        wait("vector", XOT[par])
        wait("vector", SCF)
        wait("vector", VC)
        if kc >= 1:
            wait("vector", XOF[1 - par])
        finc("vector", VC, lambda h, par=par: h.tensor_tensor(
            ttile[:, 0:DM] if False else dtile[:], xob[par][:, 0:DM], mubc[:, 0:DM], ALU.subtract))
        finc("vector", VC, lambda h, par=par: h.tensor_tensor(
            otile[:], xob[par][:, DM:TPC], mubc[:, DM:TPC], ALU.subtract))
        wait("vector", VC)
        finc("vector", VC, lambda h, kc=kc: h.tensor_tensor(
            xnv[:, kc, 0:DM], dtile[:], isbc[:, 0:DM], ALU.mult))
        finc("vector", XOF[par], lambda h, kc=kc: h.tensor_tensor(
            xnv[:, kc, DM:TPC], otile[:], isbc[:, DM:TPC], ALU.mult))
    XNDONE = (XOF[0].n, XOF[1].n)
    wait("vector", STX)
    for vv in range(TV):
        par = vv % 2
        if vv >= 2:
            wait("sync", XLF[par])
        dma("sync", XL[par], xtile[par][:], x_own[128 * vv:128 * (vv + 1), :])
        wait("vector", XL[par])
        if vv >= 1:
            wait("vector", XBS)
            wait("vector", XLF[1 - par])
        wait("vector", VC)
        finc("vector", VC, lambda h, par=par, vv=vv: h.tensor_scalar(
            ttile[:], xtile[par][:], mu_pp[:, vv:vv + 1], None, ALU.subtract))
        wait("vector", VC)
        finc("vector", XLF[par], lambda h, vv=vv: h.tensor_scalar(
            btile[:], ttile[:], is_pp[:, vv:vv + 1], None, ALU.mult))
        wait("scalar", XLF[par])
        dma("scalar", XBS, xb_hbm[128 * vv:128 * (vv + 1), :], btile[:])
    XBS_ALL = XBS.n

    wait("gpsimd", MOE)
    dma("gpsimd", MOE, tbl1.ap().rearrange("p (v c) -> p v c", c=64),
        inv1_hbm[0:NR, :].rearrange("(v p) c -> p v c", p=128))
    TBL1_ALL = MOE.n
    if "inv1" in dbg:
        wait("scalar", MOE, TBL1_ALL)
        dma("scalar", SCD, dbg["inv1"], inv1_hbm)
    if "drw" in dbg:
        wait("scalar", GPD)
        dma("scalar", SCD, dbg["drw"], dr_w[:])
    wait("vector", MOE, TBL1_ALL)
    t1v = tbl1.ap().rearrange("p (v c) -> p v c", c=64)
    v_op(lambda h: h.tensor_copy(idx_f.ap().rearrange("p (v o) -> p v o", o=1), t1v[:, :, 0:1]))
    v_op(lambda h: h.tensor_scalar(idx_f[:], idx_f[:], 1.0, None, ALU.subtract))
    v_op(lambda h: h.tensor_copy(k_f.ap().rearrange("p (v o) -> p v o", o=1), t1v[:, :, 1:2]))
    v_op(lambda h: h.tensor_scalar(mskE[:], idx_f[:], 0.0, None, ALU.is_lt))
    v_op(lambda h: h.tensor_tensor(gidx_f[:], idx_f[:], mskE[:], ALU.add))
    v_op(lambda h: h.tensor_scalar(scrE[:], mskE[:], -1.0, 1.0, ALU.mult, ALU.add))
    v_op(lambda h: h.tensor_tensor(s2_f[:], k_f[:], scrE[:], ALU.mult))
    v_op(lambda h: h.tensor_scalar(s2_f[:], s2_f[:], float(TPC), None, ALU.mult))
    v_op(lambda h: h.tensor_tensor(s2_f[:], s2_f[:], gidx_f[:], ALU.add))
    v_op(lambda h: h.tensor_scalar(scrE[:], mskE[:], float(TR2), None, ALU.mult))
    v_op(lambda h: h.tensor_tensor(s2_f[:], s2_f[:], scrE[:], ALU.add))
    v_op(lambda h: h.tensor_copy(gidx_i[:], gidx_f[:]))
    v_op(lambda h: h.tensor_copy(s2_i[:], s2_f[:]))
    dbg_dump("idxf", idx_f[:])
    dbg_dump("kf", k_f[:])

    def wrap8(t_i, t_w, nv, col0=0):
        wv = t_w.ap().rearrange("p (v hh) -> p v hh", hh=8)
        for hh in range(8):
            dma("sync", STX, wv[0:16, 0:nv, hh], t_i[16 * hh:16 * hh + 16, col0:col0 + nv],
                slow=True)
        wait("sync", STX)
        for rep in range(1, 8):
            dma("sync", STX, t_w[16 * rep:16 * (rep + 1), 0:nv * 8], t_w[0:16, 0:nv * 8])
        wait("sync", STX)

    wait("sync", VC)
    dma("sync", STX, rowcs.ap()[:, :], rowc_in)
    wrap8(gidx_i, gidx_w, NRV)
    wrap8(s2_i, s2_w, NRV)
    WRAP_ALL = STX.n
    wait("gpsimd", STX, WRAP_ALL)
    finc("gpsimd", MOE, lambda h: h.dma_scatter_add(
        tb2_hbm, rowcs.ap().rearrange("p (v c) -> p v c", c=64), s2_w[:],
        NR, NR, 64), amt=16)
    wait("gpsimd", MOE)
    dma("gpsimd", MOE, tbl2.ap().rearrange("p (v c) -> p v c", c=64),
        tb2_hbm[0:NS, :].rearrange("(v p) c -> p v c", p=128))
    TBL2_ALL = MOE.n
    winv = winm_sb.ap().rearrange("p (kc m) -> p kc m", m=DM)
    emv = embed.ap().rearrange("p (mc n) -> p mc n", n=TPC)
    wait("tensor", LD)
    wait("tensor", XOF[0], XNDONE[0])
    wait("tensor", XOF[1], XNDONE[1])
    for mc in range(DMC):
        for n4 in range(4):
            bank = ("A", (mc * 4 + n4) % 4)
            bwait("tensor", bank)
            bs = 512 * bank[1]
            for kc in range(DMC):
                fn = lambda h, mc=mc, n4=n4, kc=kc, bs=bs: h.matmul(
                    ps_A[:, bs:bs + 512], winv[:, kc, 128 * mc:128 * (mc + 1)],
                    xnv[:, kc, 512 * n4:512 * (n4 + 1)],
                    start=(kc == 0), stop=(kc == DMC - 1))
                finc("tensor", TC, fn) if kc == DMC - 1 else rawi("tensor", fn)
            wait("vector", TC)
            wait("vector", VC)
            finc("vector", VC, lambda h, mc=mc, n4=n4, bs=bs: h.tensor_scalar(
                emv[:, mc, 512 * n4:512 * (n4 + 1)], ps_A[:, bs:bs + 512],
                bic[:, mc:mc + 1], None, ALU.add))
            bset(bank, VC)
    wouv = wout_sb.ap().rearrange("p (kc m) -> p kc m", m=DM)
    wait("tensor", LD)
    wait("tensor", VC)
    wait("vector", XBS, XBS_ALL)
    for vv in range(TV):
        if vv >= 1:
            wait("vector", SST)
        for n2 in range(2):
            bank = ("B", (vv * 2 + n2) % 2)
            bwait("tensor", bank)
            bs = 512 * bank[1]
            for kc in range(DMC):
                rawi("tensor", lambda h, vv=vv, n2=n2, kc=kc, bs=bs: h.matmul(
                    ps_B[:, bs:bs + 512], emv[:, kc, 128 * vv:128 * (vv + 1)],
                    wouv[:, kc, 512 * n2:512 * (n2 + 1)], start=(kc == 0), stop=False))
            finc("tensor", TC, lambda h, vv=vv, n2=n2, bs=bs: h.matmul(
                ps_B[:, bs:bs + 512], stT[:, 128 * vv:128 * (vv + 1)],
                wout_tl[:, 512 * n2:512 * (n2 + 1)], start=False, stop=True))
            wait("vector", TC)
            wait("vector", VC)
            finc("vector", VC, lambda h, n2=n2, bs=bs: h.tensor_tensor(
                btile[:, 512 * n2:512 * (n2 + 1)], ps_B[:, bs:bs + 512],
                bb_out[:, 512 * n2:512 * (n2 + 1)], ALU.add))
            bset(bank, VC)
        wait("scalar", VC)
        dma("scalar", SST, s_hbm[128 * vv:128 * (vv + 1), :], btile[:])
    SST_ALL = SST.n
    wait("vector", MOE, TBL2_ALL)
    t2v = tbl2.ap().rearrange("p (v c) -> p v c", c=64)
    v_op(lambda h: h.tensor_copy(rid_f.ap().rearrange("p (v o) -> p v o", o=1), t2v[:, :, 0:1]))
    v_op(lambda h: h.tensor_scalar(rid_f[:], rid_f[:], 1.0, None, ALU.subtract))
    v_op(lambda h: h.tensor_scalar(scrE[:, 0:NSV], rid_f[:], 0.0, None, ALU.is_lt))
    v_op(lambda h: h.tensor_scalar(scrE[:, 0:NSV], scrE[:, 0:NSV], float(EOTR + 1), None, ALU.mult))
    v_op(lambda h: h.tensor_tensor(rid_f[:], rid_f[:], scrE[:, 0:NSV], ALU.add))
    v_op(lambda h: h.tensor_copy(rid_i[:], rid_f[:]))
    dbg_dump("rid", rid_f[:])
    wait("sync", VC)
    wrap8(rid_i, r0_w, TV, col0=0)
    wrap8(rid_i, r1_w, TV, col0=TV)
    RWRAP_ALL = STX.n

    ctxB2b.close()
    ctxB5t.close()
    ctxB2a.close()
    ctxB3a = ExitStack()
    NWB = 5
    wbuf = [ctxB3a.enter_context(nc.sbuf_tensor(f"wbuf{i}", [128, 11264], BF16))
            for i in range(NWB - 1)]
    _own_base = nc.lookup_mloc(ownrowA).addr
    assert nc.lookup_mloc(ownrowB).addr == _own_base + 8192
    assert nc.lookup_mloc(dr_i16).addr == _own_base + 16384
    assert nc.lookup_mloc(xtile[0]).addr == _own_base + 20480
    wbuf.append(nc.alloc_sbuf_tensor_at("wbuf4", [128, 11264], BF16,
                                        offset=_own_base))
    ctxB3b = ExitStack()
    a_sht = ctxB3b.enter_context(nc.sbuf_tensor("a_sht", [128, FFC * 512], BF16))
    wload_state = {"i": 0}
    wfr_expect = [0] * NWB
    wld_mark = {}

    def wload():
        i = wload_state["i"]
        nm, src, kind = wstream[i]
        par = i % NWB
        if i == 0:
            wait("gpsimd", PCW, PRE_ALL)
        if i >= NWB:
            wait("gpsimd", WFR[par], wfr_expect[par])
        if i < NPRE:
            dma("gpsimd", WLD[par], wbuf[par].ap(), wbf_hbm[i])
        elif kind == "dm":
            view = wbuf[par].ap().rearrange("p (c m) -> p c m", m=DM)[:, 0:FFC, :]
            dma("gpsimd", WLD[par], view, src.rearrange("(c p) m -> p c m", p=128))
        else:
            view = wbuf[par].ap().rearrange("p (c m) -> p c m", m=FF)[:, 0:DMC, :]
            dma("gpsimd", WLD[par], view, src.rearrange("(c p) m -> p c m", p=128))
        wld_mark[i] = (par, WLD[par].n)
        wload_state["i"] += 1
        return i

    def wrelease(i):
        par = i % NWB
        wfr_expect[par] = WFR[par].n

    wait("gpsimd", TC)
    wait("gpsimd", STX, RWRAP_ALL)
    wait("gpsimd", VC)
    for _ in range(NWB):
        wload()
    ashv = a_sht.ap().rearrange("p (fc n) -> p fc n", n=512)

    def wview(i, kind):
        par = i % NWB
        if kind == "dm":
            return wbuf[par].ap().rearrange("p (c m) -> p c m", m=DM)
        return wbuf[par].ap().rearrange("p (c m) -> p c m", m=FF)

    ig, iu, idn = 0, 1, 2
    gv = wview(ig, "ff")
    uv = wview(iu, "ff")
    dv = wview(idn, "dm")
    wait("tensor", WLD[wld_mark[ig][0]], wld_mark[ig][1])
    wait("tensor", WLD[wld_mark[iu][0]], wld_mark[iu][1])
    for t4 in range(4):
        if t4 >= 1:
            wait("vector", TC)
        for fc in range(FFC):
            gb = ("A", fc % 2)
            ub = ("A", 2 + fc % 2)
            bwait("tensor", gb)
            bwait("tensor", ub)
            gbs, ubs = 512 * gb[1], 512 * ub[1]
            for kc in range(DMC):
                fn = lambda h, fc=fc, t4=t4, kc=kc, gbs=gbs: h.matmul(
                    ps_A[:, gbs:gbs + 512], gv[:, kc, 128 * fc:128 * (fc + 1)],
                    xnv[:, kc, 512 * t4:512 * (t4 + 1)],
                    start=(kc == 0), stop=(kc == DMC - 1))
                if t4 == 3 and fc == FFC - 1 and kc == DMC - 1:
                    finc("tensor", WFR[ig], fn)
                else:
                    rawi("tensor", fn)
            for kc in range(DMC):
                fn = lambda h, fc=fc, t4=t4, kc=kc, ubs=ubs: h.matmul(
                    ps_A[:, ubs:ubs + 512], uv[:, kc, 128 * fc:128 * (fc + 1)],
                    xnv[:, kc, 512 * t4:512 * (t4 + 1)],
                    start=(kc == 0), stop=(kc == DMC - 1))
                if kc == DMC - 1:
                    if t4 == 3 and fc == FFC - 1:
                        finc("tensor", WFR[iu], fn)
                    else:
                        finc("tensor", TC, fn)
                else:
                    rawi("tensor", fn)
            if t4 == 3 and fc == FFC - 1:
                wait("scalar", WFR[iu])
            else:
                wait("scalar", TC)
            wait("scalar", VC)
            s_op(lambda h, gbs=gbs: h.activation(gtl[:], ps_A[:, gbs:gbs + 512], ACT.Silu))
            bset(gb, SCF)
            wait("vector", SCF)
            wait("vector", VC)
            finc("vector", VC, lambda h, fc=fc, ubs=ubs: h.tensor_tensor(
                ashv[:, fc, :], gtl[:], ps_A[:, ubs:ubs + 512], ALU.mult))
            bset(ub, VC)
        if t4 == 0:
            wait("tensor", WLD[wld_mark[idn][0]], wld_mark[idn][1])
            wait("vector", SST, SST_ALL)
        wait("tensor", VC)
        for vsub in range(4):
            vv = 4 * t4 + vsub
            if vv >= 1:
                wait("vector", SHST)
            for n2 in range(2):
                bank = ("B", n2)
                bwait("tensor", bank)
                bs = 512 * bank[1]
                for fc in range(FFC):
                    fn = lambda h, vsub=vsub, n2=n2, fc=fc, bs=bs: h.matmul(
                        ps_B[:, bs:bs + 512], ashv[:, fc, 128 * vsub:128 * (vsub + 1)],
                        dv[:, fc, 512 * n2:512 * (n2 + 1)],
                        start=(fc == 0), stop=(fc == FFC - 1))
                    if fc == FFC - 1:
                        if vv == TV - 1 and n2 == 1:
                            finc("tensor", WFR[idn], fn)
                        else:
                            finc("tensor", TC, fn)
                    else:
                        rawi("tensor", fn)
                if vv == TV - 1 and n2 == 1:
                    wait("vector", WFR[idn])
                else:
                    wait("vector", TC)
                wait("vector", VC)
                finc("vector", VC, lambda h, n2=n2, bs=bs: h.tensor_copy(
                    btile[:, 512 * n2:512 * (n2 + 1)], ps_B[:, bs:bs + 512]))
                bset(bank, VC)
            wait("scalar", VC)
            dma("scalar", SHST, sh_hbm[128 * vv:128 * (vv + 1), :], btile[:])
    SHST_ALL = SHST.n
    wrelease(ig)
    wrelease(iu)
    wrelease(idn)
    SHG_DONE = WFR[ig].n
    SHU_DONE = WFR[iu].n
    for _ in range(6 - NWB):
        wload()

    ctxB3b.close()
    xnT_base = nc.lookup_mloc(xnT).addr
    eo_t = nc.alloc_sbuf_tensor_at("eo_t", [128, RLC * DM], BF16, offset=xnT_base)
    xeT = [nc.alloc_sbuf_tensor_at(f"xeT{i}", [128, DMC * RL], BF16,
                                   offset=xnT_base + RLC * DM * 2 + i * DMC * RL * 2)
           for i in range(2)]
    a_e = nc.alloc_sbuf_tensor_at("a_e", [128, FFC * RL], BF16,
                                  offset=xnT_base + RLC * DM * 2 + 2 * DMC * RL * 2)
    assert RLC * DM * 2 + 2 * DMC * RL * 2 + FFC * RL * 2 <= DMC * TPC * 2
    aev = a_e.ap().rearrange("p (fc n) -> p fc n", n=RL)

    gxf_mark = [0, 0]
    for e_ in range(E):
        par = e_ % 2
        if e_ == 0:
            wait("gpsimd", WFR[0], SHG_DONE)
            wait("gpsimd", WFR[1], SHU_DONE)
        if e_ >= 2:
            wait("gpsimd", GXF[par], gxf_mark[par])
        wait("gpsimd", XBS, XBS_ALL)
        wait("gpsimd", STX, WRAP_ALL)
        finc("gpsimd", GX[par], lambda h, par=par, e_=e_: h.dma_gather(
            xeT[par].ap().rearrange("p (c n) -> p c n", n=RL), xb_hbm,
            gidx_w[:, (RL // 16) * e_:(RL // 16) * (e_ + 1)], RL, RL, DM,
            transpose=True), amt=16)
        if e_ == 0 and "xe0" in dbg:
            wait("scalar", GX[par])
            dma("scalar", SCD, dbg["xe0"], xeT[par][:])
        if e_ == 0 and "xb" in dbg:
            wait("scalar", XBS, XBS_ALL)
            dma("scalar", SCD, dbg["xb"], xb_hbm)
        iwg, iwu, iwd = 3 * e_ + 3, 3 * e_ + 4, 3 * e_ + 5
        gv_ = wview(iwg, "ff")
        uv_ = wview(iwu, "ff")
        xev = xeT[par].ap().rearrange("p (c n) -> p c n", n=RL)
        wait("tensor", GX[par])
        wait("tensor", WLD[wld_mark[iwg][0]], wld_mark[iwg][1])
        wait("tensor", WLD[wld_mark[iwu][0]], wld_mark[iwu][1])
        for fc in range(FFC):
            gb = ("A", fc % 2)
            ub = ("A", 2 + fc % 2)
            bwait("tensor", gb)
            bwait("tensor", ub)
            gbs, ubs = 512 * gb[1], 512 * ub[1]
            for kc in range(DMC):
                fn = lambda h, fc=fc, kc=kc, gbs=gbs, gv_=gv_, xev=xev: h.matmul(
                    ps_A[:, gbs:gbs + RL], gv_[:, kc, 128 * fc:128 * (fc + 1)],
                    xev[:, kc, :], start=(kc == 0), stop=(kc == DMC - 1))
                if fc == FFC - 1 and kc == DMC - 1:
                    finc("tensor", WFR[iwg % NWB], fn)
                else:
                    rawi("tensor", fn)
            for kc in range(DMC):
                fn = lambda h, fc=fc, kc=kc, ubs=ubs, uv_=uv_, xev=xev: h.matmul(
                    ps_A[:, ubs:ubs + RL], uv_[:, kc, 128 * fc:128 * (fc + 1)],
                    xev[:, kc, :], start=(kc == 0), stop=(kc == DMC - 1))
                if kc == DMC - 1:
                    if fc == FFC - 1:
                        finc("tensor", WFR[iwu % NWB], fn)
                    else:
                        finc("tensor", TC, fn)
                else:
                    rawi("tensor", fn)
            if fc == FFC - 1:
                wait("scalar", WFR[iwu % NWB])
            else:
                wait("scalar", TC)
            wait("scalar", VC)
            s_op(lambda h, gbs=gbs: h.activation(gtl[:, 0:RL], ps_A[:, gbs:gbs + RL], ACT.Silu))
            bset(gb, SCF)
            wait("vector", SCF)
            wait("vector", VC)
            finc("vector", VC, lambda h, fc=fc, ubs=ubs: h.tensor_tensor(
                aev[:, fc, :], gtl[:, 0:RL], ps_A[:, ubs:ubs + RL], ALU.mult))
            bset(ub, VC)
        if e_ == 0 and "ae0" in dbg:
            wait("scalar", VC)
            dma("scalar", SCD, dbg["ae0"], a_e[:])
        wrelease(iwg)
        wrelease(iwu)
        if e_ + 1 < E:
            wload()
            wload()
            wload()
        dv_ = wview(iwd, "dm")
        wait("tensor", WLD[wld_mark[iwd][0]], wld_mark[iwd][1])
        wait("tensor", VC)
        if e_ >= 1:
            wait("scalar", EOS)
        for rc in range(RLC):
            for n2 in range(2):
                bank = ("B", (rc * 2 + n2) % 2)
                bwait("tensor", bank)
                bs = 512 * bank[1]
                for fc in range(FFC):
                    fn = lambda h, rc=rc, n2=n2, fc=fc, bs=bs, dv_=dv_: h.matmul(
                        ps_B[:, bs:bs + 512], aev[:, fc, 128 * rc:128 * (rc + 1)],
                        dv_[:, fc, 512 * n2:512 * (n2 + 1)],
                        start=(fc == 0), stop=(fc == FFC - 1))
                    if fc == 0 and rc == 0 and n2 == 0:
                        finc("tensor", GXF[par], fn)
                    elif fc == FFC - 1:
                        if rc == RLC - 1 and n2 == 1:
                            finc("tensor", WFR[iwd % NWB], fn)
                        else:
                            finc("tensor", TC, fn)
                    else:
                        rawi("tensor", fn)
                if rc == RLC - 1 and n2 == 1:
                    wait("scalar", WFR[iwd % NWB])
                else:
                    wait("scalar", TC)
                s_op(lambda h, rc=rc, n2=n2, bs=bs: h.copy(
                    eo_t.ap().rearrange("p (r m) -> p r m", m=DM)[:, rc, 512 * n2:512 * (n2 + 1)],
                    ps_B[:, bs:bs + 512]))
                bset(bank, SCF)
        wait("scalar", SCF)
        dma("scalar", EOS, eo_hbm[RL * e_:RL * (e_ + 1), :].rearrange("(r p) m -> p r m", p=128),
            eo_t.ap().rearrange("p (r m) -> p r m", m=DM))
        wrelease(iwd)
        gxf_mark[par] = GXF[par].n
    EOS_ALL = EOS.n

    ctxB3a.close()
    ctxB1.close()
    ctxB0.close()
    ctxB4 = ExitStack()
    enB4 = ctxB4.enter_context
    g0b = enB4(nc.sbuf_tensor("g0b", [128, TV * DM], BF16))
    g1b = enB4(nc.sbuf_tensor("g1b", [128, TV * DM], BF16))
    sfull = enB4(nc.sbuf_tensor("sfull", [128, TV * DM], BF16))
    shfull = enB4(nc.sbuf_tensor("shfull", [128, TV * DM], BF16))
    ot2 = enB4(nc.sbuf_tensor("ot2", [128, DM], F32))
    xnb = [enB4(nc.sbuf_tensor(f"xnb{i}", [128, DM], BF16)) for i in range(2)]
    wait("gpsimd", EOS, EOS_ALL)
    wait("gpsimd", SCD, INIT_ALL)
    wait("gpsimd", STX, RWRAP_ALL)
    wait("gpsimd", SST, SST_ALL)
    wait("gpsimd", SHST, SHST_ALL)
    for (buf, wtab) in ((g0b, r0_w), (g1b, r1_w)):
        off = 0
        for n_ in (768, 768, 512):
            finc("gpsimd", CG, lambda h, buf=buf, wtab=wtab, off=off, n_=n_: h.dma_gather(
                buf.ap().rearrange("p (v m) -> p v m", m=DM)[:, off // 128:(off + n_) // 128, :],
                eo_hbm, wtab[:, off // 16:(off + n_) // 16], n_, n_, DM,
                transpose=False), amt=16)
            off += n_
    CG_ALL = CG.n

    wait("sync", SST, SST_ALL)
    wait("sync", SHST, SHST_ALL)
    wait("sync", EOS, EOS_ALL)
    for i_ in range(NWB):
        wait("sync", WFR[i_])
    wait("sync", TC)
    wait("sync", VC)
    wait("sync", SCF)
    dma("sync", PRE, sfull.ap().rearrange("p (v m) -> p v m", m=DM),
        s_hbm.rearrange("(v p) m -> p v m", p=128))
    dma("sync", PRE, shfull.ap().rearrange("p (v m) -> p v m", m=DM),
        sh_hbm.rearrange("(v p) m -> p v m", p=128))
    if "s" in dbg:
        wait("scalar", SST, SST_ALL)
        dma("scalar", SCD, dbg["s"], s_hbm)
    if "sh" in dbg:
        wait("scalar", SHST, SHST_ALL)
        dma("scalar", SCD, dbg["sh"], sh_hbm)
    if "eo" in dbg:
        wait("scalar", EOS, EOS_ALL)
        dma("scalar", SCD, dbg["eo"], eo_hbm)
    if "g0" in dbg:
        wait("scalar", CG, CG_ALL)
        dma("scalar", SCD, dbg["g0"], g0b[:])
        dma("scalar", SCD, dbg["g1"], g1b[:])
    sfv = sfull.ap().rearrange("p (v m) -> p v m", m=DM)
    shv = shfull.ap().rearrange("p (v m) -> p v m", m=DM)
    g0v = g0b.ap().rearrange("p (v m) -> p v m", m=DM)
    g1v = g1b.ap().rearrange("p (v m) -> p v m", m=DM)
    otl = [otile, ot2]
    OUTP = [OUTS, OUT2]
    outs_mark = [0, 0]
    wait("sync", XBS, XBS_ALL)
    for vv in range(TV):
        par = vv % 2
        wait("sync", XLF[par])
        dma("sync", XL[par], xtile[par][:], x_own[128 * vv:128 * (vv + 1), :])
        wait("sync", XNF[par])
        dma("sync", XNL[par], xnb[par][:], xb_hbm[128 * vv:128 * (vv + 1), :])
        wait("vector", XL[par])
        wait("vector", XNL[par])
        wait("vector", CG, CG_ALL)
        wait("vector", PRE)
        wait("vector", VC)
        finc("vector", VC, lambda h, vv=vv: h.tensor_scalar(
            dtile[:], g0v[:, vv, :], w0_pp[:, vv:vv + 1], None, ALU.mult))
        v_op(lambda h, vv=vv: h.tensor_scalar(ttile[:], g1v[:, vv, :], w1_pp[:, vv:vv + 1], None, ALU.mult))
        v_op(lambda h: h.tensor_tensor(dtile[:], dtile[:], ttile[:], ALU.add))
        wait("vector", VC)
        finc("vector", XNF[par], lambda h, par=par: h.tensor_tensor(
            dtile[:], dtile[:], xnb[par][:], ALU.add))
        wait("vector", XNF[par])
        v_op(lambda h, vv=vv: h.tensor_tensor(dtile[:], dtile[:], shv[:, vv, :], ALU.add))
        if "gs" in dbg:
            wait("scalar", VC)
            dma("scalar", SCD, dbg["gs"].rearrange("p (v m) -> p v m", m=DM)[:, vv, :], dtile[:])
        wait("scalar", VC)
        wait("scalar", OUTP[par], outs_mark[par])
        s_op(lambda h, par=par: h.activation(otl[par][:], dtile[:], ACT.Silu))
        wait("vector", SCF)
        v_op(lambda h, par=par, vv=vv: h.tensor_tensor(otl[par][:], otl[par][:], sfv[:, vv, :], ALU.mult))
        wait("vector", VC)
        finc("vector", XLF[par], lambda h, par=par: h.tensor_tensor(
            otl[par][:], otl[par][:], xtile[par][:], ALU.add))
        wait("scalar", XLF[par])
        dma("scalar", OUTP[par], out_hbm[128 * vv:128 * (vv + 1), :], otl[par][:])
        outs_mark[par] = OUTP[par].n
    wait("scalar", OUTS)
    wait("scalar", OUT2)

    block = en(nc.Block())
    P.replay(block)
    ctxB4.close()
    ctxT.close()
    ctx.close()
    return nc


_NC_CACHE = {}


def _get_nc(debug=()):
    key = tuple(debug)
    if key not in _NC_CACHE:
        nc = build_nc(debug=key)
        lower_extended_insts(nc)
        _NC_CACHE[key] = nc
    return _NC_CACHE[key]


def kernel(L0_attn_x, L0_attn_W_in, L0_attn_b_in, L0_attn_W_g, L0_attn_b_g,
           L0_attn_W_out, L0_attn_b_out, L0_ffn_router_w, L0_ffn_router_b,
           L0_ffn_w_gate, L0_ffn_w_up, L0_ffn_w_down, L0_ffn_sw_gate,
           L0_ffn_sw_up, L0_ffn_sw_down):
    x = np.ascontiguousarray(np.asarray(L0_attn_x, dtype=np.float32).reshape(NTOK, DM))
    xTf = np.ascontiguousarray(x.T)
    W_in = np.asarray(L0_attn_W_in, np.float32)
    consts = _host_constants()
    wsc = np.ascontiguousarray(np.concatenate(
        [np.asarray(L0_ffn_router_w, np.float32),
         W_in[:, DM:DM + 8],
         np.full((DM, 1), 1.0 / DM, np.float32)], axis=1))
    shared_map = {
        "xT": xTf,
        "w_small_c": wsc,
        "w_inT": np.ascontiguousarray(W_in.T),
        "w_g_in": np.ascontiguousarray(np.asarray(L0_attn_W_g, np.float32)),
        "b_in_col": np.ascontiguousarray(np.asarray(L0_attn_b_in, np.float32)[:, None]),
        "b_g8": np.ascontiguousarray(np.asarray(L0_attn_b_g, np.float32)[:, None]),
        "b_si32": np.ascontiguousarray(np.tile(np.asarray(L0_attn_b_in, np.float32)[DM:DM + 8], 4)[:, None]),
        "rb128": np.ascontiguousarray(np.tile(np.asarray(L0_ffn_router_b, np.float32), 8)[:, None]),
        "w_in_m": np.ascontiguousarray(W_in[:, 0:DM]),
        "w_out_in": np.ascontiguousarray(np.asarray(L0_attn_W_out, np.float32)),
        "b_out_row": np.ascontiguousarray(np.asarray(L0_attn_b_out, np.float32)[None, :]),
        "sw_gate": np.ascontiguousarray(np.asarray(L0_ffn_sw_gate, np.float32)),
        "sw_up": np.ascontiguousarray(np.asarray(L0_ffn_sw_up, np.float32)),
        "sw_down": np.ascontiguousarray(np.asarray(L0_ffn_sw_down, np.float32)),
        "w_gate": np.ascontiguousarray(np.asarray(L0_ffn_w_gate, np.float32)),
        "w_up": np.ascontiguousarray(np.asarray(L0_ffn_w_up, np.float32)),
        "w_down": np.ascontiguousarray(np.asarray(L0_ffn_w_down, np.float32)),
        "ones128": consts["ones128"],
        "E16dup": consts["E16dup"],
        "E8dup2": consts["E8dup2"],
        "P8": consts["P8"],
        "T16": consts["T16"],
        "BD16": consts["BD16"],
        "iotaE": consts["iotaE"],
        "slot1c": consts["slot1c"],
        "rowc": consts["rowc"],
    }
    in_maps = []
    for c in range(NCORE):
        sown, sst = _per_core_consts(c)
        m = dict(shared_map)
        m["xT_own"] = np.ascontiguousarray(xTf[:, TPC * c:TPC * (c + 1)])
        m["x_own"] = np.ascontiguousarray(x[TPC * c:TPC * (c + 1), :])
        m["selOwn"] = sown
        m["selST"] = sst
        in_maps.append(m)
    nc = _get_nc(DEBUG)
    res = run_bass_kernel_spmd(nc, in_maps, core_ids=list(range(NCORE)))
    out = np.concatenate([res.results[c]["out_own"] for c in range(NCORE)], axis=0)
    if DEBUG:
        kernel._dbg = [{k: v for k, v in r.items() if k.startswith("dbg_")}
                       for r in res.results]
    return out.reshape(4, 4096, DM)
